# version 28
# baseline (speedup 1.0000x reference)
import numpy as np
import concourse.bass as bass
import concourse.mybir as mybir
from concourse.bass_utils import run_bass_kernel_spmd
from concourse.bass import IndirectOffsetOnAxis

F32 = mybir.dt.float32
BF16 = mybir.dt.bfloat16
I32 = mybir.dt.int32
AF = mybir.ActivationFunctionType
ALU = mybir.AluOpType
AX = mybir.AxisListType

ENGS = ("sync", "tensor", "vector", "scalar", "gpsimd")
ESEM = {"tensor": "pe", "vector": "dve", "scalar": "act", "gpsimd": "pool", "sync": "sq"}
QLANES = {"sd": 8, "gd": 8, "st": 8, "cd": 4}
DMAQS = ("sd", "gd", "st", "cd")
SEMS = ("pe", "dve", "act", "pool", "sq", "cc") + tuple(
    f"{q}{i}" for q in DMAQS for i in range(QLANES[q]))

SCALE_ATT = 1.0 / (128.0 ** 0.5)
NEG_BIG = -1.0e30


class Buf:
    __slots__ = ("w", "readers")

    def __init__(self):
        self.w = None
        self.readers = {}

    def rdeps(self):
        return [self.w] if self.w else []

    def wdeps(self):
        d = [(n, m) for n, m in self.readers.items()]
        if self.w:
            d.append(self.w)
        return d

    def note_w(self, mark):
        self.w = mark
        self.readers = {}

    def note_r(self, mark):
        n, m = mark
        if self.readers.get(n, 0) < m:
            self.readers[n] = m


class Em:

    def __init__(self, nc):
        self.nc = nc
        self.ops = {e: [] for e in ENGS}
        self.cnt = {n: 0 for n in SEMS}
        self.waited = {e: {n: 0 for n in SEMS} for e in ENGS}
        self.sem = {}
        self.qrot = {q: 0 for q in DMAQS}

    def wait(self, e, dep):
        n, m = dep
        if m <= 0 or self.waited[e][n] >= m:
            return
        self.waited[e][n] = m
        s = self.sem[n]
        self.ops[e].append(lambda eng, s=s, m=m: eng.wait_ge(s, m))

    def waits(self, e, deps):
        for d in deps:
            self.wait(e, d)

    def raw(self, e, build):
        self.ops[e].append(build)

    def op(self, e, build, sem=None, dn=1):
        n = sem or ESEM[e]
        self.cnt[n] += dn
        m = self.cnt[n]
        s = self.sem[n]
        self.ops[e].append(lambda eng, b=build, s=s, d=dn: b(eng).then_inc(s, d))
        return (n, m)

    def do(self, e, build, reads=(), writes=(), sem=None, dn=1):
        for b in reads:
            self.waits(e, b.rdeps())
        for b in writes:
            self.waits(e, b.wdeps())
        m = self.op(e, build, sem, dn)
        for b in reads:
            b.note_r(m)
        for b in writes:
            b.note_w(m)
        return m

    def dma(self, e, build, q, reads=(), writes=()):
        i = self.qrot[q]
        self.qrot[q] += 1
        lane = f"{q}{i % QLANES[q]}"
        prev = self.cnt[lane]
        if prev > 0:
            self.wait(e, (lane, prev))
        for b in reads:
            self.waits(e, b.rdeps())
        for b in writes:
            self.waits(e, b.wdeps())
        m = self.op(e, build, sem=lane, dn=16)
        for b in reads:
            b.note_r(m)
        for b in writes:
            b.note_w(m)
        return m

    def group(self, e, builds, reads=(), writes=(), sem=None, dn=1):
        for b in reads:
            self.waits(e, b.rdeps())
        for b in writes:
            self.waits(e, b.wdeps())
        for b in builds[:-1]:
            self.raw(e, b)
        m = self.op(e, builds[-1], sem, dn)
        for b in reads:
            b.note_r(m)
        for b in writes:
            b.note_w(m)
        return m


def build_program(cfg):
    T = int(cfg.get("T", 2048))
    NR = int(cfg.get("n_routed", 32))
    phases = cfg.get("phases", ("ffn1", "attn", "ffn2"))
    H, I, DH, SH_I = 1024, 512, 128, 1024
    NE = NR + 2
    TT = T // 128
    NTW = min(512, T)
    NNT = T // NTW
    KC = H // 128
    IC = I // 128
    NSUB = NTW // 128
    assert T % 128 == 0 and 32 * TT <= 512

    nc = bass.Bass("TRN2", target_bir_lowering=False)

    COLL = bool(cfg.get("collective", False))
    NSLOT = 4
    x_in = nc.dram_tensor("x", [T, H], F32, kind="ExternalInput").ap()
    wq_in = nc.dram_tensor("wq", [H, DH], F32, kind="ExternalInput").ap()
    wk_in = nc.dram_tensor("wk", [H, DH], F32, kind="ExternalInput").ap()
    wv_in = nc.dram_tensor("wv", [H, DH], F32, kind="ExternalInput").ap()
    wo_in = nc.dram_tensor("wo", [DH, H], F32, kind="ExternalInput").ap()
    rw_in = nc.dram_tensor("rw", [32, H], F32, kind="ExternalInput").ap()
    rb_in = nc.dram_tensor("rb", [32], F32, kind="ExternalInput").ap()
    if COLL:
        gw5_in = nc.dram_tensor("gw5", [NSLOT, I, H], F32, kind="ExternalInput").ap()
        uw5_in = nc.dram_tensor("uw5", [NSLOT, I, H], F32, kind="ExternalInput").ap()
        dw5_in = nc.dram_tensor("dw5", [NSLOT, H, I], F32, kind="ExternalInput").ap()
        sgw_in = nc.dram_tensor("sgw", [SH_I, H], F32, kind="ExternalInput").ap()
        suw_in = nc.dram_tensor("suw", [SH_I, H], F32, kind="ExternalInput").ap()
        sdw_in = nc.dram_tensor("sdw", [H, SH_I], F32, kind="ExternalInput").ap()
        NE_TBL = 8 * NSLOT + 2
        wstage = nc.dram_tensor("wstage", [NSLOT, 128, 12288], BF16, kind="Internal").ap()
    else:
        gw_in = nc.dram_tensor("gw", [32, I, H], F32, kind="ExternalInput").ap()
        uw_in = nc.dram_tensor("uw", [32, I, H], F32, kind="ExternalInput").ap()
        dw_in = nc.dram_tensor("dw", [32, H, I], F32, kind="ExternalInput").ap()
        sgw_in = nc.dram_tensor("sgw", [SH_I, H], F32, kind="ExternalInput").ap()
        suw_in = nc.dram_tensor("suw", [SH_I, H], F32, kind="ExternalInput").ap()
        sdw_in = nc.dram_tensor("sdw", [H, SH_I], F32, kind="ExternalInput").ap()
        NE_TBL = NE
    y_out = nc.dram_tensor("y", [T, H], F32, kind="ExternalOutput").ap()
    wcache = nc.dram_tensor("wcache", [NE_TBL, 128, 12288], BF16, kind="Internal").ap()
    SPARSE = bool(cfg.get("sparse", True))
    NSCHED = (4 * T) // 128 + 32
    xln_d = nc.dram_tensor("xln_d", [T, 1024], BF16, kind="Internal").ap()
    wrt_d = nc.dram_tensor("wrt_d", [T * 32, 1], F32, kind="Internal").ap()
    cv_d = nc.dram_tensor("cv_d", [NSCHED * 128, 1], F32, kind="Internal").ap()
    pout_d = nc.dram_tensor("pout_d", [NSCHED * 128, 1024], BF16, kind="Internal").ap()

    from contextlib import ExitStack
    ctx = ExitStack()
    e_ = ctx.enter_context

    def sb(name, shape, dt):
        return e_(nc.sbuf_tensor(name, shape, dt))

    def psum(name, shape, dt):
        return e_(nc.psum_tensor(name, shape, dt))

    x_sb = sb("x_sb", [128, TT * 1024], F32)
    xfm = sb("xfm", [128, KC * T], BF16)
    xln = sb("xln", [128, 2 * 1024], BF16)
    wnat = sb("wnat", [128, 2 * 4096], BF16)
    wT_sb = [sb("wT0", [128, 12288], BF16), sb("wT1", [128, 12288], BF16)]
    sG = sb("sG", [128, IC * NTW], BF16)
    hfm = sb("hfm", [128, IC * NTW], BF16)
    assert 4 * T <= 12288
    qfm = wT_sb[1][:, 0 * T:1 * T]
    kfm = wT_sb[1][:, 1 * T:2 * T]
    vtm = wT_sb[1][:, 2 * T:3 * T]
    ctxfm = wT_sb[1][:, 3 * T:4 * T]
    arow2 = [xln[:, 0:T] if T <= 2048 else None, sG[:, 0:T]]
    aT = hfm[:, 0:T]
    ctxscr = sb("ctxscr", [128, 128], BF16)
    wqb = sb("wqb", [128, KC * 128], BF16)
    wkb = sb("wkb", [128, KC * 128], BF16)
    wvb = sb("wvb", [128, KC * 128], BF16)
    wob = sb("wob", [128, 1024], BF16)
    rwT = sb("rwT", [128, KC * 32], BF16)
    rw_nat = sb("rw_nat", [32, 1024], BF16)
    bias_row = sb("bias_row", [1, 32], F32)
    bias_bc = sb("bias_bc", [128, 32], F32)
    ones1 = sb("ones1", [1, 128], F32)
    ident = sb("ident", [128, 128], BF16)
    identf = sb("identf", [128, 128], F32)
    maskc = sb("maskc", [128, 128], F32)
    scores_all = sb("scores_all", [128, TT * 32], F32)
    wrt = sb("wrt", [128, TT * 32], F32)
    bnscr = sb("bnscr", [128, 24], F32)
    mv = sb("mv", [128, 2 * TT], F32)
    rstd = sb("rstd", [128, TT], F32)
    nmrs = sb("nmrs", [128, TT], F32)
    lnt1 = sb("lnt1", [128, TT], F32)
    NCH = 3
    RS = []
    for ci in range(NCH):
        RS.append({nm: sb(f"r_{nm}{ci}", [128, wdt], F32) for nm, wdt in (
            ("sfc", 32), ("g8", 32), ("gs8", 8), ("m8a", 8), ("m8b", 8),
            ("ga", 4), ("gb", 4), ("gc", 4), ("gd", 4),
            ("t1", 1), ("t2", 1), ("t3", 1),
            ("sm", 32), ("tmp", 32), ("a", 32), ("b", 32), ("c", 32),
            ("d", 32), ("W", 32))})
    ones8 = sb("ones8", [128, 8], F32)
    ATS = []
    for ci in range(2):
        ATS.append({nm: sb(f"at_{nm}{ci}", [128, wdt], F32) for nm, wdt in (
            ("part", 4), ("nm", 1), ("nms", 1), ("sum", 1), ("rinv", 1))})
    spool = sb("spool", [128, 1536], F32)
    sel_all = spool[:, 0:512]
    dstrow = spool[:, 512:1024]
    cvio = spool[:, 1024:1536]
    abc_s = spool[:, 1024:1536]
    triu128 = sb("triu128", [128, 128], BF16)
    triu16s = sb("triu16s", [16, 16], BF16)
    ones128c = sb("ones128c", [128, 1], BF16)
    thr16 = sb("thr16", [16, 1], F32)
    ones16c = sb("ones16c", [16, 1], BF16)
    selbf = sb("selbf", [128, 512], BF16)
    tcnt_sb = sb("tcnt_sb", [32, 16], BF16)
    tcnt_T = sb("tcnt_T", [16, 32], BF16)
    csum_sb = sb("csum_sb", [16, 32], F32)
    erow = sb("erow", [1, 160], F32)
    addrow_s = sb("addrow_s", [16, 32], F32)
    addflat = sb("addflat", [1, 512], F32)
    jcol = sb("jcol", [128, 1], F32)
    pcol = sb("pcol", [128, 1], F32)
    schedc = sb("schedc", [128, 1], F32)
    schedbc = sb("schedbc", [128, 1], F32)
    dst4 = sb("dst4", [128, 8], F32)
    cv4 = sb("cv4", [128, 8], F32)
    dst4a = sb("dst4a", [128, 4 * (T // 128)], F32)
    cv4a = sb("cv4a", [128, 4 * (T // 128)], F32)
    dst4i = sb("dst4i", [128, 4 * (T // 128)], I32)
    cv4i = sb("cv4i", [128, 4 * (T // 128)], I32)
    e4f = dst4a
    w4a = sb("w4a", [128, 4 * (T // 128)], F32)
    iota_e = sb("iota_e", [128, 32], F32)
    ohscr = sb("ohscr", [128, 32], F32)
    cvload = sb("cvload", [8, 128], F32)
    rowi = sb("rowi", [128, NSCHED], I32)
    idxwi = sb("idxwi", [128, NSCHED], I32)
    sent_t = sb("sent_t", [128, 32], F32)

    pGU = [psum(f"pgu{i}", [128, 512], F32) for i in range(4)]
    pD0 = psum("pd0", [128, 512], F32)
    pD1 = psum("pd1", [128, 512], F32)
    pT = psum("pt", [128, 1024], BF16)
    pX = psum("px", [128, 512], F32)

    k = Em(nc)

    _bufs = {}

    def buf(name):
        if name not in _bufs:
            _bufs[name] = Buf()
        return _bufs[name]

    xtile = [buf(f"x{t}") for t in range(TT)]
    bXfm = buf("xfm")
    bXln = [buf("xln0"), buf("xln1")]
    bWnat = [buf("wnat0"), buf("wnat1")]
    bWT = [buf("wT0"), buf("wT1")]
    bSG = buf("sG")
    bHfm = buf("hfm")
    bPGU = [buf(f"pgu{i}") for i in range(4)]
    bPX = buf("px")
    bPDrot = [buf("pd0"), buf("pd1"), bPX]
    bPT = buf("pT")
    bPTh = [buf("pTh0"), buf("pTh1")]
    bStats = buf("stats")
    bBn = [buf("bn0"), buf("bn1")]
    bScores = buf("scores")
    bWrt = buf("wrt")
    bRS = [buf(f"rsc{ci}") for ci in range(NCH)]
    bQ = bWT[1]; bK = bWT[1]; bV = bWT[1]; bCtx = bWT[1]
    bArow = [buf("arowA"), bSG]
    bAT = bHfm; bCtxS = buf("ctxscr")
    bAtt = [buf("at0"), buf("at1")]
    bConsts = buf("consts")
    bCache = [buf(f"cache{e}") for e in range(NE)]
    bSpool = buf("spool")
    bSelbf = buf("selbf")
    bSched = buf("sched")
    bD4 = buf("d4")
    bCvm = buf("cvm")
    bWv = buf("wvals")
    bXlnD = buf("xln_dram")
    bWrtD = buf("wrt_dram")
    bCvD = buf("cv_dram")
    bPoutD = buf("pout_dram")
    bGb = [buf("gb0"), buf("gb1")]
    bGf = [buf("gf0"), buf("gf1")]
    bMx = [buf("mx0"), buf("mx1")]
    bWv2 = [buf("wv0"), buf("wv1")]
    bCacheAll = buf("cacheall")

    xfm3 = xfm[:, :].rearrange("p (c t) -> p c t", c=KC)
    mv2 = mv[:, :].rearrange("p (t two) -> p t two", two=2)
    g8v = [R["g8"][:, :].rearrange("p (g j) -> p g j", g=4) for R in RS]
    wTgu_v = [w[:, 0:8192].rearrange("p (m k q c) -> p m k q c", m=2, k=KC, q=IC)
              for w in wT_sb]
    wTd_v = [w[:, 8192:12288].rearrange("p (j q c) -> p j q c", j=IC, q=KC)
             for w in wT_sb]
    pTj_v = pT[:, :].rearrange("p (j c) -> p j c", j=8)
    pTd_v = pT[:, :].rearrange("p (j w c) -> p j w c", j=IC, w=2)

    def emit_prog():
        m = k.dma("sync", lambda s: s.dma_start(
            out=x_sb[:, :], in_=x_in.rearrange("(t p) f -> p t f", p=128)),
            q="sd")
        for t in range(TT):
            xtile[t].note_w(m)

        for w_in, w_sb_ in ((wq_in, wqb), (wk_in, wkb), (wv_in, wvb)):
            k.dma("gpsimd", lambda g, w_in=w_in, w_sb_=w_sb_: g.dma_start(
                out=w_sb_[:, :], in_=w_in.rearrange("(c p) d -> p c d", p=128)),
                q="gd", writes=[bConsts])
        k.dma("gpsimd", lambda g: g.dma_start(out=wob[:, :], in_=wo_in),
              q="gd", writes=[bConsts])
        k.dma("gpsimd", lambda g: g.dma_start(out=rw_nat[:, :], in_=rw_in),
              q="gd", writes=[bConsts])
        k.dma("sync", lambda s: s.dma_start(
            out=bias_row[:, :], in_=rb_in.rearrange("(o e) -> o e", o=1)),
            q="sd", writes=[bConsts])

        k.do("gpsimd", lambda g: g.memset(ident[:, :], 0.0), writes=[bConsts])
        k.do("gpsimd", lambda g: g.affine_select(
            out=ident[:, :], in_=ident[:, :], compare_op=ALU.not_equal, fill=1.0,
            base=0, pattern=[[-1, 128]], channel_multiplier=1),
            reads=[bConsts], writes=[bConsts])
        k.do("gpsimd", lambda g: g.memset(maskc[:, :], 0.0), writes=[bConsts])
        k.do("gpsimd", lambda g: g.affine_select(
            out=maskc[:, :], in_=maskc[:, :], compare_op=ALU.is_ge, fill=NEG_BIG,
            base=0, pattern=[[-1, 128]], channel_multiplier=1),
            reads=[bConsts], writes=[bConsts])
        k.do("gpsimd", lambda g: g.memset(ones1[:, :], 1.0), writes=[bConsts])
        if SPARSE:
            k.do("gpsimd", lambda g: g.memset(identf[:, :], 0.0), writes=[bConsts])
            k.do("gpsimd", lambda g: g.affine_select(
                out=identf[:, :], in_=identf[:, :], compare_op=ALU.not_equal, fill=1.0,
                base=0, pattern=[[-1, 128]], channel_multiplier=1),
                reads=[bConsts], writes=[bConsts])
            k.do("gpsimd", lambda g: g.memset(triu128[:, :], 0.0), writes=[bConsts])
            k.do("gpsimd", lambda g: g.affine_select(
                out=triu128[:, :], in_=triu128[:, :], compare_op=ALU.is_gt, fill=1.0,
                base=0, pattern=[[-1, 128]], channel_multiplier=1),
                reads=[bConsts], writes=[bConsts])
            k.do("gpsimd", lambda g: g.memset(triu16s[:, :], 0.0), writes=[bConsts])
            k.do("gpsimd", lambda g: g.affine_select(
                out=triu16s[:, :], in_=triu16s[:, :], compare_op=ALU.is_ge, fill=1.0,
                base=0, pattern=[[-1, 16]], channel_multiplier=1),
                reads=[bConsts], writes=[bConsts])
            k.do("gpsimd", lambda g: g.memset(ones128c[:, :], 1.0), writes=[bConsts])
            k.do("gpsimd", lambda g: g.memset(ones16c[:, :], 1.0), writes=[bConsts])
            k.do("gpsimd", lambda g: g.iota(jcol[:, :], [[1, 1]], base=0,
                                            channel_multiplier=1,
                                            allow_small_or_imprecise_dtypes=True),
                 writes=[bConsts])
            k.do("gpsimd", lambda g: g.iota(pcol[:, :], [[1, 1]], base=0,
                                            channel_multiplier=1,
                                            allow_small_or_imprecise_dtypes=True),
                 writes=[bConsts])
            k.do("gpsimd", lambda g: g.memset(sent_t[:, :], 100000.0), writes=[bConsts])
            k.do("gpsimd", lambda g: g.iota(iota_e[:, :], [[1, 32]], base=0,
                                            channel_multiplier=0,
                                            allow_small_or_imprecise_dtypes=True),
                 writes=[bConsts])
            k.do("gpsimd", lambda g: g.iota(thr16[:, :], [[1, 1]], base=0,
                                            channel_multiplier=128,
                                            allow_small_or_imprecise_dtypes=True),
                 writes=[bConsts])
            k.do("gpsimd", lambda g: g.memset(wnat[:, :], 0.0),
                 writes=[bWnat[0], bWnat[1]])
        k.do("vector", lambda v: v.memset(ones8[:, :], 1.0), writes=[bConsts])
        for ci in range(NCH):
            k.do("vector", lambda v, ci=ci: v.memset(RS[ci]["gs8"][:, 4:8], -3.0e38),
                 writes=[bRS[ci]])

        builds = [(lambda te, kk=kk: te.transpose(
            pT[:, 32 * kk:32 * (kk + 1)], rw_nat[0:32, 128 * kk:128 * (kk + 1)],
            ident[0:32, 0:32])) for kk in range(KC)]
        k.group("tensor", builds, reads=[bConsts], writes=[bPT])
        k.do("vector", lambda v: v.tensor_copy(out=rwT[:, :], in_=pT[:, 0:KC * 32]),
             reads=[bPT], writes=[bConsts])

        k.do("tensor", lambda t: t.matmul(pX[:, 0:32], ones1[0:1, :],
                                          bias_row[0:1, :], start=True, stop=True),
             reads=[bConsts], writes=[bPX])
        k.do("vector", lambda v: v.tensor_copy(out=bias_bc[:, :], in_=pX[:, 0:32]),
             reads=[bPX], writes=[bConsts])

        if "ffn1" in phases:
            emit_ffn(0)
        if "attn" in phases:
            emit_attention()
        if "ffn2" in phases:
            emit_ffn(1)

        for t in range(TT):
            k.waits("sync", xtile[t].rdeps())
        mfin = k.dma("sync", lambda s: s.dma_start(
            out=y_out.rearrange("(t p) f -> p t f", p=128), in_=x_sb[:, :]),
            q="st")
        for q in DMAQS:
            for i in range(QLANES[q]):
                lane = f"{q}{i}"
                if k.cnt[lane] > 0:
                    k.wait("sync", (lane, k.cnt[lane]))
        k.wait("sync", mfin)

    def emit_ln(dram=False):
        for t in range(TT):
            sl = t % 2
            xt = x_sb[:, 1024 * t:1024 * (t + 1)]
            o = 12 * sl
            k.group("vector", [
                (lambda v, xt=xt, o=o: v.bn_stats(out=bnscr[:, o:o + 6], in_=xt[:, 0:512])),
                (lambda v, xt=xt, o=o: v.bn_stats(out=bnscr[:, o + 6:o + 12], in_=xt[:, 512:1024])),
            ], reads=[xtile[t]], writes=[bBn[sl]])
            k.do("vector", lambda v, t=t, o=o: v.bn_aggr(out=mv2[:, t, :], in_=bnscr[:, o:o + 12]),
                 reads=[bBn[sl]], writes=[bStats])
        k.do("vector", lambda v: v.tensor_scalar(
            out=lnt1[:, :], in0=mv2[:, :, 1:2], scalar1=1e-5, scalar2=None, op0=ALU.add),
            reads=[bStats], writes=[bStats])
        k.do("scalar", lambda s: s.activation(out=rstd[:, :], in_=lnt1[:, :], func=AF.Sqrt),
             reads=[bStats], writes=[bStats])
        k.do("vector", lambda v: v.reciprocal(out=rstd[:, :], in_=rstd[:, :]),
             reads=[bStats], writes=[bStats])
        k.do("vector", lambda v: v.scalar_tensor_tensor(
            out=nmrs[:, :], in0=mv2[:, :, 0:1], scalar=-1.0, in1=rstd[:, :],
            op0=ALU.mult, op1=ALU.mult), reads=[bStats], writes=[bStats])

        def norm(t):
            p = t % 2
            xt = x_sb[:, 1024 * t:1024 * (t + 1)]
            xl = xln[:, 1024 * p:1024 * (p + 1)]
            k.do("vector", lambda v, xt=xt, xl=xl, t=t: v.tensor_scalar(
                out=xl, in0=xt, scalar1=rstd[:, t:t + 1], scalar2=nmrs[:, t:t + 1],
                op0=ALU.mult, op1=ALU.add),
                reads=[xtile[t], bStats], writes=[bXln[p]])

        norm(0)
        if TT > 1:
            norm(1)
        for t in range(TT):
            p = t % 2
            xl = xln[:, 1024 * p:1024 * (p + 1)]
            builds = [(lambda te, xl=xl, kk=kk: te.transpose(
                pT[:, 128 * kk:128 * (kk + 1)], xl[:, 128 * kk:128 * (kk + 1)],
                ident[:, :])) for kk in range(KC)]
            k.group("tensor", builds, reads=[bXln[p], bConsts], writes=[bPT])
            dst = xfm3[:, :, 128 * t:128 * (t + 1)]
            k.do("vector", lambda v, dst=dst: v.tensor_copy(out=dst, in_=pT[:, 0:KC * 128]),
                 reads=[bPT], writes=[bXfm])
            if dram:
                k.dma("scalar", lambda s, xl=xl, t=t: s.dma_start(
                    out=xln_d[128 * t:128 * (t + 1), :], in_=xl),
                    q="cd", reads=[bXln[p]], writes=[bXlnD])
            if t + 2 < TT:
                norm(t + 2)

    def emit_routing():
        for t in range(TT):
            builds = [(lambda te, kk=kk, t=t: te.matmul(
                pX[:, 32 * t:32 * (t + 1)],
                xfm[:, T * kk + 128 * t:T * kk + 128 * (t + 1)],
                rwT[:, 32 * kk:32 * (kk + 1)],
                start=(kk == 0), stop=(kk == KC - 1))) for kk in range(KC)]
            k.group("tensor", builds, reads=[bXfm, bConsts], writes=[bPX])
        for t in range(TT):
            k.do("scalar", lambda s, t=t: s.activation(
                out=scores_all[:, 32 * t:32 * (t + 1)], in_=pX[:, 32 * t:32 * (t + 1)],
                func=AF.Sigmoid), reads=[bPX], writes=[bScores])

        D = "vector"

        def steps(t, ci):
            R = RS[ci]
            bR = bRS[ci]
            sc = scores_all[:, 32 * t:32 * (t + 1)]
            wout = wrt[:, 32 * t:32 * (t + 1)]
            th = []
            A = th.append
            A(lambda: k.do(D, lambda v: v.tensor_add(out=R["sfc"][:, :], in0=sc, in1=bias_bc[:, :]),
                           reads=[bScores, bConsts], writes=[bR]))
            A(lambda: k.group(D, [
                (lambda v, g=g: v.max(out=R["g8"][:, 8 * g:8 * (g + 1)],
                                      in_=R["sfc"][:, 8 * g:8 * (g + 1)])) for g in range(4)],
                reads=[bR], writes=[bR]))
            A(lambda: k.do(D, lambda v: v.tensor_add(out=R["gs8"][:, 0:4], in0=g8v[ci][:, :, 0:1],
                                                     in1=g8v[ci][:, :, 1:2]),
                           reads=[bR], writes=[bR]))
            A(lambda: k.do(D, lambda v: v.max(out=R["m8a"][:, :], in_=R["gs8"][:, :]),
                           reads=[bR], writes=[bR]))
            A(lambda: k.group(D, [
                (lambda v: v.tensor_scalar(out=R["ga"][:, :], in0=R["gs8"][:, 0:4],
                                           scalar1=R["m8a"][:, 1:2], scalar2=None, op0=ALU.is_gt)),
                (lambda v: v.tensor_scalar(out=R["gb"][:, :], in0=R["gs8"][:, 0:4],
                                           scalar1=R["m8a"][:, 1:2], scalar2=None, op0=ALU.is_equal)),
            ], reads=[bR], writes=[bR]))
            A(lambda: k.do(D, lambda v: v.reduce_sum(out=R["t1"][:, :], in_=R["ga"][:, :], axis=AX.X),
                           reads=[bR], writes=[bR]))
            A(lambda: k.group(D, [
                (lambda v: v.tensor_copy(out=R["gc"][:, 0:1], in_=R["gb"][:, 0:1])),
                (lambda v: v.tensor_add(out=R["gc"][:, 1:4], in0=R["gb"][:, 1:4], in1=R["gb"][:, 0:3])),
            ], reads=[bR], writes=[bR]))
            A(lambda: k.group(D, [
                (lambda v: v.tensor_copy(out=R["gd"][:, 0:2], in_=R["gc"][:, 0:2])),
                (lambda v: v.tensor_add(out=R["gd"][:, 2:4], in0=R["gc"][:, 2:4], in1=R["gc"][:, 0:2])),
            ], reads=[bR], writes=[bR]))
            A(lambda: k.do(D, lambda v: v.tensor_scalar(out=R["gc"][:, :], in0=R["gd"][:, :],
                                                        scalar1=R["t1"][:, 0:1], scalar2=2.0,
                                                        op0=ALU.add, op1=ALU.is_le),
                           reads=[bR], writes=[bR]))
            A(lambda: k.do(D, lambda v: v.tensor_mul(out=R["gc"][:, :], in0=R["gc"][:, :], in1=R["gb"][:, :]),
                           reads=[bR], writes=[bR]))
            A(lambda: k.do(D, lambda v: v.tensor_add(out=R["gc"][:, :], in0=R["gc"][:, :], in1=R["ga"][:, :]),
                           reads=[bR], writes=[bR]))
            A(lambda: k.group(D, [
                (lambda v, g=g: v.tensor_scalar(
                    out=R["sm"][:, 8 * g:8 * (g + 1)], in0=ones8[:, :],
                    scalar1=R["gc"][:, g:g + 1], scalar2=None, op0=ALU.mult)) for g in range(4)],
                reads=[bR, bConsts], writes=[bR]))
            A(lambda: k.do(D, lambda v: v.tensor_mul(out=R["tmp"][:, :], in0=R["sfc"][:, :], in1=R["sm"][:, :]),
                           reads=[bR], writes=[bR]))
            A(lambda: k.do(D, lambda v: v.max(out=R["m8b"][:, :], in_=R["tmp"][:, :]),
                           reads=[bR], writes=[bR]))
            A(lambda: k.group(D, [
                (lambda v: v.tensor_scalar(out=R["a"][:, :], in0=R["tmp"][:, :],
                                           scalar1=R["m8b"][:, 3:4], scalar2=None, op0=ALU.is_gt)),
                (lambda v: v.tensor_scalar(out=R["b"][:, :], in0=R["tmp"][:, :],
                                           scalar1=R["m8b"][:, 3:4], scalar2=None, op0=ALU.is_equal)),
            ], reads=[bR], writes=[bR]))
            A(lambda: k.do(D, lambda v: v.reduce_sum(out=R["t1"][:, :], in_=R["b" if False else "a"][:, :], axis=AX.X),
                           reads=[bR], writes=[bR]))
            for (srcb, dstb), s in zip((("b", "c"), ("c", "d"), ("d", "c"), ("c", "d"), ("d", "c")),
                                       (1, 2, 4, 8, 16)):
                A(lambda srcb=srcb, dstb=dstb, s=s: k.group(D, [
                    (lambda v, s=s, srcb=srcb, dstb=dstb: v.tensor_copy(
                        out=R[dstb][:, 0:s], in_=R[srcb][:, 0:s])),
                    (lambda v, s=s, srcb=srcb, dstb=dstb: v.tensor_add(
                        out=R[dstb][:, s:32], in0=R[srcb][:, s:32], in1=R[srcb][:, 0:32 - s])),
                ], reads=[bR], writes=[bR]))
            A(lambda: k.do(D, lambda v: v.tensor_scalar(out=R["d"][:, :], in0=R["c"][:, :],
                                                        scalar1=R["t1"][:, 0:1], scalar2=4.0,
                                                        op0=ALU.add, op1=ALU.is_le),
                           reads=[bR], writes=[bR]))
            A(lambda: k.do(D, lambda v: v.tensor_mul(out=R["d"][:, :], in0=R["d"][:, :], in1=R["b"][:, :]),
                           reads=[bR], writes=[bR]))
            A(lambda: k.do(D, lambda v: v.tensor_add(out=R["d"][:, :], in0=R["d"][:, :], in1=R["a"][:, :]),
                           reads=[bR], writes=[bR]))
            A(lambda: k.do(D, lambda v: v.tensor_mul(out=R["W"][:, :], in0=sc, in1=R["d"][:, :]),
                           reads=[bR, bScores], writes=[bR]))
            A(lambda: k.do(D, lambda v: v.reduce_sum(out=R["t1"][:, :], in_=R["W"][:, :], axis=AX.X),
                           reads=[bR], writes=[bR]))
            A(lambda: k.do(D, lambda v: v.tensor_scalar(out=R["t3"][:, :], in0=R["t1"][:, :],
                                                        scalar1=1e-20, scalar2=None, op0=ALU.add),
                           reads=[bR], writes=[bR]))
            A(lambda: k.do(D, lambda v: v.reciprocal(out=R["t3"][:, :], in_=R["t3"][:, :]),
                           reads=[bR], writes=[bR]))
            A(lambda: k.do(D, lambda v: v.tensor_scalar(
                out=wout, in0=R["W"][:, :], scalar1=R["t3"][:, 0:1], scalar2=1.25,
                op0=ALU.mult, op1=ALU.mult), reads=[bR], writes=[bWrt]))
            if SPARSE:
                A(lambda: k.do(D, lambda v: v.tensor_copy(
                    out=sel_all[:, 32 * t:32 * (t + 1)], in_=R["d"][:, :]),
                    reads=[bR], writes=[bSpool, bMx[0], bMx[1]]))
            return th

        from itertools import zip_longest
        t = 0
        while t < TT:
            chains = []
            for ci in range(NCH):
                if t + ci < TT:
                    chains.append(steps(t + ci, ci))
            for tup in zip_longest(*chains):
                for f in tup:
                    if f is not None:
                        f()
            t += NCH

    def wsrc_aps(e):
        if COLL and e < NSLOT:
            g = gw5_in[e].rearrange("(c p) h -> p c h", p=128)
            u = uw5_in[e].rearrange("(c p) h -> p c h", p=128)
            d = dw5_in[e].rearrange("(c p) i -> p c i", p=128)
            return g, u, d
        if e < NR:
            g = gw_in[e].rearrange("(c p) h -> p c h", p=128)
            u = uw_in[e].rearrange("(c p) h -> p c h", p=128)
            d = dw_in[e].rearrange("(c p) i -> p c i", p=128)
        else:
            a = e - NR
            g = sgw_in[512 * a:512 * (a + 1), :].rearrange("(c p) h -> p c h", p=128)
            u = suw_in[512 * a:512 * (a + 1), :].rearrange("(c p) h -> p c h", p=128)
            d = sdw_in[:, 512 * a:512 * (a + 1)].rearrange("(c p) i -> p c i", p=128)
        return g, u, d

    def emit_weight_load_transpose(e):
        q = e % 2
        wT = wT_sb[q]
        srcs = wsrc_aps(e)
        for mi in range(3):
            p = mi % 2
            nat = wnat[:, 4096 * p:4096 * (p + 1)]
            k.dma("gpsimd", lambda g, src=srcs[mi], nat=nat: g.dma_start(out=nat, in_=src),
                  q="gd", writes=[bWnat[p]])
            if mi < 2:
                for c in range(IC):
                    builds = [(lambda te, c=c, kk=kk, nat=nat: te.transpose(
                        pT[:, 128 * kk:128 * (kk + 1)],
                        nat[:, 1024 * c + 128 * kk:1024 * c + 128 * (kk + 1)],
                        ident[:, :])) for kk in range(KC)]
                    k.group("tensor", builds, reads=[bWnat[p], bConsts], writes=[bPT])
                    dst = wTgu_v[q][:, mi, :, c, :]
                    k.do("vector", lambda v, dst=dst: v.tensor_copy(
                        out=dst, in_=pT[:, 0:KC * 128]),
                        reads=[bPT], writes=[bWT[q]])
            else:
                for c2 in range(0, KC, 2):
                    builds = []
                    for j in range(IC):
                        for cc in range(2):
                            builds.append(lambda te, c=c2 + cc, j=j, cc=cc, nat=nat:
                                          te.transpose(
                                              pTd_v[:, j, cc, :],
                                              nat[:, 512 * c + 128 * j:512 * c + 128 * (j + 1)],
                                              ident[:, :]))
                    k.group("tensor", builds, reads=[bWnat[p], bConsts], writes=[bPT])
                    dst = wTd_v[q][:, :, c2:c2 + 2, :]
                    k.do("vector", lambda v, dst=dst: v.tensor_copy(
                        out=dst, in_=pT[:, 0:1024]),
                        reads=[bPT], writes=[bWT[q]])
        tgt = wstage if (COLL and e < NSLOT) else wcache
        k.dma("sync", lambda s, e=e, wT=wT, tgt=tgt: s.dma_start(out=tgt[e], in_=wT[:, :]),
              q="st", reads=[bWT[q]], writes=[bCache[e % NSLOT if (COLL and e < NR) else e], bCacheAll])

    def emit_weight_cache_load(e):
        q = e % 2
        wT = wT_sb[q]
        k.dma("sync", lambda s, e=e, wT=wT: s.dma_start(out=wT[:, :], in_=wcache[e]),
              q="sd", reads=[bCache[e]], writes=[bWT[q]])

    def emit_expert_compute(e):
        q = e % 2
        wT = wT_sb[q]
        pool = (pD0, pD1, pX)
        for nt in range(NNT):
            tok0 = NTW * nt
            xs = [xfm[:, T * kk + tok0:T * kk + tok0 + NTW] for kk in range(KC)]
            for c in range(IC):
                gb, ub = bPGU[2 * (c % 2)], bPGU[2 * (c % 2) + 1]
                gps, ups = pGU[2 * (c % 2)], pGU[2 * (c % 2) + 1]
                builds = [(lambda te, kk=kk, c=c, gps=gps, wT=wT, x=xs[kk]: te.matmul(
                    gps[:, 0:NTW], wT[:, 512 * kk + 128 * c:512 * kk + 128 * (c + 1)],
                    x, start=(kk == 0), stop=(kk == KC - 1))) for kk in range(KC)]
                k.group("tensor", builds, reads=[bWT[q], bXfm], writes=[gb])
                builds = [(lambda te, kk=kk, c=c, ups=ups, wT=wT, x=xs[kk]: te.matmul(
                    ups[:, 0:NTW],
                    wT[:, 4096 + 512 * kk + 128 * c:4096 + 512 * kk + 128 * (c + 1)],
                    x, start=(kk == 0), stop=(kk == KC - 1))) for kk in range(KC)]
                k.group("tensor", builds, reads=[bWT[q], bXfm], writes=[ub])
                k.do("scalar", lambda s, c=c, gps=gps: s.activation(
                    out=sG[:, NTW * c:NTW * (c + 1)], in_=gps[:, 0:NTW], func=AF.Silu),
                    reads=[gb], writes=[bSG])
                k.do("vector", lambda v, c=c, ups=ups: v.tensor_mul(
                    out=hfm[:, NTW * c:NTW * (c + 1)], in0=ups[:, 0:NTW],
                    in1=sG[:, NTW * c:NTW * (c + 1)]),
                    reads=[ub, bSG], writes=[bHfm])
            gi = 0
            for s_ in range(NSUB):
                t_tile = (tok0 // 128) + s_
                for h2 in range(2):
                    pb = pool[gi % 3]
                    bb = bPDrot[gi % 3]
                    gi += 1
                    builds = [(lambda te, c=c, s_=s_, h2=h2, pb=pb, wT=wT: te.matmul(
                        pb[:, 0:512],
                        hfm[:, NTW * c + 128 * s_:NTW * c + 128 * (s_ + 1)],
                        wT[:, 8192 + 1024 * c + 512 * h2:8192 + 1024 * c + 512 * (h2 + 1)],
                        start=(c == 0), stop=(c == IC - 1))) for c in range(IC)]
                    k.group("tensor", builds, reads=[bHfm, bWT[q]], writes=[bb])
                    xsub = x_sb[:, 1024 * t_tile + 512 * h2:1024 * t_tile + 512 * (h2 + 1)]
                    if e < NR:
                        wcol = wrt[:, 32 * t_tile + e:32 * t_tile + e + 1]
                        k.do("vector", lambda v, pb=pb, xsub=xsub, wcol=wcol:
                             v.scalar_tensor_tensor(
                                 out=xsub, in0=pb[:, 0:512], scalar=wcol, in1=xsub,
                                 op0=ALU.mult, op1=ALU.add),
                             reads=[bb, bWrt, xtile[t_tile]], writes=[xtile[t_tile]])
                    else:
                        k.do("vector", lambda v, pb=pb, xsub=xsub: v.scalar_tensor_tensor(
                            out=xsub, in0=pb[:, 0:512], scalar=0.5, in1=xsub,
                            op0=ALU.mult, op1=ALU.add),
                            reads=[bb, xtile[t_tile]], writes=[xtile[t_tile]])

    BREG = {}

    def _mk_bregs(g):
        if "cv" not in BREG:
            BREG["cv"] = g.to_reg(NSCHED * 128 - 1)
            BREG["tok"] = g.to_reg(T - 1)
            BREG["wc"] = g.to_reg(NE_TBL * 128 - 1)
            BREG["wr"] = g.to_reg(T * 32 - 1)
        return BREG

    def emit_sched(inv):
        SPW = [bSpool, bMx[0], bMx[1]]
        k.do("vector", lambda v: v.tensor_copy(out=selbf[:, 0:32 * TT], in_=sel_all[:, 0:32 * TT]),
             reads=[bSpool], writes=[bSelbf])
        for t in range(TT):
            k.do("tensor", lambda te, t=t: te.matmul(
                pX[:, 32 * t:32 * (t + 1)], triu128[:, :], selbf[:, 32 * t:32 * (t + 1)],
                start=True, stop=True), reads=[bSelbf, bConsts], writes=[bPX])
            k.do("tensor", lambda te, t=t: te.matmul(
                pD0[:, t:t + 1] if False else pD0[0:32, t:t + 1],
                selbf[:, 32 * t:32 * (t + 1)], ones128c[:, :],
                start=True, stop=True), reads=[bSelbf, bConsts], writes=[bPDrot[0]])
        k.do("vector", lambda v: v.tensor_copy(out=tcnt_sb[:, 0:TT], in_=pD0[0:32, 0:TT]),
             reads=[bPDrot[0]], writes=[bSched])
        k.do("tensor", lambda te: te.transpose(pT[0:TT, 0:32], tcnt_sb[0:32, 0:TT],
                                               ident[0:32, 0:32]),
             reads=[bSched, bConsts], writes=[bPT])
        k.do("vector", lambda v: v.tensor_copy(out=tcnt_T[0:TT, 0:32], in_=pT[0:TT, 0:32]),
             reads=[bPT], writes=[bSched])
        k.do("tensor", lambda te: te.matmul(pD0[0:TT, 32:64], triu16s[0:TT, 0:TT],
                                            tcnt_T[0:TT, 0:32], start=True, stop=True),
             reads=[bSched, bConsts], writes=[bPDrot[0]])
        k.do("tensor", lambda te: te.matmul(pD1[0:1, 0:32], ones16c[0:TT, :],
                                            tcnt_T[0:TT, 0:32], start=True, stop=True),
             reads=[bSched, bConsts], writes=[bPDrot[1]])
        k.do("vector", lambda v: v.tensor_copy(out=csum_sb[0:TT, 0:32], in_=pD0[0:TT, 32:64]),
             reads=[bPDrot[0]], writes=[bSched])
        k.do("vector", lambda v: v.tensor_copy(out=erow[:, 0:32], in_=pD1[0:1, 0:32]),
             reads=[bPDrot[1]], writes=[bSched])
        k.do("tensor", lambda te: te.matmul(pD1[0:16, 64:96], ones1[0:1, 0:16],
                                            erow[0:1, 0:32], start=True, stop=True),
             reads=[bSched, bConsts], writes=[bPDrot[1]])
        k.do("vector", lambda v: v.tensor_scalar(out=tcnt_T[0:16, 0:32],
                                                 in0=pD1[0:16, 64:96],
                                                 scalar1=thr16[0:16, 0:1], scalar2=None,
                                                 op0=ALU.is_gt),
             reads=[bPDrot[1], bConsts], writes=[bSched])
        k.do("tensor", lambda te: te.matmul(pD1[0:1, 96:128], ones16c[:, :],
                                            tcnt_T[0:16, 0:32], start=True, stop=True),
             reads=[bSched, bConsts], writes=[bPDrot[1]])
        k.do("vector", lambda v: v.tensor_copy(out=erow[:, 32:64], in_=pD1[0:1, 96:128]),
             reads=[bPDrot[1]], writes=[bSched])
        k.do("vector", lambda v: v.tensor_copy(out=erow[:, 64:96], in_=erow[:, 32:64]),
             reads=[bSched], writes=[bSched])
        a_off, b_off = 64, 96
        for s in (1, 2, 4, 8, 16):
            k.do("vector", lambda v, s=s, a=a_off, b=b_off: v.tensor_copy(
                out=erow[:, b:b + s], in_=erow[:, a:a + s]), reads=[bSched], writes=[bSched])
            k.do("vector", lambda v, s=s, a=a_off, b=b_off: v.tensor_add(
                out=erow[:, b + s:b + 32], in0=erow[:, a + s:a + 32], in1=erow[:, a:a + 32 - s]),
                reads=[bSched], writes=[bSched])
            a_off, b_off = b_off, a_off
        tend_off = a_off
        k.do("vector", lambda v, t=tend_off: v.tensor_sub(out=erow[:, 128:160],
                                                          in0=erow[:, t:t + 32],
                                                          in1=erow[:, 32:64]),
             reads=[bSched], writes=[bSched])
        k.do("vector", lambda v: v.tensor_scalar(out=erow[:, 128:160], in0=erow[:, 128:160],
                                                 scalar1=128.0, scalar2=None, op0=ALU.mult),
             reads=[bSched], writes=[bSched])
        k.do("tensor", lambda te: te.matmul(pD1[0:TT, 32:64], ones1[0:1, 0:TT],
                                            erow[0:1, 128:160], start=True, stop=True),
             reads=[bSched, bConsts], writes=[bPDrot[1]])
        k.do("vector", lambda v: v.tensor_add(out=addrow_s[0:TT, :], in0=csum_sb[0:TT, 0:32],
                                              in1=pD1[0:TT, 32:64]),
             reads=[bSched, bPDrot[1]], writes=[bSched])
        k.do("vector", lambda v: v.tensor_scalar(out=addrow_s[0:TT, :], in0=addrow_s[0:TT, :],
                                                 scalar1=-1.0, scalar2=None, op0=ALU.add),
             reads=[bSched], writes=[bSched])
        k.dma("gpsimd", lambda g: g.dma_start(out=addflat[:, 0:32 * TT], in_=addrow_s[0:TT, :]),
              q="gd", reads=[bSched], writes=[bSched])
        k.do("tensor", lambda te: te.matmul(pD0[:, 0:32 * TT] if 32 * TT <= 512 else None,
                                            ones1[0:1, :], addflat[0:1, 0:32 * TT],
                                            start=True, stop=True),
             reads=[bSched, bConsts], writes=[bPDrot[0]])
        k.do("vector", lambda v: v.tensor_copy(out=abc_s[:, 0:32 * TT], in_=pD0[:, 0:32 * TT]),
             reads=[bPDrot[0]], writes=[bSched, bSpool])
        k.do("vector", lambda v: v.tensor_add(out=dstrow[:, 0:32 * TT], in0=pX[:, 0:32 * TT],
                                              in1=abc_s[:, 0:32 * TT]),
             reads=[bPX, bSched], writes=SPW)
        k.do("vector", lambda v: v.tensor_mul(out=dstrow[:, 0:32 * TT], in0=dstrow[:, 0:32 * TT],
                                              in1=sel_all[:, 0:32 * TT]), reads=SPW, writes=SPW)
        k.do("vector", lambda v: v.tensor_add(out=dstrow[:, 0:32 * TT], in0=dstrow[:, 0:32 * TT],
                                              in1=sel_all[:, 0:32 * TT]), reads=SPW, writes=SPW)
        k.do("vector", lambda v: v.tensor_scalar(out=dstrow[:, 0:32 * TT], in0=dstrow[:, 0:32 * TT],
                                                 scalar1=-1.0, scalar2=None, op0=ALU.add),
             reads=SPW, writes=SPW)
        k.do("gpsimd", lambda g: g.iota(cvio[:, 0:32 * TT], [[4096, TT], [1, 32]], base=0,
                                        channel_multiplier=32,
                                        allow_small_or_imprecise_dtypes=True),
             writes=SPW)
        k.do("vector", lambda v: v.tensor_mul(out=cvio[:, 0:32 * TT], in0=cvio[:, 0:32 * TT],
                                              in1=sel_all[:, 0:32 * TT]), reads=SPW, writes=SPW)
        k.do("vector", lambda v: v.tensor_add(out=cvio[:, 0:32 * TT], in0=cvio[:, 0:32 * TT],
                                              in1=sel_all[:, 0:32 * TT]), reads=SPW, writes=SPW)
        k.do("vector", lambda v: v.tensor_scalar(out=cvio[:, 0:32 * TT], in0=cvio[:, 0:32 * TT],
                                                 scalar1=-1.0, scalar2=None, op0=ALU.add),
             reads=SPW, writes=SPW)
        for t in range(TT):
            k.do("vector", lambda v, t=t: v.max(out=dst4[:, :], in_=dstrow[:, 32 * t:32 * (t + 1)]),
                 reads=SPW, writes=[bD4])
            k.do("vector", lambda v, t=t: v.max(out=cv4[:, :], in_=cvio[:, 32 * t:32 * (t + 1)]),
                 reads=SPW, writes=[bD4])
            k.do("vector", lambda v, t=t: v.tensor_copy(out=dst4a[:, 4 * t:4 * (t + 1)],
                                                        in_=dst4[:, 0:4]),
                 reads=[bD4], writes=[bD4])
            k.do("vector", lambda v, t=t: v.tensor_copy(out=cv4a[:, 4 * t:4 * (t + 1)],
                                                        in_=cv4[:, 0:4]),
                 reads=[bD4], writes=[bD4])
        k.do("vector", lambda v: v.tensor_copy(out=dst4i[:, 0:4 * TT], in_=dst4a[:, 0:4 * TT]),
             reads=[bD4], writes=[bD4])
        cvd_j = cv_d.rearrange("(j p) o -> j (p o)", p=128)
        for kq in range(4):
            k.dma("sync", lambda s, kq=kq: s.dma_start(
                out=cvd_j[:, 32 * kq:32 * (kq + 1)],
                in_=sent_t[0:NSCHED, 0:32]),
                q="sd", reads=[bConsts], writes=[bCvD])
        for c in range(4 * TT):
            if "cvs" not in DBG_SKIP:
                k.dma("gpsimd", lambda g, c=c: g.indirect_dma_start(
                    out=cv_d, out_offset=IndirectOffsetOnAxis(dst4i[:, c:c + 1], 0),
                    in_=cv4a[:, c:c + 1], in_offset=None,
                    bounds_check=_mk_bregs(g)["cv"], oob_is_err=False),
                    q="gd", reads=[bD4], writes=[bCvD])
        k.do("tensor", lambda te, t=tend_off: te.matmul(
            pD1[0:NSCHED, 0:32] if NSCHED <= 128 else None, ones1[0:1, 0:NSCHED],
            erow[0:1, t:t + 32], start=True, stop=True),
            reads=[bSched, bConsts], writes=[bPDrot[1]])
        k.do("vector", lambda v: v.tensor_scalar(out=abc_s[0:NSCHED, 0:32],
                                                 in0=pD1[0:NSCHED, 0:32],
                                                 scalar1=jcol[0:NSCHED, 0:1], scalar2=None,
                                                 op0=ALU.is_le),
             reads=[bPDrot[1], bConsts], writes=[bSched, bSpool])
        k.do("vector", lambda v: v.reduce_sum(out=schedc[0:NSCHED, :], in_=abc_s[0:NSCHED, 0:32],
                                              axis=AX.X), reads=[bSched, bSpool], writes=[bSched])
        k.do("vector", lambda v: v.tensor_scalar(out=schedbc[0:NSCHED, 0:1],
                                                 in0=schedc[0:NSCHED, :], scalar1=float(NR),
                                                 scalar2=None, op0=ALU.is_equal),
             reads=[bSched], writes=[bSched])
        k.do("vector", lambda v: v.scalar_tensor_tensor(out=schedc[0:NSCHED, :],
                                                        in0=schedbc[0:NSCHED, 0:1],
                                                        scalar=float(255 - NR),
                                                        in1=schedc[0:NSCHED, :],
                                                        op0=ALU.mult, op1=ALU.add),
             reads=[bSched], writes=[bSched])
        k.do("tensor", lambda te: te.transpose(pD1[0:1, 32:32 + NSCHED],
                                               schedc[0:NSCHED, 0:1],
                                               identf[0:NSCHED, 0:NSCHED]),
             reads=[bSched, bConsts], writes=[bPDrot[1]])
        k.do("vector", lambda v: v.tensor_copy(out=addflat[:, 0:NSCHED],
                                               in_=pD1[0:1, 32:32 + NSCHED]),
             reads=[bPDrot[1]], writes=[bSched])
        k.do("tensor", lambda te: te.matmul(pD0[:, 0:NSCHED], ones1[0:1, :],
                                            addflat[0:1, 0:NSCHED], start=True, stop=True),
             reads=[bSched, bConsts], writes=[bPDrot[0]])
        k.do("vector", lambda v: v.tensor_scalar(out=idxwi[:, :], in0=pD0[:, 0:NSCHED],
                                                 scalar1=128.0, scalar2=pcol[:, 0:1],
                                                 op0=ALU.mult, op1=ALU.add),
             reads=[bPDrot[0], bConsts], writes=[bCvm])
        nb8 = (NSCHED + 7) // 8
        for b in range(nb8):
            j0 = 8 * b
            jn = min(8, NSCHED - j0)
            k.dma("sync", lambda s, j0=j0, jn=jn: s.dma_start(
                out=cvload[0:jn, :],
                in_=cv_d[128 * j0:128 * (j0 + jn)].rearrange("(j p) o -> j (p o)", p=128)),
                q="sd", reads=[bCvD], writes=[bSched])
            k.do("tensor", lambda te, jn=jn: te.transpose(
                pD1[:, 64:64 + jn] if True else None, cvload[0:jn, :], identf[0:jn, 0:jn]),
                reads=[bSched, bConsts], writes=[bPDrot[1]])
            k.do("vector", lambda v, j0=j0, jn=jn: v.tensor_copy(
                out=rowi[:, j0:j0 + jn], in_=pD1[:, 64:64 + jn]),
                reads=[bPDrot[1]], writes=[bCvm])
        k.do("vector", lambda v: v.tensor_scalar(out=rowi[:, :], in0=rowi[:, :],
                                                 scalar1=5, scalar2=None,
                                                 op0=ALU.arith_shift_right),
             reads=[bCvm], writes=[bCvm])

    DBG_SKIP = set(cfg.get("dbg_skip", ()))

    def emit_w4_prep():
        k.do("vector", lambda v: v.tensor_copy(out=cv4i[:, 0:4 * TT], in_=cv4a[:, 0:4 * TT]),
             reads=[bD4], writes=[bD4])
        k.do("vector", lambda v: v.tensor_scalar(out=cv4i[:, 0:4 * TT], in0=cv4i[:, 0:4 * TT],
                                                 scalar1=31, scalar2=None,
                                                 op0=ALU.bitwise_and),
             reads=[bD4], writes=[bD4])
        k.do("vector", lambda v: v.tensor_copy(out=e4f[:, 0:4 * TT], in_=cv4i[:, 0:4 * TT]),
             reads=[bD4], writes=[bD4])

    def emit_w4_step(c):
        t = c // 4
        k.do("vector", lambda v, c=c: v.tensor_scalar(
            out=ohscr[:, :], in0=iota_e[:, :], scalar1=e4f[:, c:c + 1],
            scalar2=None, op0=ALU.is_equal),
            reads=[bD4, bConsts], writes=[bD4])
        k.do("vector", lambda v, t=t: v.tensor_mul(
            out=ohscr[:, :], in0=ohscr[:, :], in1=wrt[:, 32 * t:32 * (t + 1)]),
            reads=[bD4, bWrt], writes=[bD4])
        k.do("vector", lambda v, c=c: v.reduce_sum(
            out=w4a[:, c:c + 1], in_=ohscr[:, :], axis=AX.X),
            reads=[bD4], writes=[bD4])

    def sp_gather(j):
        q = j % 2
        gb = wnat[:, 2048 * q:2048 * q + 1024]
        if "rowg" not in DBG_SKIP:
            k.dma("gpsimd", lambda g, j=j, gb=gb: g.indirect_dma_start(
                out=gb, out_offset=None,
                in_=xln_d, in_offset=IndirectOffsetOnAxis(rowi[:, j:j + 1], 0),
                bounds_check=_mk_bregs(g)["tok"], oob_is_err=False),
                q="gd", reads=[bCvm, bXlnD], writes=[bGb[q]])
        if "wg" not in DBG_SKIP:
            k.dma("gpsimd", lambda g, j=j, q=q: g.indirect_dma_start(
                out=wT_sb[q][:, :], out_offset=None,
                in_=wcache.rearrange("e p f -> (e p) f"),
                in_offset=IndirectOffsetOnAxis(idxwi[:, j:j + 1], 0),
                bounds_check=_mk_bregs(g)["wc"], oob_is_err=False),
                q="gd", reads=[bCvm, bCacheAll], writes=[bWT[q]])


    def sp_tile_a(j):
        q = j % 2
        gb = wnat[:, 2048 * q:2048 * q + 1024]
        gf = wnat[:, 4096 + 2048 * q:4096 + 2048 * q + 1024]
        wT = wT_sb[q]
        builds = [(lambda te, kk=kk, gb=gb: te.transpose(
            pT[:, 128 * kk:128 * (kk + 1)], gb[:, 128 * kk:128 * (kk + 1)], ident[:, :]))
            for kk in range(KC)]
        k.group("tensor", builds, reads=[bGb[q], bConsts], writes=[bPT])
        k.do("vector", lambda v, gf=gf: v.tensor_copy(out=gf, in_=pT[:, 0:1024]),
             reads=[bPT], writes=[bGf[q]])
        for c in range(IC):
            builds = [(lambda te, kk=kk, c=c, gf=gf, wT=wT: te.matmul(
                pGU[0][:, 128 * c:128 * (c + 1)],
                wT[:, 512 * kk + 128 * c:512 * kk + 128 * (c + 1)],
                gf[:, 128 * kk:128 * (kk + 1)],
                start=(kk == 0), stop=(kk == KC - 1))) for kk in range(KC)]
            k.group("tensor", builds, reads=[bWT[q], bGf[q]], writes=[bPGU[0]])
        for c in range(IC):
            builds = [(lambda te, kk=kk, c=c, gf=gf, wT=wT: te.matmul(
                pGU[1][:, 128 * c:128 * (c + 1)],
                wT[:, 4096 + 512 * kk + 128 * c:4096 + 512 * kk + 128 * (c + 1)],
                gf[:, 128 * kk:128 * (kk + 1)],
                start=(kk == 0), stop=(kk == KC - 1))) for kk in range(KC)]
            k.group("tensor", builds, reads=[bWT[q], bGf[q]], writes=[bPGU[1]])
        k.do("scalar", lambda s, q=q: s.activation(
            out=sG[:, 512 * q:512 * (q + 1)], in_=pGU[0][:, 0:512], func=AF.Silu),
            reads=[bPGU[0]], writes=[bSG])
        k.do("vector", lambda v, q=q: v.tensor_mul(
            out=hfm[:, 512 * q:512 * (q + 1)], in0=pGU[1][:, 0:512],
            in1=sG[:, 512 * q:512 * (q + 1)]),
            reads=[bPGU[1], bSG], writes=[bHfm])

    def sp_tile_b(j, first=False):
        q = j % 2
        wT = wT_sb[q]
        gf = wnat[:, 4096 + 2048 * q:4096 + 2048 * q + 1024]
        for h2 in range(2):
            pb = (pD0, pD1)[h2]
            builds = [(lambda te, c=c, h2=h2, pb=pb, q=q, wT=wT: te.matmul(
                pb[:, 0:512],
                hfm[:, 512 * q + 128 * c:512 * q + 128 * (c + 1)],
                wT[:, 8192 + 1024 * c + 512 * h2:8192 + 1024 * c + 512 * (h2 + 1)],
                start=(c == 0), stop=(c == IC - 1))) for c in range(IC)]
            k.group("tensor", builds, reads=[bHfm, bWT[q]], writes=[bPDrot[h2]])
            k.do("vector", lambda v, pb=pb, h2=h2, gf=gf: v.tensor_copy(
                out=gf[:, 512 * h2:512 * (h2 + 1)], in_=pb[:, 0:512]),
                reads=[bPDrot[h2]], writes=[bGf[q]])
        k.dma("scalar", lambda s, j=j, gf=gf: s.dma_start(
            out=pout_d[128 * j:128 * (j + 1), :], in_=gf[:, :]),
            q="cd", reads=[bGf[q]], writes=[bPoutD])

    def emit_sparse_tiles():
        k.do("gpsimd", lambda g: g.engine_nop(),
             writes=[bWnat[0], bWnat[1], bGb[0], bGb[1], bGf[0], bGf[1]])
        sp_gather(0)
        emit_w4_prep()
        for j in range(NSCHED):
            if j + 1 < NSCHED:
                sp_gather(j + 1)
            sp_tile_a(j)
            sp_tile_b(j, first=(j < 2))
            if j < 4 * TT:
                emit_w4_step(j)

    def emit_mix_merge():
        gbm = [wnat[:, 0:1024], wnat[:, 2048:3072]]
        for t in range(TT):
            xt = x_sb[:, 1024 * t:1024 * (t + 1)]
            for half in range(2):
                for m in range(2):
                    sl = 2 * half + m
                    k.dma("gpsimd", lambda g, t=t, sl=sl, m=m, gbm=gbm: g.indirect_dma_start(
                        out=gbm[m][:, :], out_offset=None,
                        in_=pout_d,
                        in_offset=IndirectOffsetOnAxis(dst4i[:, 4 * t + sl:4 * t + sl + 1], 0),
                        bounds_check=_mk_bregs(g)["cv"], oob_is_err=False),
                        q="gd", reads=[bD4, bPoutD], writes=[bGb[m]])
                for m in range(2):
                    sl = 2 * half + m
                    k.do("vector", lambda v, xt=xt, m=m, c=4 * t + sl, gbm=gbm: v.scalar_tensor_tensor(
                        out=xt, in0=gbm[m][:, :], scalar=w4a[:, c:c + 1], in1=xt,
                        op0=ALU.mult, op1=ALU.add),
                        reads=[bGb[m], bD4, xtile[t]], writes=[xtile[t]])

    def emit_ffn(inv):
        if SPARSE:
            emit_ln(dram=True)
            emit_routing()
            if inv == 0:
                k.do("gpsimd", lambda g: g.engine_nop(),
                     writes=[bPT, bPTh[0], bPTh[1]])
                if COLL:
                    for s in range(NSLOT):
                        emit_weight_load_transpose(s)
                    emit_weight_load_transpose(NR)
                    emit_weight_load_transpose(NR + 1)
                    for s in range(NSLOT):
                        k.do("gpsimd", lambda g, s=s: g.collective_compute(
                            "AllGather", ALU.bypass,
                            replica_groups=[list(range(8))],
                            ins=[wstage[s]],
                            outs=[wcache[8 * s:8 * (s + 1)].rearrange("e p f -> (e p) f")]),
                            reads=[bCache[s], bCacheAll], writes=[bCacheAll], sem="cc")
                else:
                    for e in range(NE):
                        emit_weight_load_transpose(e)
                k.do("gpsimd", lambda g: g.engine_nop(),
                     writes=[bPT, bPTh[0], bPTh[1]])
            emit_sched(inv)
            if inv > 0 or COLL:
                emit_weight_cache_load(NR)
                emit_weight_cache_load(NR + 1)
            emit_expert_compute(NR)
            emit_expert_compute(NR + 1)
            emit_sparse_tiles()
            emit_mix_merge()
            return
        emit_ln()
        emit_routing()
        if inv == 0:
            emit_weight_load_transpose(0)
        else:
            emit_weight_cache_load(0)
        for e in range(NE):
            if e + 1 < NE:
                if inv == 0:
                    emit_weight_load_transpose(e + 1)
                else:
                    emit_weight_cache_load(e + 1)
            emit_expert_compute(e)

    def emit_attention():
        emit_ln()
        nq = TT
        for (wb_, dest, bank, bbank) in ((wqb, qfm, pGU[0], bPGU[0]),
                                         (wkb, kfm, pGU[1], bPGU[1])):
            for nt in range(NNT):
                builds = [(lambda te, kk=kk, nt=nt, wb_=wb_, bank=bank: te.matmul(
                    bank[:, 0:NTW], wb_[:, 128 * kk:128 * (kk + 1)],
                    xfm[:, T * kk + NTW * nt:T * kk + NTW * (nt + 1)],
                    start=(kk == 0), stop=(kk == KC - 1))) for kk in range(KC)]
                k.group("tensor", builds, reads=[bXfm, bConsts], writes=[bbank])
                k.do("vector", lambda v, nt=nt, dest=dest, bank=bank: v.tensor_copy(
                    out=dest[:, NTW * nt:NTW * (nt + 1)], in_=bank[:, 0:NTW]),
                    reads=[bbank], writes=[bWT[1]])
        t = 0
        while t < nq:
            gn = min(4, nq - t)
            for ts_ in range(gn):
                builds = [(lambda te, kk=kk, tt_=t + ts_, ts_=ts_: te.matmul(
                    pGU[2][:, 128 * ts_:128 * (ts_ + 1)],
                    xfm[:, T * kk + 128 * tt_:T * kk + 128 * (tt_ + 1)],
                    wvb[:, 128 * kk:128 * (kk + 1)],
                    start=(kk == 0), stop=(kk == KC - 1))) for kk in range(KC)]
                k.group("tensor", builds, reads=[bXfm, bConsts], writes=[bPGU[2]])
            k.do("vector", lambda v, t=t, gn=gn: v.tensor_copy(
                out=vtm[:, 128 * t:128 * (t + gn)], in_=pGU[2][:, 0:128 * gn]),
                reads=[bPGU[2]], writes=[bWT[1]])
            t += gn

        def ph1(qi):
            W = (qi + 1) * 128
            nb = (qi + 4) // 4
            pr = qi % 2
            AT = ATS[pr]
            bA = bAtt[pr]
            ar = arow2[pr]
            for j in range(qi + 1):
                bk = j // 4
                k.do("tensor", lambda te, qi=qi, j=j, bk=bk: te.matmul(
                    pGU[bk][:, 128 * (j % 4):128 * (j % 4 + 1)],
                    qfm[:, 128 * qi:128 * (qi + 1)], kfm[:, 128 * j:128 * (j + 1)],
                    start=True, stop=True),
                    reads=[bWT[1]], writes=[bPGU[bk]])
            dbk = qi // 4
            k.do("vector", lambda v, qi=qi, dbk=dbk: v.tensor_add(
                out=pGU[dbk][:, 128 * (qi % 4):128 * (qi % 4 + 1)],
                in0=pGU[dbk][:, 128 * (qi % 4):128 * (qi % 4 + 1)], in1=maskc[:, :]),
                reads=[bPGU[dbk], bConsts], writes=[bPGU[dbk]])
            k.group("vector", [
                (lambda v, b=b, wb2=min(512, W - 512 * b), AT=AT: v.reduce_max(
                    out=AT["part"][:, b:b + 1], in_=pGU[b][:, 0:wb2], axis=AX.X))
                for b in range(nb)],
                reads=[bPGU[b2] for b2 in range(nb)], writes=[bA])
            k.do("vector", lambda v, nb=nb, AT=AT: v.reduce_max(
                out=AT["nm"][:, :], in_=AT["part"][:, 0:nb], axis=AX.X, negate=True),
                reads=[bA], writes=[bA])
            k.do("vector", lambda v, AT=AT: v.tensor_scalar(
                out=AT["nms"][:, :], in0=AT["nm"][:, :], scalar1=SCALE_ATT, scalar2=None,
                op0=ALU.mult), reads=[bA], writes=[bA])
            for b in range(nb):
                wb2 = min(512, W - 512 * b)
                k.do("scalar", lambda s, b=b, wb2=wb2, AT=AT, ar=ar: s.activation(
                    out=ar[:, 512 * b:512 * b + wb2], in_=pGU[b][:, 0:wb2],
                    func=AF.Exp, bias=AT["nms"][:, 0:1], scale=SCALE_ATT),
                    reads=[bPGU[b], bA], writes=[bArow[pr]])
            k.do("vector", lambda v, W=W, AT=AT, ar=ar: v.reduce_sum(
                out=AT["sum"][:, :], in_=ar[:, 0:W], axis=AX.X),
                reads=[bArow[pr]], writes=[bA])
            k.do("vector", lambda v, AT=AT: v.reciprocal(out=AT["rinv"][:, :], in_=AT["sum"][:, :]),
                 reads=[bA], writes=[bA])

        def ph2(qi):
            pr = qi % 2
            AT = ATS[pr]
            bA = bAtt[pr]
            ar = arow2[pr]
            j0 = 0
            while j0 <= qi:
                jn = min(8, qi + 1 - j0)
                builds = [(lambda te, j=j0 + jj, jj=jj, ar=ar: te.transpose(
                    pT[:, 128 * jj:128 * (jj + 1)], ar[:, 128 * j:128 * (j + 1)],
                    ident[:, :])) for jj in range(jn)]
                k.group("tensor", builds, reads=[bArow[pr], bConsts], writes=[bPT])
                k.do("vector", lambda v, j0=j0, jn=jn: v.tensor_copy(
                    out=aT[:, 128 * j0:128 * (j0 + jn)], in_=pT[:, 0:128 * jn]),
                    reads=[bPT], writes=[bAT])
                j0 += jn
            builds = [(lambda te, j=j, qi=qi: te.matmul(
                pX[:, 0:128], aT[:, 128 * j:128 * (j + 1)],
                vtm[:, 128 * j:128 * (j + 1)],
                start=(j == 0), stop=(j == qi))) for j in range(qi + 1)]
            k.group("tensor", builds, reads=[bAT, bWT[1]], writes=[bPX])
            k.do("vector", lambda v, AT=AT: v.tensor_scalar(
                out=ctxscr[:, :], in0=pX[:, 0:128], scalar1=AT["rinv"][:, 0:1],
                scalar2=None, op0=ALU.mult), reads=[bPX, bA], writes=[bCtxS])
            k.do("tensor", lambda te: te.transpose(pT[:, 0:128], ctxscr[:, :], ident[:, :]),
                 reads=[bCtxS, bConsts], writes=[bPT])
            k.do("vector", lambda v, qi=qi: v.tensor_copy(
                out=ctxfm[:, 128 * qi:128 * (qi + 1)], in_=pT[:, 0:128]),
                reads=[bPT], writes=[bWT[1]])

        k.do("gpsimd", lambda g: g.engine_nop(),
             writes=[bXln[0], bXln[1], bArow[0]])
        ph1(0)
        for qi in range(1, nq):
            ph1(qi)
            ph2(qi - 1)
        ph2(nq - 1)
        k.do("gpsimd", lambda g: g.engine_nop(),
             writes=[bArow[0], bXln[0], bXln[1]])

        for t in range(nq):
            for h2 in range(2):
                pb = (pD0, pD1)[h2]
                bb = bPDrot[h2]
                k.do("tensor", lambda te, t=t, h2=h2, pb=pb: te.matmul(
                    pb[:, 0:512], ctxfm[:, 128 * t:128 * (t + 1)],
                    wob[:, 512 * h2:512 * (h2 + 1)], start=True, stop=True),
                    reads=[bWT[1], bConsts], writes=[bb])
                xsub = x_sb[:, 1024 * t + 512 * h2:1024 * t + 512 * (h2 + 1)]
                k.do("vector", lambda v, pb=pb, xsub=xsub: v.tensor_add(
                    out=xsub, in0=xsub, in1=pb[:, 0:512]),
                    reads=[bb, xtile[t]], writes=[xtile[t]])

    with ExitStack() as semctx:
        for n in SEMS:
            k.sem[n] = semctx.enter_context(nc.semaphore(n))
        emit_prog()
        with nc.Block() as blk:
            @blk.sync
            def _(eng):
                for f in k.ops["sync"]:
                    f(eng)

            @blk.tensor
            def _(eng):
                for f in k.ops["tensor"]:
                    f(eng)

            @blk.vector
            def _(eng):
                for f in k.ops["vector"]:
                    f(eng)

            @blk.scalar
            def _(eng):
                for f in k.ops["scalar"]:
                    f(eng)

            @blk.gpsimd
            def _(eng):
                for f in k.ops["gpsimd"]:
                    f(eng)

    ctx.close()
    return nc


_IN_NAMES = {
    "x": "L0_attn_x", "wq": "L0_attn_wq", "wk": "L0_attn_wk", "wv": "L0_attn_wv",
    "wo": "L0_attn_wo", "rw": "L0_ffn_router_weight", "rb": "L0_ffn_router_bias",
    "gw": "L0_ffn_gate_w", "uw": "L0_ffn_up_w", "dw": "L0_ffn_down_w",
    "sgw": "L0_ffn_shared_gate_w", "suw": "L0_ffn_shared_up_w",
    "sdw": "L0_ffn_shared_down_w",
}


def kernel(**inputs):
    nc = build_program({"collective": True})
    g = lambda n: np.asarray(inputs[n])
    common = {
        "wq": np.ascontiguousarray(g("L0_attn_wq")),
        "wk": np.ascontiguousarray(g("L0_attn_wk")),
        "wv": np.ascontiguousarray(g("L0_attn_wv")),
        "wo": np.ascontiguousarray(g("L0_attn_wo")),
        "rw": np.ascontiguousarray(g("L0_ffn_router_weight")),
        "rb": np.ascontiguousarray(g("L0_ffn_router_bias")),
        "sgw": np.ascontiguousarray(g("L0_ffn_shared_gate_w")),
        "suw": np.ascontiguousarray(g("L0_ffn_shared_up_w")),
        "sdw": np.ascontiguousarray(g("L0_ffn_shared_down_w")),
    }
    gw, uw, dw = g("L0_ffn_gate_w"), g("L0_ffn_up_w"), g("L0_ffn_down_w")
    sgw, suw, sdw = g("L0_ffn_shared_gate_w"), g("L0_ffn_shared_up_w"), g("L0_ffn_shared_down_w")
    x = np.ascontiguousarray(g("L0_attn_x"))
    zg = np.zeros((512, 1024), np.float32)
    zd = np.zeros((1024, 512), np.float32)

    def slot(e, kind):
        if e < 32:
            return {"g": gw[e], "u": uw[e], "d": dw[e]}[kind]
        if e in (32, 33):
            a = e - 32
            if kind == "g":
                return sgw[512 * a:512 * (a + 1), :]
            if kind == "u":
                return suw[512 * a:512 * (a + 1), :]
            return np.ascontiguousarray(sdw[:, 512 * a:512 * (a + 1)])
        return zg if kind in ("g", "u") else zd

    in_maps = []
    for c in range(8):
        es = [8 * s + c for s in range(4)]
        in_maps.append(dict(
            common, x=x[c],
            gw5=np.ascontiguousarray(np.stack([slot(e, "g") for e in es])),
            uw5=np.ascontiguousarray(np.stack([slot(e, "u") for e in es])),
            dw5=np.ascontiguousarray(np.stack([slot(e, "d") for e in es])),
        ))
    res = run_bass_kernel_spmd(nc, in_maps, core_ids=list(range(8)))
    return np.stack([res.results[c]["y"] for c in range(8)], axis=0)


# version 29
# speedup vs baseline: 1.0420x; 1.0420x over previous
import numpy as np
import concourse.bass as bass
import concourse.mybir as mybir
from concourse.bass_utils import run_bass_kernel_spmd
from concourse.bass import IndirectOffsetOnAxis

F32 = mybir.dt.float32
BF16 = mybir.dt.bfloat16
I32 = mybir.dt.int32
AF = mybir.ActivationFunctionType
ALU = mybir.AluOpType
AX = mybir.AxisListType

ENGS = ("sync", "tensor", "vector", "scalar", "gpsimd")
ESEM = {"tensor": "pe", "vector": "dve", "scalar": "act", "gpsimd": "pool", "sync": "sq"}
QLANES = {"sd": 8, "gd": 8, "st": 8}
DMAQS = ("sd", "gd", "st")
SEMS = ("pe", "dve", "act", "pool", "sq", "cc") + tuple(
    f"{q}{i}" for q in DMAQS for i in range(QLANES[q]))

SCALE_ATT = 1.0 / (128.0 ** 0.5)
NEG_BIG = -1.0e30


class Buf:
    __slots__ = ("w", "readers")

    def __init__(self):
        self.w = None
        self.readers = {}

    def rdeps(self):
        return [self.w] if self.w else []

    def wdeps(self):
        d = [(n, m) for n, m in self.readers.items()]
        if self.w:
            d.append(self.w)
        return d

    def note_w(self, mark):
        self.w = mark
        self.readers = {}

    def note_r(self, mark):
        n, m = mark
        if self.readers.get(n, 0) < m:
            self.readers[n] = m


class Em:

    def __init__(self, nc):
        self.nc = nc
        self.ops = {e: [] for e in ENGS}
        self.cnt = {n: 0 for n in SEMS}
        self.waited = {e: {n: 0 for n in SEMS} for e in ENGS}
        self.sem = {}
        self.qrot = {q: 0 for q in DMAQS}

    def wait(self, e, dep):
        n, m = dep
        if m <= 0 or self.waited[e][n] >= m:
            return
        self.waited[e][n] = m
        s = self.sem[n]
        self.ops[e].append(lambda eng, s=s, m=m: eng.wait_ge(s, m))

    def waits(self, e, deps):
        for d in deps:
            self.wait(e, d)

    def raw(self, e, build):
        self.ops[e].append(build)

    def op(self, e, build, sem=None, dn=1):
        n = sem or ESEM[e]
        self.cnt[n] += dn
        m = self.cnt[n]
        s = self.sem[n]
        self.ops[e].append(lambda eng, b=build, s=s, d=dn: b(eng).then_inc(s, d))
        return (n, m)

    def do(self, e, build, reads=(), writes=(), sem=None, dn=1):
        for b in reads:
            self.waits(e, b.rdeps())
        for b in writes:
            self.waits(e, b.wdeps())
        m = self.op(e, build, sem, dn)
        for b in reads:
            b.note_r(m)
        for b in writes:
            b.note_w(m)
        return m

    def dma(self, e, build, q, reads=(), writes=()):
        i = self.qrot[q]
        self.qrot[q] += 1
        lane = f"{q}{i % QLANES[q]}"
        prev = self.cnt[lane]
        if prev > 0:
            self.wait(e, (lane, prev))
        for b in reads:
            self.waits(e, b.rdeps())
        for b in writes:
            self.waits(e, b.wdeps())
        m = self.op(e, build, sem=lane, dn=16)
        for b in reads:
            b.note_r(m)
        for b in writes:
            b.note_w(m)
        return m

    def group(self, e, builds, reads=(), writes=(), sem=None, dn=1):
        for b in reads:
            self.waits(e, b.rdeps())
        for b in writes:
            self.waits(e, b.wdeps())
        for b in builds[:-1]:
            self.raw(e, b)
        m = self.op(e, builds[-1], sem, dn)
        for b in reads:
            b.note_r(m)
        for b in writes:
            b.note_w(m)
        return m


def build_program(cfg):
    T = int(cfg.get("T", 2048))
    NR = int(cfg.get("n_routed", 32))
    phases = cfg.get("phases", ("ffn1", "attn", "ffn2"))
    H, I, DH, SH_I = 1024, 512, 128, 1024
    NE = NR + 2
    TT = T // 128
    NTW = min(512, T)
    NNT = T // NTW
    KC = H // 128
    IC = I // 128
    NSUB = NTW // 128
    assert T % 128 == 0 and 32 * TT <= 512

    nc = bass.Bass("TRN2", target_bir_lowering=False)

    COLL = bool(cfg.get("collective", False))
    NSLOT = 4
    x_in = nc.dram_tensor("x", [T, H], F32, kind="ExternalInput").ap()
    wq_in = nc.dram_tensor("wq", [H, DH], F32, kind="ExternalInput").ap()
    wk_in = nc.dram_tensor("wk", [H, DH], F32, kind="ExternalInput").ap()
    wv_in = nc.dram_tensor("wv", [H, DH], F32, kind="ExternalInput").ap()
    wo_in = nc.dram_tensor("wo", [DH, H], F32, kind="ExternalInput").ap()
    rw_in = nc.dram_tensor("rw", [32, H], F32, kind="ExternalInput").ap()
    rb_in = nc.dram_tensor("rb", [32], F32, kind="ExternalInput").ap()
    if COLL:
        gw5_in = nc.dram_tensor("gw5", [NSLOT, I, H], F32, kind="ExternalInput").ap()
        uw5_in = nc.dram_tensor("uw5", [NSLOT, I, H], F32, kind="ExternalInput").ap()
        dw5_in = nc.dram_tensor("dw5", [NSLOT, H, I], F32, kind="ExternalInput").ap()
        sgw_in = nc.dram_tensor("sgw", [SH_I, H], F32, kind="ExternalInput").ap()
        suw_in = nc.dram_tensor("suw", [SH_I, H], F32, kind="ExternalInput").ap()
        sdw_in = nc.dram_tensor("sdw", [H, SH_I], F32, kind="ExternalInput").ap()
        NE_TBL = 8 * NSLOT + 2
        wstage = nc.dram_tensor("wstage", [NSLOT, 128, 12288], BF16, kind="Internal").ap()
    else:
        gw_in = nc.dram_tensor("gw", [32, I, H], F32, kind="ExternalInput").ap()
        uw_in = nc.dram_tensor("uw", [32, I, H], F32, kind="ExternalInput").ap()
        dw_in = nc.dram_tensor("dw", [32, H, I], F32, kind="ExternalInput").ap()
        sgw_in = nc.dram_tensor("sgw", [SH_I, H], F32, kind="ExternalInput").ap()
        suw_in = nc.dram_tensor("suw", [SH_I, H], F32, kind="ExternalInput").ap()
        sdw_in = nc.dram_tensor("sdw", [H, SH_I], F32, kind="ExternalInput").ap()
        NE_TBL = NE
    y_out = nc.dram_tensor("y", [T, H], F32, kind="ExternalOutput").ap()
    wcache = nc.dram_tensor("wcache", [NE_TBL, 128, 12288], BF16, kind="Internal").ap()
    SPARSE = bool(cfg.get("sparse", True))
    NSCHED = (4 * T) // 128 + 32
    xln_d = nc.dram_tensor("xln_d", [T, 1024], BF16, kind="Internal").ap()
    wrt_d = nc.dram_tensor("wrt_d", [T * 32, 1], F32, kind="Internal").ap()
    cv_d = nc.dram_tensor("cv_d", [NSCHED * 128, 1], F32, kind="Internal").ap()
    pout_d = nc.dram_tensor("pout_d", [NSCHED * 128, 1024], BF16, kind="Internal").ap()

    from contextlib import ExitStack
    ctx = ExitStack()
    e_ = ctx.enter_context

    def sb(name, shape, dt):
        return e_(nc.sbuf_tensor(name, shape, dt))

    def psum(name, shape, dt):
        return e_(nc.psum_tensor(name, shape, dt))

    x_sb = sb("x_sb", [128, TT * 1024], F32)
    xfm = sb("xfm", [128, KC * T], BF16)
    xln = sb("xln", [128, 2 * 1024], BF16)
    wnat = sb("wnat", [128, 2 * 4096], BF16)
    wT_sb = [sb("wT0", [128, 12288], BF16), sb("wT1", [128, 12288], BF16)]
    sG = sb("sG", [128, IC * NTW], BF16)
    hfm = sb("hfm", [128, IC * NTW], BF16)
    assert 4 * T <= 12288
    qfm = wT_sb[1][:, 0 * T:1 * T]
    kfm = wT_sb[1][:, 1 * T:2 * T]
    vtm = wT_sb[1][:, 2 * T:3 * T]
    ctxfm = wT_sb[1][:, 3 * T:4 * T]
    arow2 = [xln[:, 0:T] if T <= 2048 else None, sG[:, 0:T]]
    aT = hfm[:, 0:T]
    ctxscr = sb("ctxscr", [128, 128], BF16)
    wqb = sb("wqb", [128, KC * 128], BF16)
    wkb = sb("wkb", [128, KC * 128], BF16)
    wvb = sb("wvb", [128, KC * 128], BF16)
    wob = sb("wob", [128, 1024], BF16)
    rwT = sb("rwT", [128, KC * 32], BF16)
    rw_nat = sb("rw_nat", [32, 1024], BF16)
    bias_row = sb("bias_row", [1, 32], F32)
    bias_bc = sb("bias_bc", [128, 32], F32)
    ones1 = sb("ones1", [1, 128], F32)
    ident = sb("ident", [128, 128], BF16)
    identf = sb("identf", [128, 128], F32)
    maskc = sb("maskc", [128, 128], F32)
    scores_all = sb("scores_all", [128, TT * 32], F32)
    wrt = sb("wrt", [128, TT * 32], F32)
    bnscr = sb("bnscr", [128, 24], F32)
    mv = sb("mv", [128, 2 * TT], F32)
    rstd = sb("rstd", [128, TT], F32)
    nmrs = sb("nmrs", [128, TT], F32)
    lnt1 = sb("lnt1", [128, TT], F32)
    NCH = 3
    RS = []
    for ci in range(NCH):
        RS.append({nm: sb(f"r_{nm}{ci}", [128, wdt], F32) for nm, wdt in (
            ("sfc", 32), ("g8", 32), ("gs8", 8), ("m8a", 8), ("m8b", 8),
            ("ga", 4), ("gb", 4), ("gc", 4), ("gd", 4),
            ("t1", 1), ("t2", 1), ("t3", 1),
            ("sm", 32), ("tmp", 32), ("a", 32), ("b", 32), ("c", 32),
            ("d", 32), ("W", 32))})
    ones8 = sb("ones8", [128, 8], F32)
    ATS = []
    for ci in range(2):
        ATS.append({nm: sb(f"at_{nm}{ci}", [128, wdt], F32) for nm, wdt in (
            ("part", 4), ("nm", 1), ("nms", 1), ("sum", 1), ("rinv", 1))})
    spool = sb("spool", [128, 1536], F32)
    sel_all = spool[:, 0:512]
    dstrow = spool[:, 512:1024]
    cvio = spool[:, 1024:1536]
    abc_s = spool[:, 1024:1536]
    triu128 = sb("triu128", [128, 128], BF16)
    triu16s = sb("triu16s", [16, 16], BF16)
    ones128c = sb("ones128c", [128, 1], BF16)
    thr16 = sb("thr16", [16, 1], F32)
    ones16c = sb("ones16c", [16, 1], BF16)
    selbf = sb("selbf", [128, 512], BF16)
    tcnt_sb = sb("tcnt_sb", [32, 16], BF16)
    tcnt_T = sb("tcnt_T", [16, 32], BF16)
    csum_sb = sb("csum_sb", [16, 32], F32)
    erow = sb("erow", [1, 160], F32)
    addrow_s = sb("addrow_s", [16, 32], F32)
    addflat = sb("addflat", [1, 512], F32)
    jcol = sb("jcol", [128, 1], F32)
    pcol = sb("pcol", [128, 1], F32)
    schedc = sb("schedc", [128, 1], F32)
    schedbc = sb("schedbc", [128, 1], F32)
    dst4 = sb("dst4", [128, 8], F32)
    cv4 = sb("cv4", [128, 8], F32)
    dst4a = sb("dst4a", [128, 4 * (T // 128)], F32)
    cv4a = sb("cv4a", [128, 4 * (T // 128)], F32)
    dst4i = sb("dst4i", [128, 4 * (T // 128)], I32)
    cv4i = sb("cv4i", [128, 4 * (T // 128)], I32)
    e4f = dst4a
    w4a = sb("w4a", [128, 4 * (T // 128)], F32)
    iota_e = sb("iota_e", [128, 32], F32)
    ohscr = sb("ohscr", [128, 32], F32)
    cvload = sb("cvload", [8, 128], F32)
    rowi = sb("rowi", [128, NSCHED], I32)
    idxwi = sb("idxwi", [128, NSCHED], I32)
    sent_t = sb("sent_t", [128, 32], F32)

    pGU = [psum(f"pgu{i}", [128, 512], F32) for i in range(4)]
    pD0 = psum("pd0", [128, 512], F32)
    pD1 = psum("pd1", [128, 512], F32)
    pT = psum("pt", [128, 1024], BF16)
    pX = psum("px", [128, 512], F32)

    k = Em(nc)

    _bufs = {}

    def buf(name):
        if name not in _bufs:
            _bufs[name] = Buf()
        return _bufs[name]

    xtile = [buf(f"x{t}") for t in range(TT)]
    bXfm = buf("xfm")
    bXln = [buf("xln0"), buf("xln1")]
    bWnat = [buf("wnat0"), buf("wnat1")]
    bWT = [buf("wT0"), buf("wT1")]
    bSG = buf("sG")
    bHfm = buf("hfm")
    bPGU = [buf(f"pgu{i}") for i in range(4)]
    bPX = buf("px")
    bPDrot = [buf("pd0"), buf("pd1"), bPX]
    bPT = buf("pT")
    bPTh = [buf("pTh0"), buf("pTh1")]
    bStats = buf("stats")
    bBn = [buf("bn0"), buf("bn1")]
    bScores = buf("scores")
    bWrt = buf("wrt")
    bRS = [buf(f"rsc{ci}") for ci in range(NCH)]
    bQ = bWT[1]; bK = bWT[1]; bV = bWT[1]; bCtx = bWT[1]
    bArow = [buf("arowA"), bSG]
    bAT = bHfm; bCtxS = buf("ctxscr")
    bAtt = [buf("at0"), buf("at1")]
    bConsts = buf("consts")
    bCache = [buf(f"cache{e}") for e in range(NE)]
    bSpool = buf("spool")
    bSelbf = buf("selbf")
    bSched = buf("sched")
    bD4 = buf("d4")
    bCvm = buf("cvm")
    bWv = buf("wvals")
    bXlnD = buf("xln_dram")
    bWrtD = buf("wrt_dram")
    bCvD = buf("cv_dram")
    bPoutD = buf("pout_dram")
    bGb = [buf("gb0"), buf("gb1")]
    bGf = [buf("gf0"), buf("gf1")]
    bMx = [buf("mx0"), buf("mx1")]
    bWTd = [buf("wTd0"), buf("wTd1")]
    bWv2 = [buf("wv0"), buf("wv1")]
    bCacheAll = buf("cacheall")

    xfm3 = xfm[:, :].rearrange("p (c t) -> p c t", c=KC)
    mv2 = mv[:, :].rearrange("p (t two) -> p t two", two=2)
    g8v = [R["g8"][:, :].rearrange("p (g j) -> p g j", g=4) for R in RS]
    wTgu_v = [w[:, 0:8192].rearrange("p (m k q c) -> p m k q c", m=2, k=KC, q=IC)
              for w in wT_sb]
    wTd_v = [w[:, 8192:12288].rearrange("p (j q c) -> p j q c", j=IC, q=KC)
             for w in wT_sb]
    pTj_v = pT[:, :].rearrange("p (j c) -> p j c", j=8)
    pTd_v = pT[:, :].rearrange("p (j w c) -> p j w c", j=IC, w=2)

    def emit_prog():
        m = k.dma("sync", lambda s: s.dma_start(
            out=x_sb[:, :], in_=x_in.rearrange("(t p) f -> p t f", p=128)),
            q="sd")
        for t in range(TT):
            xtile[t].note_w(m)

        for w_in, w_sb_ in ((wq_in, wqb), (wk_in, wkb), (wv_in, wvb)):
            k.dma("gpsimd", lambda g, w_in=w_in, w_sb_=w_sb_: g.dma_start(
                out=w_sb_[:, :], in_=w_in.rearrange("(c p) d -> p c d", p=128)),
                q="gd", writes=[bConsts])
        k.dma("gpsimd", lambda g: g.dma_start(out=wob[:, :], in_=wo_in),
              q="gd", writes=[bConsts])
        k.dma("gpsimd", lambda g: g.dma_start(out=rw_nat[:, :], in_=rw_in),
              q="gd", writes=[bConsts])
        k.dma("sync", lambda s: s.dma_start(
            out=bias_row[:, :], in_=rb_in.rearrange("(o e) -> o e", o=1)),
            q="sd", writes=[bConsts])

        k.do("gpsimd", lambda g: g.memset(ident[:, :], 0.0), writes=[bConsts])
        k.do("gpsimd", lambda g: g.affine_select(
            out=ident[:, :], in_=ident[:, :], compare_op=ALU.not_equal, fill=1.0,
            base=0, pattern=[[-1, 128]], channel_multiplier=1),
            reads=[bConsts], writes=[bConsts])
        k.do("gpsimd", lambda g: g.memset(maskc[:, :], 0.0), writes=[bConsts])
        k.do("gpsimd", lambda g: g.affine_select(
            out=maskc[:, :], in_=maskc[:, :], compare_op=ALU.is_ge, fill=NEG_BIG,
            base=0, pattern=[[-1, 128]], channel_multiplier=1),
            reads=[bConsts], writes=[bConsts])
        k.do("gpsimd", lambda g: g.memset(ones1[:, :], 1.0), writes=[bConsts])
        if SPARSE:
            k.do("gpsimd", lambda g: g.memset(identf[:, :], 0.0), writes=[bConsts])
            k.do("gpsimd", lambda g: g.affine_select(
                out=identf[:, :], in_=identf[:, :], compare_op=ALU.not_equal, fill=1.0,
                base=0, pattern=[[-1, 128]], channel_multiplier=1),
                reads=[bConsts], writes=[bConsts])
            k.do("gpsimd", lambda g: g.memset(triu128[:, :], 0.0), writes=[bConsts])
            k.do("gpsimd", lambda g: g.affine_select(
                out=triu128[:, :], in_=triu128[:, :], compare_op=ALU.is_gt, fill=1.0,
                base=0, pattern=[[-1, 128]], channel_multiplier=1),
                reads=[bConsts], writes=[bConsts])
            k.do("gpsimd", lambda g: g.memset(triu16s[:, :], 0.0), writes=[bConsts])
            k.do("gpsimd", lambda g: g.affine_select(
                out=triu16s[:, :], in_=triu16s[:, :], compare_op=ALU.is_ge, fill=1.0,
                base=0, pattern=[[-1, 16]], channel_multiplier=1),
                reads=[bConsts], writes=[bConsts])
            k.do("gpsimd", lambda g: g.memset(ones128c[:, :], 1.0), writes=[bConsts])
            k.do("gpsimd", lambda g: g.memset(ones16c[:, :], 1.0), writes=[bConsts])
            k.do("gpsimd", lambda g: g.iota(jcol[:, :], [[1, 1]], base=0,
                                            channel_multiplier=1,
                                            allow_small_or_imprecise_dtypes=True),
                 writes=[bConsts])
            k.do("gpsimd", lambda g: g.iota(pcol[:, :], [[1, 1]], base=0,
                                            channel_multiplier=1,
                                            allow_small_or_imprecise_dtypes=True),
                 writes=[bConsts])
            k.do("gpsimd", lambda g: g.memset(sent_t[:, :], 100000.0), writes=[bConsts])
            k.do("gpsimd", lambda g: g.iota(iota_e[:, :], [[1, 32]], base=0,
                                            channel_multiplier=0,
                                            allow_small_or_imprecise_dtypes=True),
                 writes=[bConsts])
            k.do("gpsimd", lambda g: g.iota(thr16[:, :], [[1, 1]], base=0,
                                            channel_multiplier=128,
                                            allow_small_or_imprecise_dtypes=True),
                 writes=[bConsts])
            k.do("gpsimd", lambda g: g.memset(wnat[:, :], 0.0),
                 writes=[bWnat[0], bWnat[1]])
        k.do("vector", lambda v: v.memset(ones8[:, :], 1.0), writes=[bConsts])
        for ci in range(NCH):
            k.do("vector", lambda v, ci=ci: v.memset(RS[ci]["gs8"][:, 4:8], -3.0e38),
                 writes=[bRS[ci]])

        builds = [(lambda te, kk=kk: te.transpose(
            pT[:, 32 * kk:32 * (kk + 1)], rw_nat[0:32, 128 * kk:128 * (kk + 1)],
            ident[0:32, 0:32])) for kk in range(KC)]
        k.group("tensor", builds, reads=[bConsts], writes=[bPT])
        k.do("vector", lambda v: v.tensor_copy(out=rwT[:, :], in_=pT[:, 0:KC * 32]),
             reads=[bPT], writes=[bConsts])

        k.do("tensor", lambda t: t.matmul(pX[:, 0:32], ones1[0:1, :],
                                          bias_row[0:1, :], start=True, stop=True),
             reads=[bConsts], writes=[bPX])
        k.do("vector", lambda v: v.tensor_copy(out=bias_bc[:, :], in_=pX[:, 0:32]),
             reads=[bPX], writes=[bConsts])

        if "ffn1" in phases:
            emit_ffn(0)
        if "attn" in phases:
            emit_attention()
        if "ffn2" in phases:
            emit_ffn(1)

        for t in range(TT):
            k.waits("sync", xtile[t].rdeps())
        mfin = k.dma("sync", lambda s: s.dma_start(
            out=y_out.rearrange("(t p) f -> p t f", p=128), in_=x_sb[:, :]),
            q="st")
        for q in DMAQS:
            for i in range(QLANES[q]):
                lane = f"{q}{i}"
                if k.cnt[lane] > 0:
                    k.wait("sync", (lane, k.cnt[lane]))
        k.wait("sync", mfin)

    def emit_ln(dram=False):
        for t in range(TT):
            sl = t % 2
            xt = x_sb[:, 1024 * t:1024 * (t + 1)]
            o = 12 * sl
            k.group("vector", [
                (lambda v, xt=xt, o=o: v.bn_stats(out=bnscr[:, o:o + 6], in_=xt[:, 0:512])),
                (lambda v, xt=xt, o=o: v.bn_stats(out=bnscr[:, o + 6:o + 12], in_=xt[:, 512:1024])),
            ], reads=[xtile[t]], writes=[bBn[sl]])
            k.do("vector", lambda v, t=t, o=o: v.bn_aggr(out=mv2[:, t, :], in_=bnscr[:, o:o + 12]),
                 reads=[bBn[sl]], writes=[bStats])
        k.do("vector", lambda v: v.tensor_scalar(
            out=lnt1[:, :], in0=mv2[:, :, 1:2], scalar1=1e-5, scalar2=None, op0=ALU.add),
            reads=[bStats], writes=[bStats])
        k.do("scalar", lambda s: s.activation(out=rstd[:, :], in_=lnt1[:, :], func=AF.Sqrt),
             reads=[bStats], writes=[bStats])
        k.do("vector", lambda v: v.reciprocal(out=rstd[:, :], in_=rstd[:, :]),
             reads=[bStats], writes=[bStats])
        k.do("vector", lambda v: v.scalar_tensor_tensor(
            out=nmrs[:, :], in0=mv2[:, :, 0:1], scalar=-1.0, in1=rstd[:, :],
            op0=ALU.mult, op1=ALU.mult), reads=[bStats], writes=[bStats])

        def norm(t):
            p = t % 2
            xt = x_sb[:, 1024 * t:1024 * (t + 1)]
            xl = xln[:, 1024 * p:1024 * (p + 1)]
            k.do("vector", lambda v, xt=xt, xl=xl, t=t: v.tensor_scalar(
                out=xl, in0=xt, scalar1=rstd[:, t:t + 1], scalar2=nmrs[:, t:t + 1],
                op0=ALU.mult, op1=ALU.add),
                reads=[xtile[t], bStats], writes=[bXln[p]])

        norm(0)
        if TT > 1:
            norm(1)
        for t in range(TT):
            p = t % 2
            xl = xln[:, 1024 * p:1024 * (p + 1)]
            builds = [(lambda te, xl=xl, kk=kk: te.transpose(
                pT[:, 128 * kk:128 * (kk + 1)], xl[:, 128 * kk:128 * (kk + 1)],
                ident[:, :])) for kk in range(KC)]
            k.group("tensor", builds, reads=[bXln[p], bConsts], writes=[bPT])
            dst = xfm3[:, :, 128 * t:128 * (t + 1)]
            k.do("vector", lambda v, dst=dst: v.tensor_copy(out=dst, in_=pT[:, 0:KC * 128]),
                 reads=[bPT], writes=[bXfm])
            if dram:
                k.dma("sync", lambda s, xl=xl, t=t: s.dma_start(
                    out=xln_d[128 * t:128 * (t + 1), :], in_=xl),
                    q="sd", reads=[bXln[p]], writes=[bXlnD])
            if t + 2 < TT:
                norm(t + 2)

    def emit_routing():
        for t in range(TT):
            builds = [(lambda te, kk=kk, t=t: te.matmul(
                pX[:, 32 * t:32 * (t + 1)],
                xfm[:, T * kk + 128 * t:T * kk + 128 * (t + 1)],
                rwT[:, 32 * kk:32 * (kk + 1)],
                start=(kk == 0), stop=(kk == KC - 1))) for kk in range(KC)]
            k.group("tensor", builds, reads=[bXfm, bConsts], writes=[bPX])
        for t in range(TT):
            k.do("scalar", lambda s, t=t: s.activation(
                out=scores_all[:, 32 * t:32 * (t + 1)], in_=pX[:, 32 * t:32 * (t + 1)],
                func=AF.Sigmoid), reads=[bPX], writes=[bScores])

        D = "vector"

        def steps(t, ci):
            R = RS[ci]
            bR = bRS[ci]
            sc = scores_all[:, 32 * t:32 * (t + 1)]
            wout = wrt[:, 32 * t:32 * (t + 1)]
            th = []
            A = th.append
            A(lambda: k.do(D, lambda v: v.tensor_add(out=R["sfc"][:, :], in0=sc, in1=bias_bc[:, :]),
                           reads=[bScores, bConsts], writes=[bR]))
            A(lambda: k.group(D, [
                (lambda v, g=g: v.max(out=R["g8"][:, 8 * g:8 * (g + 1)],
                                      in_=R["sfc"][:, 8 * g:8 * (g + 1)])) for g in range(4)],
                reads=[bR], writes=[bR]))
            A(lambda: k.do(D, lambda v: v.tensor_add(out=R["gs8"][:, 0:4], in0=g8v[ci][:, :, 0:1],
                                                     in1=g8v[ci][:, :, 1:2]),
                           reads=[bR], writes=[bR]))
            A(lambda: k.do(D, lambda v: v.max(out=R["m8a"][:, :], in_=R["gs8"][:, :]),
                           reads=[bR], writes=[bR]))
            A(lambda: k.group(D, [
                (lambda v: v.tensor_scalar(out=R["ga"][:, :], in0=R["gs8"][:, 0:4],
                                           scalar1=R["m8a"][:, 1:2], scalar2=None, op0=ALU.is_gt)),
                (lambda v: v.tensor_scalar(out=R["gb"][:, :], in0=R["gs8"][:, 0:4],
                                           scalar1=R["m8a"][:, 1:2], scalar2=None, op0=ALU.is_equal)),
            ], reads=[bR], writes=[bR]))
            A(lambda: k.do(D, lambda v: v.reduce_sum(out=R["t1"][:, :], in_=R["ga"][:, :], axis=AX.X),
                           reads=[bR], writes=[bR]))
            A(lambda: k.group(D, [
                (lambda v: v.tensor_copy(out=R["gc"][:, 0:1], in_=R["gb"][:, 0:1])),
                (lambda v: v.tensor_add(out=R["gc"][:, 1:4], in0=R["gb"][:, 1:4], in1=R["gb"][:, 0:3])),
            ], reads=[bR], writes=[bR]))
            A(lambda: k.group(D, [
                (lambda v: v.tensor_copy(out=R["gd"][:, 0:2], in_=R["gc"][:, 0:2])),
                (lambda v: v.tensor_add(out=R["gd"][:, 2:4], in0=R["gc"][:, 2:4], in1=R["gc"][:, 0:2])),
            ], reads=[bR], writes=[bR]))
            A(lambda: k.do(D, lambda v: v.tensor_scalar(out=R["gc"][:, :], in0=R["gd"][:, :],
                                                        scalar1=R["t1"][:, 0:1], scalar2=2.0,
                                                        op0=ALU.add, op1=ALU.is_le),
                           reads=[bR], writes=[bR]))
            A(lambda: k.do(D, lambda v: v.tensor_mul(out=R["gc"][:, :], in0=R["gc"][:, :], in1=R["gb"][:, :]),
                           reads=[bR], writes=[bR]))
            A(lambda: k.do(D, lambda v: v.tensor_add(out=R["gc"][:, :], in0=R["gc"][:, :], in1=R["ga"][:, :]),
                           reads=[bR], writes=[bR]))
            A(lambda: k.group(D, [
                (lambda v, g=g: v.tensor_scalar(
                    out=R["sm"][:, 8 * g:8 * (g + 1)], in0=ones8[:, :],
                    scalar1=R["gc"][:, g:g + 1], scalar2=None, op0=ALU.mult)) for g in range(4)],
                reads=[bR, bConsts], writes=[bR]))
            A(lambda: k.do(D, lambda v: v.tensor_mul(out=R["tmp"][:, :], in0=R["sfc"][:, :], in1=R["sm"][:, :]),
                           reads=[bR], writes=[bR]))
            A(lambda: k.do(D, lambda v: v.max(out=R["m8b"][:, :], in_=R["tmp"][:, :]),
                           reads=[bR], writes=[bR]))
            A(lambda: k.group(D, [
                (lambda v: v.tensor_scalar(out=R["a"][:, :], in0=R["tmp"][:, :],
                                           scalar1=R["m8b"][:, 3:4], scalar2=None, op0=ALU.is_gt)),
                (lambda v: v.tensor_scalar(out=R["b"][:, :], in0=R["tmp"][:, :],
                                           scalar1=R["m8b"][:, 3:4], scalar2=None, op0=ALU.is_equal)),
            ], reads=[bR], writes=[bR]))
            A(lambda: k.do(D, lambda v: v.reduce_sum(out=R["t1"][:, :], in_=R["b" if False else "a"][:, :], axis=AX.X),
                           reads=[bR], writes=[bR]))
            for (srcb, dstb), s in zip((("b", "c"), ("c", "d"), ("d", "c"), ("c", "d"), ("d", "c")),
                                       (1, 2, 4, 8, 16)):
                A(lambda srcb=srcb, dstb=dstb, s=s: k.group(D, [
                    (lambda v, s=s, srcb=srcb, dstb=dstb: v.tensor_copy(
                        out=R[dstb][:, 0:s], in_=R[srcb][:, 0:s])),
                    (lambda v, s=s, srcb=srcb, dstb=dstb: v.tensor_add(
                        out=R[dstb][:, s:32], in0=R[srcb][:, s:32], in1=R[srcb][:, 0:32 - s])),
                ], reads=[bR], writes=[bR]))
            A(lambda: k.do(D, lambda v: v.tensor_scalar(out=R["d"][:, :], in0=R["c"][:, :],
                                                        scalar1=R["t1"][:, 0:1], scalar2=4.0,
                                                        op0=ALU.add, op1=ALU.is_le),
                           reads=[bR], writes=[bR]))
            A(lambda: k.do(D, lambda v: v.tensor_mul(out=R["d"][:, :], in0=R["d"][:, :], in1=R["b"][:, :]),
                           reads=[bR], writes=[bR]))
            A(lambda: k.do(D, lambda v: v.tensor_add(out=R["d"][:, :], in0=R["d"][:, :], in1=R["a"][:, :]),
                           reads=[bR], writes=[bR]))
            A(lambda: k.do(D, lambda v: v.tensor_mul(out=R["W"][:, :], in0=sc, in1=R["d"][:, :]),
                           reads=[bR, bScores], writes=[bR]))
            A(lambda: k.do(D, lambda v: v.reduce_sum(out=R["t1"][:, :], in_=R["W"][:, :], axis=AX.X),
                           reads=[bR], writes=[bR]))
            A(lambda: k.do(D, lambda v: v.tensor_scalar(out=R["t3"][:, :], in0=R["t1"][:, :],
                                                        scalar1=1e-20, scalar2=None, op0=ALU.add),
                           reads=[bR], writes=[bR]))
            A(lambda: k.do(D, lambda v: v.reciprocal(out=R["t3"][:, :], in_=R["t3"][:, :]),
                           reads=[bR], writes=[bR]))
            A(lambda: k.do(D, lambda v: v.tensor_scalar(
                out=wout, in0=R["W"][:, :], scalar1=R["t3"][:, 0:1], scalar2=1.25,
                op0=ALU.mult, op1=ALU.mult), reads=[bR], writes=[bWrt]))
            if SPARSE:
                A(lambda: k.do(D, lambda v: v.tensor_copy(
                    out=sel_all[:, 32 * t:32 * (t + 1)], in_=R["d"][:, :]),
                    reads=[bR], writes=[bSpool, bMx[0], bMx[1]]))
            return th

        from itertools import zip_longest
        t = 0
        while t < TT:
            chains = []
            for ci in range(NCH):
                if t + ci < TT:
                    chains.append(steps(t + ci, ci))
            for tup in zip_longest(*chains):
                for f in tup:
                    if f is not None:
                        f()
            t += NCH

    def wsrc_aps(e):
        if COLL and e < NSLOT:
            g = gw5_in[e].rearrange("(c p) h -> p c h", p=128)
            u = uw5_in[e].rearrange("(c p) h -> p c h", p=128)
            d = dw5_in[e].rearrange("(c p) i -> p c i", p=128)
            return g, u, d
        if e < NR:
            g = gw_in[e].rearrange("(c p) h -> p c h", p=128)
            u = uw_in[e].rearrange("(c p) h -> p c h", p=128)
            d = dw_in[e].rearrange("(c p) i -> p c i", p=128)
        else:
            a = e - NR
            g = sgw_in[512 * a:512 * (a + 1), :].rearrange("(c p) h -> p c h", p=128)
            u = suw_in[512 * a:512 * (a + 1), :].rearrange("(c p) h -> p c h", p=128)
            d = sdw_in[:, 512 * a:512 * (a + 1)].rearrange("(c p) i -> p c i", p=128)
        return g, u, d

    def emit_weight_load_transpose(e):
        q = e % 2
        wT = wT_sb[q]
        srcs = wsrc_aps(e)
        for mi in range(3):
            p = mi % 2
            nat = wnat[:, 4096 * p:4096 * (p + 1)]
            k.dma("gpsimd", lambda g, src=srcs[mi], nat=nat: g.dma_start(out=nat, in_=src),
                  q="gd", writes=[bWnat[p]])
            if mi < 2:
                for c in range(IC):
                    builds = [(lambda te, c=c, kk=kk, nat=nat: te.transpose(
                        pT[:, 128 * kk:128 * (kk + 1)],
                        nat[:, 1024 * c + 128 * kk:1024 * c + 128 * (kk + 1)],
                        ident[:, :])) for kk in range(KC)]
                    k.group("tensor", builds, reads=[bWnat[p], bConsts], writes=[bPT])
                    dst = wTgu_v[q][:, mi, :, c, :]
                    k.do("vector", lambda v, dst=dst: v.tensor_copy(
                        out=dst, in_=pT[:, 0:KC * 128]),
                        reads=[bPT], writes=[bWT[q]])
            else:
                for c2 in range(0, KC, 2):
                    builds = []
                    for j in range(IC):
                        for cc in range(2):
                            builds.append(lambda te, c=c2 + cc, j=j, cc=cc, nat=nat:
                                          te.transpose(
                                              pTd_v[:, j, cc, :],
                                              nat[:, 512 * c + 128 * j:512 * c + 128 * (j + 1)],
                                              ident[:, :]))
                    k.group("tensor", builds, reads=[bWnat[p], bConsts], writes=[bPT])
                    dst = wTd_v[q][:, :, c2:c2 + 2, :]
                    k.do("vector", lambda v, dst=dst: v.tensor_copy(
                        out=dst, in_=pT[:, 0:1024]),
                        reads=[bPT], writes=[bWTd[q]])
        tgt = wstage if (COLL and e < NSLOT) else wcache
        k.dma("sync", lambda s, e=e, wT=wT, tgt=tgt: s.dma_start(out=tgt[e], in_=wT[:, :]),
              q="st", reads=[bWT[q], bWTd[q]],
              writes=[bCache[e % NSLOT if (COLL and e < NR) else e], bCacheAll])

    def emit_weight_cache_load(e):
        q = e % 2
        wT = wT_sb[q]
        k.dma("sync", lambda s, e=e, wT=wT: s.dma_start(out=wT[:, :], in_=wcache[e]),
              q="sd", reads=[bCache[e]], writes=[bWT[q], bWTd[q]])

    def emit_expert_compute(e):
        q = e % 2
        wT = wT_sb[q]
        pool = (pD0, pD1, pX)
        for nt in range(NNT):
            tok0 = NTW * nt
            xs = [xfm[:, T * kk + tok0:T * kk + tok0 + NTW] for kk in range(KC)]
            for c in range(IC):
                gb, ub = bPGU[2 * (c % 2)], bPGU[2 * (c % 2) + 1]
                gps, ups = pGU[2 * (c % 2)], pGU[2 * (c % 2) + 1]
                builds = [(lambda te, kk=kk, c=c, gps=gps, wT=wT, x=xs[kk]: te.matmul(
                    gps[:, 0:NTW], wT[:, 512 * kk + 128 * c:512 * kk + 128 * (c + 1)],
                    x, start=(kk == 0), stop=(kk == KC - 1))) for kk in range(KC)]
                k.group("tensor", builds, reads=[bWT[q], bXfm], writes=[gb])
                builds = [(lambda te, kk=kk, c=c, ups=ups, wT=wT, x=xs[kk]: te.matmul(
                    ups[:, 0:NTW],
                    wT[:, 4096 + 512 * kk + 128 * c:4096 + 512 * kk + 128 * (c + 1)],
                    x, start=(kk == 0), stop=(kk == KC - 1))) for kk in range(KC)]
                k.group("tensor", builds, reads=[bWT[q], bXfm], writes=[ub])
                k.do("scalar", lambda s, c=c, gps=gps: s.activation(
                    out=sG[:, NTW * c:NTW * (c + 1)], in_=gps[:, 0:NTW], func=AF.Silu),
                    reads=[gb], writes=[bSG])
                k.do("vector", lambda v, c=c, ups=ups: v.tensor_mul(
                    out=hfm[:, NTW * c:NTW * (c + 1)], in0=ups[:, 0:NTW],
                    in1=sG[:, NTW * c:NTW * (c + 1)]),
                    reads=[ub, bSG], writes=[bHfm])
            gi = 0
            for s_ in range(NSUB):
                t_tile = (tok0 // 128) + s_
                for h2 in range(2):
                    pb = pool[gi % 3]
                    bb = bPDrot[gi % 3]
                    gi += 1
                    builds = [(lambda te, c=c, s_=s_, h2=h2, pb=pb, wT=wT: te.matmul(
                        pb[:, 0:512],
                        hfm[:, NTW * c + 128 * s_:NTW * c + 128 * (s_ + 1)],
                        wT[:, 8192 + 1024 * c + 512 * h2:8192 + 1024 * c + 512 * (h2 + 1)],
                        start=(c == 0), stop=(c == IC - 1))) for c in range(IC)]
                    k.group("tensor", builds, reads=[bHfm, bWT[q]], writes=[bb])
                    xsub = x_sb[:, 1024 * t_tile + 512 * h2:1024 * t_tile + 512 * (h2 + 1)]
                    if e < NR:
                        wcol = wrt[:, 32 * t_tile + e:32 * t_tile + e + 1]
                        k.do("vector", lambda v, pb=pb, xsub=xsub, wcol=wcol:
                             v.scalar_tensor_tensor(
                                 out=xsub, in0=pb[:, 0:512], scalar=wcol, in1=xsub,
                                 op0=ALU.mult, op1=ALU.add),
                             reads=[bb, bWrt, xtile[t_tile]], writes=[xtile[t_tile]])
                    else:
                        k.do("vector", lambda v, pb=pb, xsub=xsub: v.scalar_tensor_tensor(
                            out=xsub, in0=pb[:, 0:512], scalar=0.5, in1=xsub,
                            op0=ALU.mult, op1=ALU.add),
                            reads=[bb, xtile[t_tile]], writes=[xtile[t_tile]])

    BREG = {}

    def _mk_bregs(g):
        if "cv" not in BREG:
            BREG["cv"] = g.to_reg(NSCHED * 128 - 1)
            BREG["tok"] = g.to_reg(T - 1)
            BREG["wc"] = g.to_reg(NE_TBL * 128 - 1)
            BREG["wr"] = g.to_reg(T * 32 - 1)
        return BREG

    def emit_sched(inv):
        SPW = [bSpool, bMx[0], bMx[1]]
        k.do("vector", lambda v: v.tensor_copy(out=selbf[:, 0:32 * TT], in_=sel_all[:, 0:32 * TT]),
             reads=[bSpool], writes=[bSelbf])
        for t in range(TT):
            k.do("tensor", lambda te, t=t: te.matmul(
                pX[:, 32 * t:32 * (t + 1)], triu128[:, :], selbf[:, 32 * t:32 * (t + 1)],
                start=True, stop=True), reads=[bSelbf, bConsts], writes=[bPX])
            k.do("tensor", lambda te, t=t: te.matmul(
                pD0[:, t:t + 1] if False else pD0[0:32, t:t + 1],
                selbf[:, 32 * t:32 * (t + 1)], ones128c[:, :],
                start=True, stop=True), reads=[bSelbf, bConsts], writes=[bPDrot[0]])
        k.do("vector", lambda v: v.tensor_copy(out=tcnt_sb[:, 0:TT], in_=pD0[0:32, 0:TT]),
             reads=[bPDrot[0]], writes=[bSched])
        k.do("tensor", lambda te: te.transpose(pT[0:TT, 0:32], tcnt_sb[0:32, 0:TT],
                                               ident[0:32, 0:32]),
             reads=[bSched, bConsts], writes=[bPT])
        k.do("vector", lambda v: v.tensor_copy(out=tcnt_T[0:TT, 0:32], in_=pT[0:TT, 0:32]),
             reads=[bPT], writes=[bSched])
        k.do("tensor", lambda te: te.matmul(pD0[0:TT, 32:64], triu16s[0:TT, 0:TT],
                                            tcnt_T[0:TT, 0:32], start=True, stop=True),
             reads=[bSched, bConsts], writes=[bPDrot[0]])
        k.do("tensor", lambda te: te.matmul(pD1[0:1, 0:32], ones16c[0:TT, :],
                                            tcnt_T[0:TT, 0:32], start=True, stop=True),
             reads=[bSched, bConsts], writes=[bPDrot[1]])
        k.do("vector", lambda v: v.tensor_copy(out=csum_sb[0:TT, 0:32], in_=pD0[0:TT, 32:64]),
             reads=[bPDrot[0]], writes=[bSched])
        k.do("vector", lambda v: v.tensor_copy(out=erow[:, 0:32], in_=pD1[0:1, 0:32]),
             reads=[bPDrot[1]], writes=[bSched])
        k.do("tensor", lambda te: te.matmul(pD1[0:16, 64:96], ones1[0:1, 0:16],
                                            erow[0:1, 0:32], start=True, stop=True),
             reads=[bSched, bConsts], writes=[bPDrot[1]])
        k.do("vector", lambda v: v.tensor_scalar(out=tcnt_T[0:16, 0:32],
                                                 in0=pD1[0:16, 64:96],
                                                 scalar1=thr16[0:16, 0:1], scalar2=None,
                                                 op0=ALU.is_gt),
             reads=[bPDrot[1], bConsts], writes=[bSched])
        k.do("tensor", lambda te: te.matmul(pD1[0:1, 96:128], ones16c[:, :],
                                            tcnt_T[0:16, 0:32], start=True, stop=True),
             reads=[bSched, bConsts], writes=[bPDrot[1]])
        k.do("vector", lambda v: v.tensor_copy(out=erow[:, 32:64], in_=pD1[0:1, 96:128]),
             reads=[bPDrot[1]], writes=[bSched])
        k.do("vector", lambda v: v.tensor_copy(out=erow[:, 64:96], in_=erow[:, 32:64]),
             reads=[bSched], writes=[bSched])
        a_off, b_off = 64, 96
        for s in (1, 2, 4, 8, 16):
            k.do("vector", lambda v, s=s, a=a_off, b=b_off: v.tensor_copy(
                out=erow[:, b:b + s], in_=erow[:, a:a + s]), reads=[bSched], writes=[bSched])
            k.do("vector", lambda v, s=s, a=a_off, b=b_off: v.tensor_add(
                out=erow[:, b + s:b + 32], in0=erow[:, a + s:a + 32], in1=erow[:, a:a + 32 - s]),
                reads=[bSched], writes=[bSched])
            a_off, b_off = b_off, a_off
        tend_off = a_off
        k.do("vector", lambda v, t=tend_off: v.tensor_sub(out=erow[:, 128:160],
                                                          in0=erow[:, t:t + 32],
                                                          in1=erow[:, 32:64]),
             reads=[bSched], writes=[bSched])
        k.do("vector", lambda v: v.tensor_scalar(out=erow[:, 128:160], in0=erow[:, 128:160],
                                                 scalar1=128.0, scalar2=None, op0=ALU.mult),
             reads=[bSched], writes=[bSched])
        k.do("tensor", lambda te: te.matmul(pD1[0:TT, 32:64], ones1[0:1, 0:TT],
                                            erow[0:1, 128:160], start=True, stop=True),
             reads=[bSched, bConsts], writes=[bPDrot[1]])
        k.do("vector", lambda v: v.tensor_add(out=addrow_s[0:TT, :], in0=csum_sb[0:TT, 0:32],
                                              in1=pD1[0:TT, 32:64]),
             reads=[bSched, bPDrot[1]], writes=[bSched])
        k.do("vector", lambda v: v.tensor_scalar(out=addrow_s[0:TT, :], in0=addrow_s[0:TT, :],
                                                 scalar1=-1.0, scalar2=None, op0=ALU.add),
             reads=[bSched], writes=[bSched])
        k.dma("gpsimd", lambda g: g.dma_start(out=addflat[:, 0:32 * TT], in_=addrow_s[0:TT, :]),
              q="gd", reads=[bSched], writes=[bSched])
        k.do("tensor", lambda te: te.matmul(pD0[:, 0:32 * TT] if 32 * TT <= 512 else None,
                                            ones1[0:1, :], addflat[0:1, 0:32 * TT],
                                            start=True, stop=True),
             reads=[bSched, bConsts], writes=[bPDrot[0]])
        k.do("vector", lambda v: v.tensor_copy(out=abc_s[:, 0:32 * TT], in_=pD0[:, 0:32 * TT]),
             reads=[bPDrot[0]], writes=[bSched, bSpool])
        k.do("vector", lambda v: v.tensor_add(out=dstrow[:, 0:32 * TT], in0=pX[:, 0:32 * TT],
                                              in1=abc_s[:, 0:32 * TT]),
             reads=[bPX, bSched], writes=SPW)
        k.do("vector", lambda v: v.tensor_mul(out=dstrow[:, 0:32 * TT], in0=dstrow[:, 0:32 * TT],
                                              in1=sel_all[:, 0:32 * TT]), reads=SPW, writes=SPW)
        k.do("vector", lambda v: v.tensor_add(out=dstrow[:, 0:32 * TT], in0=dstrow[:, 0:32 * TT],
                                              in1=sel_all[:, 0:32 * TT]), reads=SPW, writes=SPW)
        k.do("vector", lambda v: v.tensor_scalar(out=dstrow[:, 0:32 * TT], in0=dstrow[:, 0:32 * TT],
                                                 scalar1=-1.0, scalar2=None, op0=ALU.add),
             reads=SPW, writes=SPW)
        k.do("gpsimd", lambda g: g.iota(cvio[:, 0:32 * TT], [[4096, TT], [1, 32]], base=0,
                                        channel_multiplier=32,
                                        allow_small_or_imprecise_dtypes=True),
             writes=SPW)
        k.do("vector", lambda v: v.tensor_mul(out=cvio[:, 0:32 * TT], in0=cvio[:, 0:32 * TT],
                                              in1=sel_all[:, 0:32 * TT]), reads=SPW, writes=SPW)
        k.do("vector", lambda v: v.tensor_add(out=cvio[:, 0:32 * TT], in0=cvio[:, 0:32 * TT],
                                              in1=sel_all[:, 0:32 * TT]), reads=SPW, writes=SPW)
        k.do("vector", lambda v: v.tensor_scalar(out=cvio[:, 0:32 * TT], in0=cvio[:, 0:32 * TT],
                                                 scalar1=-1.0, scalar2=None, op0=ALU.add),
             reads=SPW, writes=SPW)
        for t in range(TT):
            k.do("vector", lambda v, t=t: v.max(out=dst4[:, :], in_=dstrow[:, 32 * t:32 * (t + 1)]),
                 reads=SPW, writes=[bD4])
            k.do("vector", lambda v, t=t: v.max(out=cv4[:, :], in_=cvio[:, 32 * t:32 * (t + 1)]),
                 reads=SPW, writes=[bD4])
            k.do("vector", lambda v, t=t: v.tensor_copy(out=dst4a[:, 4 * t:4 * (t + 1)],
                                                        in_=dst4[:, 0:4]),
                 reads=[bD4], writes=[bD4])
            k.do("vector", lambda v, t=t: v.tensor_copy(out=cv4a[:, 4 * t:4 * (t + 1)],
                                                        in_=cv4[:, 0:4]),
                 reads=[bD4], writes=[bD4])
        k.do("vector", lambda v: v.tensor_copy(out=dst4i[:, 0:4 * TT], in_=dst4a[:, 0:4 * TT]),
             reads=[bD4], writes=[bD4])
        cvd_j = cv_d.rearrange("(j p) o -> j (p o)", p=128)
        for kq in range(4):
            k.dma("sync", lambda s, kq=kq: s.dma_start(
                out=cvd_j[:, 32 * kq:32 * (kq + 1)],
                in_=sent_t[0:NSCHED, 0:32]),
                q="sd", reads=[bConsts], writes=[bCvD])
        for c in range(4 * TT):
            if "cvs" not in DBG_SKIP:
                k.dma("gpsimd", lambda g, c=c: g.indirect_dma_start(
                    out=cv_d, out_offset=IndirectOffsetOnAxis(dst4i[:, c:c + 1], 0),
                    in_=cv4a[:, c:c + 1], in_offset=None,
                    bounds_check=_mk_bregs(g)["cv"], oob_is_err=False),
                    q="gd", reads=[bD4], writes=[bCvD])
        k.do("tensor", lambda te, t=tend_off: te.matmul(
            pD1[0:NSCHED, 0:32] if NSCHED <= 128 else None, ones1[0:1, 0:NSCHED],
            erow[0:1, t:t + 32], start=True, stop=True),
            reads=[bSched, bConsts], writes=[bPDrot[1]])
        k.do("vector", lambda v: v.tensor_scalar(out=abc_s[0:NSCHED, 0:32],
                                                 in0=pD1[0:NSCHED, 0:32],
                                                 scalar1=jcol[0:NSCHED, 0:1], scalar2=None,
                                                 op0=ALU.is_le),
             reads=[bPDrot[1], bConsts], writes=[bSched, bSpool])
        k.do("vector", lambda v: v.reduce_sum(out=schedc[0:NSCHED, :], in_=abc_s[0:NSCHED, 0:32],
                                              axis=AX.X), reads=[bSched, bSpool], writes=[bSched])
        k.do("vector", lambda v: v.tensor_scalar(out=schedbc[0:NSCHED, 0:1],
                                                 in0=schedc[0:NSCHED, :], scalar1=float(NR),
                                                 scalar2=None, op0=ALU.is_equal),
             reads=[bSched], writes=[bSched])
        k.do("vector", lambda v: v.scalar_tensor_tensor(out=schedc[0:NSCHED, :],
                                                        in0=schedbc[0:NSCHED, 0:1],
                                                        scalar=float(255 - NR),
                                                        in1=schedc[0:NSCHED, :],
                                                        op0=ALU.mult, op1=ALU.add),
             reads=[bSched], writes=[bSched])
        k.do("tensor", lambda te: te.transpose(pD1[0:1, 32:32 + NSCHED],
                                               schedc[0:NSCHED, 0:1],
                                               identf[0:NSCHED, 0:NSCHED]),
             reads=[bSched, bConsts], writes=[bPDrot[1]])
        k.do("vector", lambda v: v.tensor_copy(out=addflat[:, 0:NSCHED],
                                               in_=pD1[0:1, 32:32 + NSCHED]),
             reads=[bPDrot[1]], writes=[bSched])
        k.do("tensor", lambda te: te.matmul(pD0[:, 0:NSCHED], ones1[0:1, :],
                                            addflat[0:1, 0:NSCHED], start=True, stop=True),
             reads=[bSched, bConsts], writes=[bPDrot[0]])
        k.do("vector", lambda v: v.tensor_scalar(out=idxwi[:, :], in0=pD0[:, 0:NSCHED],
                                                 scalar1=128.0, scalar2=pcol[:, 0:1],
                                                 op0=ALU.mult, op1=ALU.add),
             reads=[bPDrot[0], bConsts], writes=[bCvm])
        nb8 = (NSCHED + 7) // 8
        for b in range(nb8):
            j0 = 8 * b
            jn = min(8, NSCHED - j0)
            k.dma("sync", lambda s, j0=j0, jn=jn: s.dma_start(
                out=cvload[0:jn, :],
                in_=cv_d[128 * j0:128 * (j0 + jn)].rearrange("(j p) o -> j (p o)", p=128)),
                q="sd", reads=[bCvD], writes=[bSched])
            k.do("tensor", lambda te, jn=jn: te.transpose(
                pD1[:, 64:64 + jn] if True else None, cvload[0:jn, :], identf[0:jn, 0:jn]),
                reads=[bSched, bConsts], writes=[bPDrot[1]])
            k.do("vector", lambda v, j0=j0, jn=jn: v.tensor_copy(
                out=rowi[:, j0:j0 + jn], in_=pD1[:, 64:64 + jn]),
                reads=[bPDrot[1]], writes=[bCvm])
        k.do("vector", lambda v: v.tensor_scalar(out=rowi[:, :], in0=rowi[:, :],
                                                 scalar1=5, scalar2=None,
                                                 op0=ALU.arith_shift_right),
             reads=[bCvm], writes=[bCvm])

    DBG_SKIP = set(cfg.get("dbg_skip", ()))

    def emit_w4_prep():
        k.do("vector", lambda v: v.tensor_copy(out=cv4i[:, 0:4 * TT], in_=cv4a[:, 0:4 * TT]),
             reads=[bD4], writes=[bD4])
        k.do("vector", lambda v: v.tensor_scalar(out=cv4i[:, 0:4 * TT], in0=cv4i[:, 0:4 * TT],
                                                 scalar1=31, scalar2=None,
                                                 op0=ALU.bitwise_and),
             reads=[bD4], writes=[bD4])
        k.do("vector", lambda v: v.tensor_copy(out=e4f[:, 0:4 * TT], in_=cv4i[:, 0:4 * TT]),
             reads=[bD4], writes=[bD4])

    def emit_w4_step(c):
        t = c // 4
        k.do("vector", lambda v, c=c: v.tensor_scalar(
            out=ohscr[:, :], in0=iota_e[:, :], scalar1=e4f[:, c:c + 1],
            scalar2=None, op0=ALU.is_equal),
            reads=[bD4, bConsts], writes=[bD4])
        k.do("vector", lambda v, t=t: v.tensor_mul(
            out=ohscr[:, :], in0=ohscr[:, :], in1=wrt[:, 32 * t:32 * (t + 1)]),
            reads=[bD4, bWrt], writes=[bD4])
        k.do("vector", lambda v, c=c: v.reduce_sum(
            out=w4a[:, c:c + 1], in_=ohscr[:, :], axis=AX.X),
            reads=[bD4], writes=[bD4])

    def sp_gather(j):
        q = j % 2
        gb = wnat[:, 2048 * q:2048 * q + 1024]
        if "rowg" not in DBG_SKIP:
            k.dma("gpsimd", lambda g, j=j, gb=gb: g.indirect_dma_start(
                out=gb, out_offset=None,
                in_=xln_d, in_offset=IndirectOffsetOnAxis(rowi[:, j:j + 1], 0),
                bounds_check=_mk_bregs(g)["tok"], oob_is_err=False),
                q="gd", reads=[bCvm, bXlnD], writes=[bGb[q]])
        if "wg" not in DBG_SKIP:
            k.dma("gpsimd", lambda g, j=j, q=q: g.indirect_dma_start(
                out=wT_sb[q][:, 0:8192], out_offset=None,
                in_=wcache.rearrange("e p f -> (e p) f"),
                in_offset=IndirectOffsetOnAxis(idxwi[:, j:j + 1], 0),
                bounds_check=_mk_bregs(g)["wc"], oob_is_err=False),
                q="gd", reads=[bCvm, bCacheAll], writes=[bWT[q]])
            k.dma("gpsimd", lambda g, j=j, q=q: g.indirect_dma_start(
                out=wT_sb[q][:, 8192:12288], out_offset=None,
                in_=wcache.rearrange("e p f -> (e p) f"),
                in_offset=IndirectOffsetOnAxis(idxwi[:, j:j + 1], 0),
                element_offset=8192,
                bounds_check=_mk_bregs(g)["wc"], oob_is_err=False),
                q="gd", reads=[bCvm, bCacheAll], writes=[bWTd[q]])


    def sp_tile_a(j):
        q = j % 2
        gb = wnat[:, 2048 * q:2048 * q + 1024]
        gf = wnat[:, 4096 + 2048 * q:4096 + 2048 * q + 1024]
        wT = wT_sb[q]
        builds = [(lambda te, kk=kk, gb=gb: te.transpose(
            pT[:, 128 * kk:128 * (kk + 1)], gb[:, 128 * kk:128 * (kk + 1)], ident[:, :]))
            for kk in range(KC)]
        k.group("tensor", builds, reads=[bGb[q], bConsts], writes=[bPT])
        k.do("vector", lambda v, gf=gf: v.tensor_copy(out=gf, in_=pT[:, 0:1024]),
             reads=[bPT], writes=[bGf[q]])
        for c in range(IC):
            builds = [(lambda te, kk=kk, c=c, gf=gf, wT=wT: te.matmul(
                pGU[0][:, 128 * c:128 * (c + 1)],
                wT[:, 512 * kk + 128 * c:512 * kk + 128 * (c + 1)],
                gf[:, 128 * kk:128 * (kk + 1)],
                start=(kk == 0), stop=(kk == KC - 1))) for kk in range(KC)]
            k.group("tensor", builds, reads=[bWT[q], bGf[q]], writes=[bPGU[0]])
        for c in range(IC):
            builds = [(lambda te, kk=kk, c=c, gf=gf, wT=wT: te.matmul(
                pGU[1][:, 128 * c:128 * (c + 1)],
                wT[:, 4096 + 512 * kk + 128 * c:4096 + 512 * kk + 128 * (c + 1)],
                gf[:, 128 * kk:128 * (kk + 1)],
                start=(kk == 0), stop=(kk == KC - 1))) for kk in range(KC)]
            k.group("tensor", builds, reads=[bWT[q], bGf[q]], writes=[bPGU[1]])
        k.do("scalar", lambda s, q=q: s.activation(
            out=sG[:, 512 * q:512 * (q + 1)], in_=pGU[0][:, 0:512], func=AF.Silu),
            reads=[bPGU[0]], writes=[bSG])
        k.do("vector", lambda v, q=q: v.tensor_mul(
            out=hfm[:, 512 * q:512 * (q + 1)], in0=pGU[1][:, 0:512],
            in1=sG[:, 512 * q:512 * (q + 1)]),
            reads=[bPGU[1], bSG], writes=[bHfm])

    def sp_tile_b(j, first=False):
        q = j % 2
        wT = wT_sb[q]
        gf = wnat[:, 4096 + 2048 * q:4096 + 2048 * q + 1024]
        for h2 in range(2):
            pb = (pD0, pD1)[h2]
            builds = [(lambda te, c=c, h2=h2, pb=pb, q=q, wT=wT: te.matmul(
                pb[:, 0:512],
                hfm[:, 512 * q + 128 * c:512 * q + 128 * (c + 1)],
                wT[:, 8192 + 1024 * c + 512 * h2:8192 + 1024 * c + 512 * (h2 + 1)],
                start=(c == 0), stop=(c == IC - 1))) for c in range(IC)]
            k.group("tensor", builds, reads=[bHfm, bWTd[q]], writes=[bPDrot[h2]])
            k.do("vector", lambda v, pb=pb, h2=h2, gf=gf: v.tensor_copy(
                out=gf[:, 512 * h2:512 * (h2 + 1)], in_=pb[:, 0:512]),
                reads=[bPDrot[h2]], writes=[bGf[q]])
        k.dma("sync", lambda s, j=j, gf=gf: s.dma_start(
            out=pout_d[128 * j:128 * (j + 1), :], in_=gf[:, :]),
            q="sd", reads=[bGf[q]], writes=[bPoutD])

    def emit_sparse_tiles():
        k.do("gpsimd", lambda g: g.engine_nop(),
             writes=[bWnat[0], bWnat[1], bGb[0], bGb[1], bGf[0], bGf[1]])
        sp_gather(0)
        emit_w4_prep()
        for j in range(NSCHED):
            if j + 1 < NSCHED:
                sp_gather(j + 1)
            sp_tile_a(j)
            sp_tile_b(j, first=(j < 2))
            if j < 4 * TT:
                emit_w4_step(j)

    def emit_mix_merge():
        gbm = [wnat[:, 0:1024], wnat[:, 2048:3072]]
        for t in range(TT):
            xt = x_sb[:, 1024 * t:1024 * (t + 1)]
            for half in range(2):
                for m in range(2):
                    sl = 2 * half + m
                    k.dma("gpsimd", lambda g, t=t, sl=sl, m=m, gbm=gbm: g.indirect_dma_start(
                        out=gbm[m][:, :], out_offset=None,
                        in_=pout_d,
                        in_offset=IndirectOffsetOnAxis(dst4i[:, 4 * t + sl:4 * t + sl + 1], 0),
                        bounds_check=_mk_bregs(g)["cv"], oob_is_err=False),
                        q="gd", reads=[bD4, bPoutD], writes=[bGb[m]])
                for m in range(2):
                    sl = 2 * half + m
                    k.do("vector", lambda v, xt=xt, m=m, c=4 * t + sl, gbm=gbm: v.scalar_tensor_tensor(
                        out=xt, in0=gbm[m][:, :], scalar=w4a[:, c:c + 1], in1=xt,
                        op0=ALU.mult, op1=ALU.add),
                        reads=[bGb[m], bD4, xtile[t]], writes=[xtile[t]])

    def emit_ffn(inv):
        if SPARSE:
            emit_ln(dram=True)
            emit_routing()
            if inv == 0:
                k.do("gpsimd", lambda g: g.engine_nop(),
                     writes=[bPT, bPTh[0], bPTh[1]])
                if COLL:
                    for s in range(NSLOT):
                        emit_weight_load_transpose(s)
                    emit_weight_load_transpose(NR)
                    emit_weight_load_transpose(NR + 1)
                    for s in range(NSLOT):
                        k.do("gpsimd", lambda g, s=s: g.collective_compute(
                            "AllGather", ALU.bypass,
                            replica_groups=[list(range(8))],
                            ins=[wstage[s]],
                            outs=[wcache[8 * s:8 * (s + 1)].rearrange("e p f -> (e p) f")]),
                            reads=[bCache[s], bCacheAll], writes=[bCacheAll], sem="cc")
                else:
                    for e in range(NE):
                        emit_weight_load_transpose(e)
                k.do("gpsimd", lambda g: g.engine_nop(),
                     writes=[bPT, bPTh[0], bPTh[1]])
            emit_sched(inv)
            if inv > 0 or COLL:
                emit_weight_cache_load(NR)
                emit_weight_cache_load(NR + 1)
            emit_expert_compute(NR)
            emit_expert_compute(NR + 1)
            emit_sparse_tiles()
            emit_mix_merge()
            return
        emit_ln()
        emit_routing()
        if inv == 0:
            emit_weight_load_transpose(0)
        else:
            emit_weight_cache_load(0)
        for e in range(NE):
            if e + 1 < NE:
                if inv == 0:
                    emit_weight_load_transpose(e + 1)
                else:
                    emit_weight_cache_load(e + 1)
            emit_expert_compute(e)

    def emit_attention():
        emit_ln()
        nq = TT
        for (wb_, dest, bank, bbank) in ((wqb, qfm, pGU[0], bPGU[0]),
                                         (wkb, kfm, pGU[1], bPGU[1])):
            for nt in range(NNT):
                builds = [(lambda te, kk=kk, nt=nt, wb_=wb_, bank=bank: te.matmul(
                    bank[:, 0:NTW], wb_[:, 128 * kk:128 * (kk + 1)],
                    xfm[:, T * kk + NTW * nt:T * kk + NTW * (nt + 1)],
                    start=(kk == 0), stop=(kk == KC - 1))) for kk in range(KC)]
                k.group("tensor", builds, reads=[bXfm, bConsts], writes=[bbank])
                k.do("vector", lambda v, nt=nt, dest=dest, bank=bank: v.tensor_copy(
                    out=dest[:, NTW * nt:NTW * (nt + 1)], in_=bank[:, 0:NTW]),
                    reads=[bbank], writes=[bWT[1]])
        t = 0
        while t < nq:
            gn = min(4, nq - t)
            for ts_ in range(gn):
                builds = [(lambda te, kk=kk, tt_=t + ts_, ts_=ts_: te.matmul(
                    pGU[2][:, 128 * ts_:128 * (ts_ + 1)],
                    xfm[:, T * kk + 128 * tt_:T * kk + 128 * (tt_ + 1)],
                    wvb[:, 128 * kk:128 * (kk + 1)],
                    start=(kk == 0), stop=(kk == KC - 1))) for kk in range(KC)]
                k.group("tensor", builds, reads=[bXfm, bConsts], writes=[bPGU[2]])
            k.do("vector", lambda v, t=t, gn=gn: v.tensor_copy(
                out=vtm[:, 128 * t:128 * (t + gn)], in_=pGU[2][:, 0:128 * gn]),
                reads=[bPGU[2]], writes=[bWT[1]])
            t += gn

        def ph1(qi):
            W = (qi + 1) * 128
            nb = (qi + 4) // 4
            pr = qi % 2
            AT = ATS[pr]
            bA = bAtt[pr]
            ar = arow2[pr]
            for j in range(qi + 1):
                bk = j // 4
                k.do("tensor", lambda te, qi=qi, j=j, bk=bk: te.matmul(
                    pGU[bk][:, 128 * (j % 4):128 * (j % 4 + 1)],
                    qfm[:, 128 * qi:128 * (qi + 1)], kfm[:, 128 * j:128 * (j + 1)],
                    start=True, stop=True),
                    reads=[bWT[1]], writes=[bPGU[bk]])
            dbk = qi // 4
            k.do("vector", lambda v, qi=qi, dbk=dbk: v.tensor_add(
                out=pGU[dbk][:, 128 * (qi % 4):128 * (qi % 4 + 1)],
                in0=pGU[dbk][:, 128 * (qi % 4):128 * (qi % 4 + 1)], in1=maskc[:, :]),
                reads=[bPGU[dbk], bConsts], writes=[bPGU[dbk]])
            k.group("vector", [
                (lambda v, b=b, wb2=min(512, W - 512 * b), AT=AT: v.reduce_max(
                    out=AT["part"][:, b:b + 1], in_=pGU[b][:, 0:wb2], axis=AX.X))
                for b in range(nb)],
                reads=[bPGU[b2] for b2 in range(nb)], writes=[bA])
            k.do("vector", lambda v, nb=nb, AT=AT: v.reduce_max(
                out=AT["nm"][:, :], in_=AT["part"][:, 0:nb], axis=AX.X, negate=True),
                reads=[bA], writes=[bA])
            k.do("vector", lambda v, AT=AT: v.tensor_scalar(
                out=AT["nms"][:, :], in0=AT["nm"][:, :], scalar1=SCALE_ATT, scalar2=None,
                op0=ALU.mult), reads=[bA], writes=[bA])
            for b in range(nb):
                wb2 = min(512, W - 512 * b)
                k.do("scalar", lambda s, b=b, wb2=wb2, AT=AT, ar=ar: s.activation(
                    out=ar[:, 512 * b:512 * b + wb2], in_=pGU[b][:, 0:wb2],
                    func=AF.Exp, bias=AT["nms"][:, 0:1], scale=SCALE_ATT),
                    reads=[bPGU[b], bA], writes=[bArow[pr]])
            k.do("vector", lambda v, W=W, AT=AT, ar=ar: v.reduce_sum(
                out=AT["sum"][:, :], in_=ar[:, 0:W], axis=AX.X),
                reads=[bArow[pr]], writes=[bA])
            k.do("vector", lambda v, AT=AT: v.reciprocal(out=AT["rinv"][:, :], in_=AT["sum"][:, :]),
                 reads=[bA], writes=[bA])

        def ph2(qi):
            pr = qi % 2
            AT = ATS[pr]
            bA = bAtt[pr]
            ar = arow2[pr]
            j0 = 0
            while j0 <= qi:
                jn = min(8, qi + 1 - j0)
                builds = [(lambda te, j=j0 + jj, jj=jj, ar=ar: te.transpose(
                    pT[:, 128 * jj:128 * (jj + 1)], ar[:, 128 * j:128 * (j + 1)],
                    ident[:, :])) for jj in range(jn)]
                k.group("tensor", builds, reads=[bArow[pr], bConsts], writes=[bPT])
                k.do("vector", lambda v, j0=j0, jn=jn: v.tensor_copy(
                    out=aT[:, 128 * j0:128 * (j0 + jn)], in_=pT[:, 0:128 * jn]),
                    reads=[bPT], writes=[bAT])
                j0 += jn
            builds = [(lambda te, j=j, qi=qi: te.matmul(
                pX[:, 0:128], aT[:, 128 * j:128 * (j + 1)],
                vtm[:, 128 * j:128 * (j + 1)],
                start=(j == 0), stop=(j == qi))) for j in range(qi + 1)]
            k.group("tensor", builds, reads=[bAT, bWT[1]], writes=[bPX])
            k.do("vector", lambda v, AT=AT: v.tensor_scalar(
                out=ctxscr[:, :], in0=pX[:, 0:128], scalar1=AT["rinv"][:, 0:1],
                scalar2=None, op0=ALU.mult), reads=[bPX, bA], writes=[bCtxS])
            k.do("tensor", lambda te: te.transpose(pT[:, 0:128], ctxscr[:, :], ident[:, :]),
                 reads=[bCtxS, bConsts], writes=[bPT])
            k.do("vector", lambda v, qi=qi: v.tensor_copy(
                out=ctxfm[:, 128 * qi:128 * (qi + 1)], in_=pT[:, 0:128]),
                reads=[bPT], writes=[bWT[1]])

        k.do("gpsimd", lambda g: g.engine_nop(),
             writes=[bXln[0], bXln[1], bArow[0]])
        ph1(0)
        for qi in range(1, nq):
            ph1(qi)
            ph2(qi - 1)
        ph2(nq - 1)
        k.do("gpsimd", lambda g: g.engine_nop(),
             writes=[bArow[0], bXln[0], bXln[1]])

        for t in range(nq):
            for h2 in range(2):
                pb = (pD0, pD1)[h2]
                bb = bPDrot[h2]
                k.do("tensor", lambda te, t=t, h2=h2, pb=pb: te.matmul(
                    pb[:, 0:512], ctxfm[:, 128 * t:128 * (t + 1)],
                    wob[:, 512 * h2:512 * (h2 + 1)], start=True, stop=True),
                    reads=[bWT[1], bConsts], writes=[bb])
                xsub = x_sb[:, 1024 * t + 512 * h2:1024 * t + 512 * (h2 + 1)]
                k.do("vector", lambda v, pb=pb, xsub=xsub: v.tensor_add(
                    out=xsub, in0=xsub, in1=pb[:, 0:512]),
                    reads=[bb, xtile[t]], writes=[xtile[t]])

    with ExitStack() as semctx:
        for n in SEMS:
            k.sem[n] = semctx.enter_context(nc.semaphore(n))
        emit_prog()
        with nc.Block() as blk:
            @blk.sync
            def _(eng):
                for f in k.ops["sync"]:
                    f(eng)

            @blk.tensor
            def _(eng):
                for f in k.ops["tensor"]:
                    f(eng)

            @blk.vector
            def _(eng):
                for f in k.ops["vector"]:
                    f(eng)

            @blk.scalar
            def _(eng):
                for f in k.ops["scalar"]:
                    f(eng)

            @blk.gpsimd
            def _(eng):
                for f in k.ops["gpsimd"]:
                    f(eng)

    ctx.close()
    return nc


_IN_NAMES = {
    "x": "L0_attn_x", "wq": "L0_attn_wq", "wk": "L0_attn_wk", "wv": "L0_attn_wv",
    "wo": "L0_attn_wo", "rw": "L0_ffn_router_weight", "rb": "L0_ffn_router_bias",
    "gw": "L0_ffn_gate_w", "uw": "L0_ffn_up_w", "dw": "L0_ffn_down_w",
    "sgw": "L0_ffn_shared_gate_w", "suw": "L0_ffn_shared_up_w",
    "sdw": "L0_ffn_shared_down_w",
}


def kernel(**inputs):
    nc = build_program({"collective": True})
    g = lambda n: np.asarray(inputs[n])
    common = {
        "wq": np.ascontiguousarray(g("L0_attn_wq")),
        "wk": np.ascontiguousarray(g("L0_attn_wk")),
        "wv": np.ascontiguousarray(g("L0_attn_wv")),
        "wo": np.ascontiguousarray(g("L0_attn_wo")),
        "rw": np.ascontiguousarray(g("L0_ffn_router_weight")),
        "rb": np.ascontiguousarray(g("L0_ffn_router_bias")),
        "sgw": np.ascontiguousarray(g("L0_ffn_shared_gate_w")),
        "suw": np.ascontiguousarray(g("L0_ffn_shared_up_w")),
        "sdw": np.ascontiguousarray(g("L0_ffn_shared_down_w")),
    }
    gw, uw, dw = g("L0_ffn_gate_w"), g("L0_ffn_up_w"), g("L0_ffn_down_w")
    sgw, suw, sdw = g("L0_ffn_shared_gate_w"), g("L0_ffn_shared_up_w"), g("L0_ffn_shared_down_w")
    x = np.ascontiguousarray(g("L0_attn_x"))
    zg = np.zeros((512, 1024), np.float32)
    zd = np.zeros((1024, 512), np.float32)

    def slot(e, kind):
        if e < 32:
            return {"g": gw[e], "u": uw[e], "d": dw[e]}[kind]
        if e in (32, 33):
            a = e - 32
            if kind == "g":
                return sgw[512 * a:512 * (a + 1), :]
            if kind == "u":
                return suw[512 * a:512 * (a + 1), :]
            return np.ascontiguousarray(sdw[:, 512 * a:512 * (a + 1)])
        return zg if kind in ("g", "u") else zd

    in_maps = []
    for c in range(8):
        es = [8 * s + c for s in range(4)]
        in_maps.append(dict(
            common, x=x[c],
            gw5=np.ascontiguousarray(np.stack([slot(e, "g") for e in es])),
            uw5=np.ascontiguousarray(np.stack([slot(e, "u") for e in es])),
            dw5=np.ascontiguousarray(np.stack([slot(e, "d") for e in es])),
        ))
    res = run_bass_kernel_spmd(nc, in_maps, core_ids=list(range(8)))
    return np.stack([res.results[c]["y"] for c in range(8)], axis=0)


# version 30 (final)
# speedup vs baseline: 1.0449x; 1.0029x over previous
import numpy as np
import concourse.bass as bass
import concourse.mybir as mybir
from concourse.bass_utils import run_bass_kernel_spmd
from concourse.bass import IndirectOffsetOnAxis

F32 = mybir.dt.float32
BF16 = mybir.dt.bfloat16
I32 = mybir.dt.int32
AF = mybir.ActivationFunctionType
ALU = mybir.AluOpType
AX = mybir.AxisListType

ENGS = ("sync", "tensor", "vector", "scalar", "gpsimd")
ESEM = {"tensor": "pe", "vector": "dve", "scalar": "act", "gpsimd": "pool", "sync": "sq"}
QLANES = {"sd": 8, "gd": 8, "st": 8}
DMAQS = ("sd", "gd", "st")
SEMS = ("pe", "dve", "act", "pool", "sq", "cc") + tuple(
    f"{q}{i}" for q in DMAQS for i in range(QLANES[q]))

SCALE_ATT = 1.0 / (128.0 ** 0.5)
NEG_BIG = -1.0e30


class Buf:
    __slots__ = ("w", "readers")

    def __init__(self):
        self.w = None
        self.readers = {}

    def rdeps(self):
        return [self.w] if self.w else []

    def wdeps(self):
        d = [(n, m) for n, m in self.readers.items()]
        if self.w:
            d.append(self.w)
        return d

    def note_w(self, mark):
        self.w = mark
        self.readers = {}

    def note_r(self, mark):
        n, m = mark
        if self.readers.get(n, 0) < m:
            self.readers[n] = m


class Em:

    def __init__(self, nc):
        self.nc = nc
        self.ops = {e: [] for e in ENGS}
        self.cnt = {n: 0 for n in SEMS}
        self.waited = {e: {n: 0 for n in SEMS} for e in ENGS}
        self.sem = {}
        self.qrot = {q: 0 for q in DMAQS}

    def wait(self, e, dep):
        n, m = dep
        if m <= 0 or self.waited[e][n] >= m:
            return
        self.waited[e][n] = m
        s = self.sem[n]
        self.ops[e].append(lambda eng, s=s, m=m: eng.wait_ge(s, m))

    def waits(self, e, deps):
        for d in deps:
            self.wait(e, d)

    def raw(self, e, build):
        self.ops[e].append(build)

    def op(self, e, build, sem=None, dn=1):
        n = sem or ESEM[e]
        self.cnt[n] += dn
        m = self.cnt[n]
        s = self.sem[n]
        self.ops[e].append(lambda eng, b=build, s=s, d=dn: b(eng).then_inc(s, d))
        return (n, m)

    def do(self, e, build, reads=(), writes=(), sem=None, dn=1):
        for b in reads:
            self.waits(e, b.rdeps())
        for b in writes:
            self.waits(e, b.wdeps())
        m = self.op(e, build, sem, dn)
        for b in reads:
            b.note_r(m)
        for b in writes:
            b.note_w(m)
        return m

    def dma(self, e, build, q, reads=(), writes=()):
        i = self.qrot[q]
        self.qrot[q] += 1
        lane = f"{q}{i % QLANES[q]}"
        prev = self.cnt[lane]
        if prev > 0:
            self.wait(e, (lane, prev))
        for b in reads:
            self.waits(e, b.rdeps())
        for b in writes:
            self.waits(e, b.wdeps())
        m = self.op(e, build, sem=lane, dn=16)
        for b in reads:
            b.note_r(m)
        for b in writes:
            b.note_w(m)
        return m

    def group(self, e, builds, reads=(), writes=(), sem=None, dn=1):
        for b in reads:
            self.waits(e, b.rdeps())
        for b in writes:
            self.waits(e, b.wdeps())
        for b in builds[:-1]:
            self.raw(e, b)
        m = self.op(e, builds[-1], sem, dn)
        for b in reads:
            b.note_r(m)
        for b in writes:
            b.note_w(m)
        return m


def build_program(cfg):
    T = int(cfg.get("T", 2048))
    NR = int(cfg.get("n_routed", 32))
    phases = cfg.get("phases", ("ffn1", "attn", "ffn2"))
    H, I, DH, SH_I = 1024, 512, 128, 1024
    NE = NR + 2
    TT = T // 128
    NTW = min(512, T)
    NNT = T // NTW
    KC = H // 128
    IC = I // 128
    NSUB = NTW // 128
    assert T % 128 == 0 and 32 * TT <= 512

    nc = bass.Bass("TRN2", target_bir_lowering=False)

    COLL = bool(cfg.get("collective", False))
    NSLOT = 4
    x_in = nc.dram_tensor("x", [T, H], F32, kind="ExternalInput").ap()
    wq_in = nc.dram_tensor("wq", [H, DH], F32, kind="ExternalInput").ap()
    wk_in = nc.dram_tensor("wk", [H, DH], F32, kind="ExternalInput").ap()
    wv_in = nc.dram_tensor("wv", [H, DH], F32, kind="ExternalInput").ap()
    wo_in = nc.dram_tensor("wo", [DH, H], F32, kind="ExternalInput").ap()
    rw_in = nc.dram_tensor("rw", [32, H], F32, kind="ExternalInput").ap()
    rb_in = nc.dram_tensor("rb", [32], F32, kind="ExternalInput").ap()
    if COLL:
        gw5_in = nc.dram_tensor("gw5", [NSLOT, I, H], F32, kind="ExternalInput").ap()
        uw5_in = nc.dram_tensor("uw5", [NSLOT, I, H], F32, kind="ExternalInput").ap()
        dw5_in = nc.dram_tensor("dw5", [NSLOT, H, I], F32, kind="ExternalInput").ap()
        sgw_in = nc.dram_tensor("sgw", [SH_I, H], F32, kind="ExternalInput").ap()
        suw_in = nc.dram_tensor("suw", [SH_I, H], F32, kind="ExternalInput").ap()
        sdw_in = nc.dram_tensor("sdw", [H, SH_I], F32, kind="ExternalInput").ap()
        NE_TBL = 8 * NSLOT + 2
        wstage = nc.dram_tensor("wstage", [NSLOT, 128, 12288], BF16, kind="Internal").ap()
    else:
        gw_in = nc.dram_tensor("gw", [32, I, H], F32, kind="ExternalInput").ap()
        uw_in = nc.dram_tensor("uw", [32, I, H], F32, kind="ExternalInput").ap()
        dw_in = nc.dram_tensor("dw", [32, H, I], F32, kind="ExternalInput").ap()
        sgw_in = nc.dram_tensor("sgw", [SH_I, H], F32, kind="ExternalInput").ap()
        suw_in = nc.dram_tensor("suw", [SH_I, H], F32, kind="ExternalInput").ap()
        sdw_in = nc.dram_tensor("sdw", [H, SH_I], F32, kind="ExternalInput").ap()
        NE_TBL = NE
    y_out = nc.dram_tensor("y", [T, H], F32, kind="ExternalOutput").ap()
    wcache = nc.dram_tensor("wcache", [NE_TBL, 128, 12288], BF16, kind="Internal").ap()
    SPARSE = bool(cfg.get("sparse", True))
    NSCHED = (4 * T) // 128 + 32
    xln_d = nc.dram_tensor("xln_d", [T, 1024], BF16, kind="Internal").ap()
    wrt_d = nc.dram_tensor("wrt_d", [T * 32, 1], F32, kind="Internal").ap()
    cv_d = nc.dram_tensor("cv_d", [NSCHED * 128, 1], F32, kind="Internal").ap()
    pout_d = nc.dram_tensor("pout_d", [NSCHED * 128, 1024], BF16, kind="Internal").ap()

    from contextlib import ExitStack
    ctx = ExitStack()
    e_ = ctx.enter_context

    def sb(name, shape, dt):
        return e_(nc.sbuf_tensor(name, shape, dt))

    def psum(name, shape, dt):
        return e_(nc.psum_tensor(name, shape, dt))

    x_sb = sb("x_sb", [128, TT * 1024], F32)
    xfm = sb("xfm", [128, KC * T], BF16)
    xln = sb("xln", [128, 2 * 1024], BF16)
    wnat = sb("wnat", [128, 2 * 4096], BF16)
    wT_sb = [sb("wT0", [128, 12288], BF16), sb("wT1", [128, 12288], BF16)]
    sG = sb("sG", [128, IC * NTW], BF16)
    hfm = sb("hfm", [128, IC * NTW], BF16)
    assert 4 * T <= 12288
    qfm = wT_sb[1][:, 0 * T:1 * T]
    kfm = wT_sb[1][:, 1 * T:2 * T]
    vtm = wT_sb[1][:, 2 * T:3 * T]
    ctxfm = wT_sb[1][:, 3 * T:4 * T]
    arow2 = [xln[:, 0:T] if T <= 2048 else None, sG[:, 0:T]]
    aT = hfm[:, 0:T]
    ctxscr = sb("ctxscr", [128, 128], BF16)
    wqb = sb("wqb", [128, KC * 128], BF16)
    wkb = sb("wkb", [128, KC * 128], BF16)
    wvb = sb("wvb", [128, KC * 128], BF16)
    wob = sb("wob", [128, 1024], BF16)
    rwT = sb("rwT", [128, KC * 32], BF16)
    rw_nat = sb("rw_nat", [32, 1024], BF16)
    bias_row = sb("bias_row", [1, 32], F32)
    bias_bc = sb("bias_bc", [128, 32], F32)
    ones1 = sb("ones1", [1, 128], F32)
    ident = sb("ident", [128, 128], BF16)
    identf = sb("identf", [128, 128], F32)
    maskc = sb("maskc", [128, 128], F32)
    scores_all = sb("scores_all", [128, TT * 32], F32)
    wrt = sb("wrt", [128, TT * 32], F32)
    bnscr = sb("bnscr", [128, 24], F32)
    mv = sb("mv", [128, 2 * TT], F32)
    rstd = sb("rstd", [128, TT], F32)
    nmrs = sb("nmrs", [128, TT], F32)
    lnt1 = sb("lnt1", [128, TT], F32)
    NCH = 3
    RS = []
    for ci in range(NCH):
        RS.append({nm: sb(f"r_{nm}{ci}", [128, wdt], F32) for nm, wdt in (
            ("sfc", 32), ("g8", 32), ("gs8", 8), ("m8a", 8), ("m8b", 8),
            ("ga", 4), ("gb", 4), ("gc", 4), ("gd", 4),
            ("t1", 1), ("t2", 1), ("t3", 1),
            ("sm", 32), ("tmp", 32), ("a", 32), ("b", 32), ("c", 32),
            ("d", 32), ("W", 32))})
    ones8 = sb("ones8", [128, 8], F32)
    ATS = []
    for ci in range(2):
        ATS.append({nm: sb(f"at_{nm}{ci}", [128, wdt], F32) for nm, wdt in (
            ("part", 4), ("nm", 1), ("nms", 1), ("sum", 1), ("rinv", 1))})
    spool = sb("spool", [128, 1536], F32)
    sel_all = spool[:, 0:512]
    dstrow = spool[:, 512:1024]
    cvio = spool[:, 1024:1536]
    abc_s = spool[:, 1024:1536]
    triu128 = sb("triu128", [128, 128], BF16)
    triu16s = sb("triu16s", [16, 16], BF16)
    ones128c = sb("ones128c", [128, 1], BF16)
    thr16 = sb("thr16", [16, 1], F32)
    ones16c = sb("ones16c", [16, 1], BF16)
    selbf = sb("selbf", [128, 512], BF16)
    tcnt_sb = sb("tcnt_sb", [32, 16], BF16)
    tcnt_T = sb("tcnt_T", [16, 32], BF16)
    csum_sb = sb("csum_sb", [16, 32], F32)
    erow = sb("erow", [1, 160], F32)
    addrow_s = sb("addrow_s", [16, 32], F32)
    addflat = sb("addflat", [1, 512], F32)
    jcol = sb("jcol", [128, 1], F32)
    pcol = sb("pcol", [128, 1], F32)
    schedc = sb("schedc", [128, 1], F32)
    schedbc = sb("schedbc", [128, 1], F32)
    dst4 = sb("dst4", [128, 8], F32)
    cv4 = sb("cv4", [128, 8], F32)
    dst4a = sb("dst4a", [128, 4 * (T // 128)], F32)
    cv4a = sb("cv4a", [128, 4 * (T // 128)], F32)
    dst4i = sb("dst4i", [128, 4 * (T // 128)], I32)
    cv4i = sb("cv4i", [128, 4 * (T // 128)], I32)
    e4f = dst4a
    w4a = sb("w4a", [128, 4 * (T // 128)], F32)
    iota_e = sb("iota_e", [128, 32], F32)
    ohscr = sb("ohscr", [128, 32], F32)
    cvload = sb("cvload", [8, 128], F32)
    rowi = sb("rowi", [128, NSCHED], I32)
    idxwi = sb("idxwi", [128, NSCHED], I32)
    sent_t = sb("sent_t", [128, 32], F32)

    pGU = [psum(f"pgu{i}", [128, 512], F32) for i in range(4)]
    pD0 = psum("pd0", [128, 512], F32)
    pD1 = psum("pd1", [128, 512], F32)
    pT = psum("pt", [128, 1024], BF16)
    pX = psum("px", [128, 512], F32)

    k = Em(nc)

    _bufs = {}

    def buf(name):
        if name not in _bufs:
            _bufs[name] = Buf()
        return _bufs[name]

    xtile = [buf(f"x{t}") for t in range(TT)]
    bXfm = buf("xfm")
    bXln = [buf("xln0"), buf("xln1")]
    bWnat = [buf("wnat0"), buf("wnat1")]
    bWT = [buf("wT0"), buf("wT1")]
    bSG = buf("sG")
    bHfm = buf("hfm")
    bPGU = [buf(f"pgu{i}") for i in range(4)]
    bPX = buf("px")
    bPDrot = [buf("pd0"), buf("pd1"), bPX]
    bPT = buf("pT")
    bPTh = [buf("pTh0"), buf("pTh1")]
    bStats = buf("stats")
    bBn = [buf("bn0"), buf("bn1")]
    bScores = buf("scores")
    bWrt = buf("wrt")
    bRS = [buf(f"rsc{ci}") for ci in range(NCH)]
    bQ = bWT[1]; bK = bWT[1]; bV = bWT[1]; bCtx = bWT[1]
    bArow = [buf("arowA"), bSG]
    bAT = bHfm; bCtxS = buf("ctxscr")
    bAtt = [buf("at0"), buf("at1")]
    bConsts = buf("consts")
    bCache = [buf(f"cache{e}") for e in range(NE)]
    bSpool = buf("spool")
    bSelbf = buf("selbf")
    bSched = buf("sched")
    bD4 = buf("d4")
    bCvm = buf("cvm")
    bWv = buf("wvals")
    bXlnD = buf("xln_dram")
    bWrtD = buf("wrt_dram")
    bCvD = buf("cv_dram")
    bPoutD = buf("pout_dram")
    bGb = [buf("gb0"), buf("gb1")]
    bGf = [buf("gf0"), buf("gf1")]
    bMx = [buf("mx0"), buf("mx1")]
    bWTd = [buf("wTd0"), buf("wTd1")]
    bWv2 = [buf("wv0"), buf("wv1")]
    bCacheAll = buf("cacheall")

    xfm3 = xfm[:, :].rearrange("p (c t) -> p c t", c=KC)
    mv2 = mv[:, :].rearrange("p (t two) -> p t two", two=2)
    g8v = [R["g8"][:, :].rearrange("p (g j) -> p g j", g=4) for R in RS]
    wTgu_v = [w[:, 0:8192].rearrange("p (m k q c) -> p m k q c", m=2, k=KC, q=IC)
              for w in wT_sb]
    wTd_v = [w[:, 8192:12288].rearrange("p (j q c) -> p j q c", j=IC, q=KC)
             for w in wT_sb]
    pTj_v = pT[:, :].rearrange("p (j c) -> p j c", j=8)
    pTd_v = pT[:, :].rearrange("p (j w c) -> p j w c", j=IC, w=2)

    def emit_prog():
        m = k.dma("sync", lambda s: s.dma_start(
            out=x_sb[:, :], in_=x_in.rearrange("(t p) f -> p t f", p=128)),
            q="sd")
        for t in range(TT):
            xtile[t].note_w(m)

        for w_in, w_sb_ in ((wq_in, wqb), (wk_in, wkb), (wv_in, wvb)):
            k.dma("gpsimd", lambda g, w_in=w_in, w_sb_=w_sb_: g.dma_start(
                out=w_sb_[:, :], in_=w_in.rearrange("(c p) d -> p c d", p=128)),
                q="gd", writes=[bConsts])
        k.dma("gpsimd", lambda g: g.dma_start(out=wob[:, :], in_=wo_in),
              q="gd", writes=[bConsts])
        k.dma("gpsimd", lambda g: g.dma_start(out=rw_nat[:, :], in_=rw_in),
              q="gd", writes=[bConsts])
        k.dma("sync", lambda s: s.dma_start(
            out=bias_row[:, :], in_=rb_in.rearrange("(o e) -> o e", o=1)),
            q="sd", writes=[bConsts])

        k.do("gpsimd", lambda g: g.memset(ident[:, :], 0.0), writes=[bConsts])
        k.do("gpsimd", lambda g: g.affine_select(
            out=ident[:, :], in_=ident[:, :], compare_op=ALU.not_equal, fill=1.0,
            base=0, pattern=[[-1, 128]], channel_multiplier=1),
            reads=[bConsts], writes=[bConsts])
        k.do("gpsimd", lambda g: g.memset(maskc[:, :], 0.0), writes=[bConsts])
        k.do("gpsimd", lambda g: g.affine_select(
            out=maskc[:, :], in_=maskc[:, :], compare_op=ALU.is_ge, fill=NEG_BIG,
            base=0, pattern=[[-1, 128]], channel_multiplier=1),
            reads=[bConsts], writes=[bConsts])
        k.do("gpsimd", lambda g: g.memset(ones1[:, :], 1.0), writes=[bConsts])
        if SPARSE:
            k.do("gpsimd", lambda g: g.memset(identf[:, :], 0.0), writes=[bConsts])
            k.do("gpsimd", lambda g: g.affine_select(
                out=identf[:, :], in_=identf[:, :], compare_op=ALU.not_equal, fill=1.0,
                base=0, pattern=[[-1, 128]], channel_multiplier=1),
                reads=[bConsts], writes=[bConsts])
            k.do("gpsimd", lambda g: g.memset(triu128[:, :], 0.0), writes=[bConsts])
            k.do("gpsimd", lambda g: g.affine_select(
                out=triu128[:, :], in_=triu128[:, :], compare_op=ALU.is_gt, fill=1.0,
                base=0, pattern=[[-1, 128]], channel_multiplier=1),
                reads=[bConsts], writes=[bConsts])
            k.do("gpsimd", lambda g: g.memset(triu16s[:, :], 0.0), writes=[bConsts])
            k.do("gpsimd", lambda g: g.affine_select(
                out=triu16s[:, :], in_=triu16s[:, :], compare_op=ALU.is_ge, fill=1.0,
                base=0, pattern=[[-1, 16]], channel_multiplier=1),
                reads=[bConsts], writes=[bConsts])
            k.do("gpsimd", lambda g: g.memset(ones128c[:, :], 1.0), writes=[bConsts])
            k.do("gpsimd", lambda g: g.memset(ones16c[:, :], 1.0), writes=[bConsts])
            k.do("gpsimd", lambda g: g.iota(jcol[:, :], [[1, 1]], base=0,
                                            channel_multiplier=1,
                                            allow_small_or_imprecise_dtypes=True),
                 writes=[bConsts])
            k.do("gpsimd", lambda g: g.iota(pcol[:, :], [[1, 1]], base=0,
                                            channel_multiplier=1,
                                            allow_small_or_imprecise_dtypes=True),
                 writes=[bConsts])
            k.do("gpsimd", lambda g: g.memset(sent_t[:, :], 100000.0), writes=[bConsts])
            k.do("gpsimd", lambda g: g.iota(iota_e[:, :], [[1, 32]], base=0,
                                            channel_multiplier=0,
                                            allow_small_or_imprecise_dtypes=True),
                 writes=[bConsts])
            k.do("gpsimd", lambda g: g.iota(thr16[:, :], [[1, 1]], base=0,
                                            channel_multiplier=128,
                                            allow_small_or_imprecise_dtypes=True),
                 writes=[bConsts])
            k.do("gpsimd", lambda g: g.memset(wnat[:, :], 0.0),
                 writes=[bWnat[0], bWnat[1]])
        k.do("vector", lambda v: v.memset(ones8[:, :], 1.0), writes=[bConsts])
        for ci in range(NCH):
            k.do("vector", lambda v, ci=ci: v.memset(RS[ci]["gs8"][:, 4:8], -3.0e38),
                 writes=[bRS[ci]])

        builds = [(lambda te, kk=kk: te.transpose(
            pT[:, 32 * kk:32 * (kk + 1)], rw_nat[0:32, 128 * kk:128 * (kk + 1)],
            ident[0:32, 0:32])) for kk in range(KC)]
        k.group("tensor", builds, reads=[bConsts], writes=[bPT])
        k.do("vector", lambda v: v.tensor_copy(out=rwT[:, :], in_=pT[:, 0:KC * 32]),
             reads=[bPT], writes=[bConsts])

        k.do("tensor", lambda t: t.matmul(pX[:, 0:32], ones1[0:1, :],
                                          bias_row[0:1, :], start=True, stop=True),
             reads=[bConsts], writes=[bPX])
        k.do("vector", lambda v: v.tensor_copy(out=bias_bc[:, :], in_=pX[:, 0:32]),
             reads=[bPX], writes=[bConsts])

        if "ffn1" in phases:
            emit_ffn(0)
        if "attn" in phases:
            emit_attention()
        if "ffn2" in phases:
            emit_ffn(1)

        for t in range(TT):
            k.waits("sync", xtile[t].rdeps())
        mfin = k.dma("sync", lambda s: s.dma_start(
            out=y_out.rearrange("(t p) f -> p t f", p=128), in_=x_sb[:, :]),
            q="st")
        for q in DMAQS:
            for i in range(QLANES[q]):
                lane = f"{q}{i}"
                if k.cnt[lane] > 0:
                    k.wait("sync", (lane, k.cnt[lane]))
        k.wait("sync", mfin)

    def emit_ln(dram=False):
        for t in range(TT):
            sl = t % 2
            xt = x_sb[:, 1024 * t:1024 * (t + 1)]
            o = 12 * sl
            k.group("vector", [
                (lambda v, xt=xt, o=o: v.bn_stats(out=bnscr[:, o:o + 6], in_=xt[:, 0:512])),
                (lambda v, xt=xt, o=o: v.bn_stats(out=bnscr[:, o + 6:o + 12], in_=xt[:, 512:1024])),
            ], reads=[xtile[t]], writes=[bBn[sl]])
            k.do("vector", lambda v, t=t, o=o: v.bn_aggr(out=mv2[:, t, :], in_=bnscr[:, o:o + 12]),
                 reads=[bBn[sl]], writes=[bStats])
        k.do("vector", lambda v: v.tensor_scalar(
            out=lnt1[:, :], in0=mv2[:, :, 1:2], scalar1=1e-5, scalar2=None, op0=ALU.add),
            reads=[bStats], writes=[bStats])
        k.do("scalar", lambda s: s.activation(out=rstd[:, :], in_=lnt1[:, :], func=AF.Sqrt),
             reads=[bStats], writes=[bStats])
        k.do("vector", lambda v: v.reciprocal(out=rstd[:, :], in_=rstd[:, :]),
             reads=[bStats], writes=[bStats])
        k.do("vector", lambda v: v.scalar_tensor_tensor(
            out=nmrs[:, :], in0=mv2[:, :, 0:1], scalar=-1.0, in1=rstd[:, :],
            op0=ALU.mult, op1=ALU.mult), reads=[bStats], writes=[bStats])

        def norm(t):
            p = t % 2
            xt = x_sb[:, 1024 * t:1024 * (t + 1)]
            xl = xln[:, 1024 * p:1024 * (p + 1)]
            k.do("vector", lambda v, xt=xt, xl=xl, t=t: v.tensor_scalar(
                out=xl, in0=xt, scalar1=rstd[:, t:t + 1], scalar2=nmrs[:, t:t + 1],
                op0=ALU.mult, op1=ALU.add),
                reads=[xtile[t], bStats], writes=[bXln[p]])

        norm(0)
        if TT > 1:
            norm(1)
        for t in range(TT):
            p = t % 2
            xl = xln[:, 1024 * p:1024 * (p + 1)]
            builds = [(lambda te, xl=xl, kk=kk: te.transpose(
                pT[:, 128 * kk:128 * (kk + 1)], xl[:, 128 * kk:128 * (kk + 1)],
                ident[:, :])) for kk in range(KC)]
            k.group("tensor", builds, reads=[bXln[p], bConsts], writes=[bPT])
            dst = xfm3[:, :, 128 * t:128 * (t + 1)]
            k.do("vector", lambda v, dst=dst: v.tensor_copy(out=dst, in_=pT[:, 0:KC * 128]),
                 reads=[bPT], writes=[bXfm])
            if dram:
                k.dma("sync", lambda s, xl=xl, t=t: s.dma_start(
                    out=xln_d[128 * t:128 * (t + 1), :], in_=xl),
                    q="sd", reads=[bXln[p]], writes=[bXlnD])
            if t + 2 < TT:
                norm(t + 2)

    def emit_routing():
        for t in range(TT):
            builds = [(lambda te, kk=kk, t=t: te.matmul(
                pX[:, 32 * t:32 * (t + 1)],
                xfm[:, T * kk + 128 * t:T * kk + 128 * (t + 1)],
                rwT[:, 32 * kk:32 * (kk + 1)],
                start=(kk == 0), stop=(kk == KC - 1))) for kk in range(KC)]
            k.group("tensor", builds, reads=[bXfm, bConsts], writes=[bPX])
        for t in range(TT):
            k.do("scalar", lambda s, t=t: s.activation(
                out=scores_all[:, 32 * t:32 * (t + 1)], in_=pX[:, 32 * t:32 * (t + 1)],
                func=AF.Sigmoid), reads=[bPX], writes=[bScores])

        D = "vector"

        def steps(t, ci):
            R = RS[ci]
            bR = bRS[ci]
            sc = scores_all[:, 32 * t:32 * (t + 1)]
            wout = wrt[:, 32 * t:32 * (t + 1)]
            th = []
            A = th.append
            A(lambda: k.do(D, lambda v: v.tensor_add(out=R["sfc"][:, :], in0=sc, in1=bias_bc[:, :]),
                           reads=[bScores, bConsts], writes=[bR]))
            A(lambda: k.group(D, [
                (lambda v, g=g: v.max(out=R["g8"][:, 8 * g:8 * (g + 1)],
                                      in_=R["sfc"][:, 8 * g:8 * (g + 1)])) for g in range(4)],
                reads=[bR], writes=[bR]))
            A(lambda: k.do(D, lambda v: v.tensor_add(out=R["gs8"][:, 0:4], in0=g8v[ci][:, :, 0:1],
                                                     in1=g8v[ci][:, :, 1:2]),
                           reads=[bR], writes=[bR]))
            A(lambda: k.do(D, lambda v: v.max(out=R["m8a"][:, :], in_=R["gs8"][:, :]),
                           reads=[bR], writes=[bR]))
            A(lambda: k.group(D, [
                (lambda v: v.tensor_scalar(out=R["ga"][:, :], in0=R["gs8"][:, 0:4],
                                           scalar1=R["m8a"][:, 1:2], scalar2=None, op0=ALU.is_gt)),
                (lambda v: v.tensor_scalar(out=R["gb"][:, :], in0=R["gs8"][:, 0:4],
                                           scalar1=R["m8a"][:, 1:2], scalar2=None, op0=ALU.is_equal)),
            ], reads=[bR], writes=[bR]))
            A(lambda: k.do(D, lambda v: v.reduce_sum(out=R["t1"][:, :], in_=R["ga"][:, :], axis=AX.X),
                           reads=[bR], writes=[bR]))
            A(lambda: k.group(D, [
                (lambda v: v.tensor_copy(out=R["gc"][:, 0:1], in_=R["gb"][:, 0:1])),
                (lambda v: v.tensor_add(out=R["gc"][:, 1:4], in0=R["gb"][:, 1:4], in1=R["gb"][:, 0:3])),
            ], reads=[bR], writes=[bR]))
            A(lambda: k.group(D, [
                (lambda v: v.tensor_copy(out=R["gd"][:, 0:2], in_=R["gc"][:, 0:2])),
                (lambda v: v.tensor_add(out=R["gd"][:, 2:4], in0=R["gc"][:, 2:4], in1=R["gc"][:, 0:2])),
            ], reads=[bR], writes=[bR]))
            A(lambda: k.do(D, lambda v: v.tensor_scalar(out=R["gc"][:, :], in0=R["gd"][:, :],
                                                        scalar1=R["t1"][:, 0:1], scalar2=2.0,
                                                        op0=ALU.add, op1=ALU.is_le),
                           reads=[bR], writes=[bR]))
            A(lambda: k.do(D, lambda v: v.tensor_mul(out=R["gc"][:, :], in0=R["gc"][:, :], in1=R["gb"][:, :]),
                           reads=[bR], writes=[bR]))
            A(lambda: k.do(D, lambda v: v.tensor_add(out=R["gc"][:, :], in0=R["gc"][:, :], in1=R["ga"][:, :]),
                           reads=[bR], writes=[bR]))
            A(lambda: k.group(D, [
                (lambda v, g=g: v.tensor_scalar(
                    out=R["sm"][:, 8 * g:8 * (g + 1)], in0=ones8[:, :],
                    scalar1=R["gc"][:, g:g + 1], scalar2=None, op0=ALU.mult)) for g in range(4)],
                reads=[bR, bConsts], writes=[bR]))
            A(lambda: k.do(D, lambda v: v.tensor_mul(out=R["tmp"][:, :], in0=R["sfc"][:, :], in1=R["sm"][:, :]),
                           reads=[bR], writes=[bR]))
            A(lambda: k.do(D, lambda v: v.max(out=R["m8b"][:, :], in_=R["tmp"][:, :]),
                           reads=[bR], writes=[bR]))
            A(lambda: k.group(D, [
                (lambda v: v.tensor_scalar(out=R["a"][:, :], in0=R["tmp"][:, :],
                                           scalar1=R["m8b"][:, 3:4], scalar2=None, op0=ALU.is_gt)),
                (lambda v: v.tensor_scalar(out=R["b"][:, :], in0=R["tmp"][:, :],
                                           scalar1=R["m8b"][:, 3:4], scalar2=None, op0=ALU.is_equal)),
            ], reads=[bR], writes=[bR]))
            A(lambda: k.do(D, lambda v: v.reduce_sum(out=R["t1"][:, :], in_=R["b" if False else "a"][:, :], axis=AX.X),
                           reads=[bR], writes=[bR]))
            for (srcb, dstb), s in zip((("b", "c"), ("c", "d"), ("d", "c"), ("c", "d"), ("d", "c")),
                                       (1, 2, 4, 8, 16)):
                A(lambda srcb=srcb, dstb=dstb, s=s: k.group(D, [
                    (lambda v, s=s, srcb=srcb, dstb=dstb: v.tensor_copy(
                        out=R[dstb][:, 0:s], in_=R[srcb][:, 0:s])),
                    (lambda v, s=s, srcb=srcb, dstb=dstb: v.tensor_add(
                        out=R[dstb][:, s:32], in0=R[srcb][:, s:32], in1=R[srcb][:, 0:32 - s])),
                ], reads=[bR], writes=[bR]))
            A(lambda: k.do(D, lambda v: v.tensor_scalar(out=R["d"][:, :], in0=R["c"][:, :],
                                                        scalar1=R["t1"][:, 0:1], scalar2=4.0,
                                                        op0=ALU.add, op1=ALU.is_le),
                           reads=[bR], writes=[bR]))
            A(lambda: k.do(D, lambda v: v.tensor_mul(out=R["d"][:, :], in0=R["d"][:, :], in1=R["b"][:, :]),
                           reads=[bR], writes=[bR]))
            A(lambda: k.do(D, lambda v: v.tensor_add(out=R["d"][:, :], in0=R["d"][:, :], in1=R["a"][:, :]),
                           reads=[bR], writes=[bR]))
            A(lambda: k.do(D, lambda v: v.tensor_mul(out=R["W"][:, :], in0=sc, in1=R["d"][:, :]),
                           reads=[bR, bScores], writes=[bR]))
            A(lambda: k.do(D, lambda v: v.reduce_sum(out=R["t1"][:, :], in_=R["W"][:, :], axis=AX.X),
                           reads=[bR], writes=[bR]))
            A(lambda: k.do(D, lambda v: v.tensor_scalar(out=R["t3"][:, :], in0=R["t1"][:, :],
                                                        scalar1=1e-20, scalar2=None, op0=ALU.add),
                           reads=[bR], writes=[bR]))
            A(lambda: k.do(D, lambda v: v.reciprocal(out=R["t3"][:, :], in_=R["t3"][:, :]),
                           reads=[bR], writes=[bR]))
            A(lambda: k.do(D, lambda v: v.tensor_scalar(
                out=wout, in0=R["W"][:, :], scalar1=R["t3"][:, 0:1], scalar2=1.25,
                op0=ALU.mult, op1=ALU.mult), reads=[bR], writes=[bWrt]))
            if SPARSE:
                A(lambda: k.do(D, lambda v: v.tensor_copy(
                    out=sel_all[:, 32 * t:32 * (t + 1)], in_=R["d"][:, :]),
                    reads=[bR], writes=[bSpool, bMx[0], bMx[1]]))
            return th

        from itertools import zip_longest
        t = 0
        while t < TT:
            chains = []
            for ci in range(NCH):
                if t + ci < TT:
                    chains.append(steps(t + ci, ci))
            for tup in zip_longest(*chains):
                for f in tup:
                    if f is not None:
                        f()
            t += NCH

    def wsrc_aps(e):
        if COLL and e < NSLOT:
            g = gw5_in[e].rearrange("(c p) h -> p c h", p=128)
            u = uw5_in[e].rearrange("(c p) h -> p c h", p=128)
            d = dw5_in[e].rearrange("(c p) i -> p c i", p=128)
            return g, u, d
        if e < NR:
            g = gw_in[e].rearrange("(c p) h -> p c h", p=128)
            u = uw_in[e].rearrange("(c p) h -> p c h", p=128)
            d = dw_in[e].rearrange("(c p) i -> p c i", p=128)
        else:
            a = e - NR
            g = sgw_in[512 * a:512 * (a + 1), :].rearrange("(c p) h -> p c h", p=128)
            u = suw_in[512 * a:512 * (a + 1), :].rearrange("(c p) h -> p c h", p=128)
            d = sdw_in[:, 512 * a:512 * (a + 1)].rearrange("(c p) i -> p c i", p=128)
        return g, u, d

    def emit_weight_load_transpose(e):
        q = e % 2
        wT = wT_sb[q]
        srcs = wsrc_aps(e)
        for mi in range(3):
            p = mi % 2
            nat = wnat[:, 4096 * p:4096 * (p + 1)]
            k.dma("gpsimd", lambda g, src=srcs[mi], nat=nat: g.dma_start(out=nat, in_=src),
                  q="gd", writes=[bWnat[p]])
            if mi < 2:
                for c in range(IC):
                    builds = [(lambda te, c=c, kk=kk, nat=nat: te.transpose(
                        pT[:, 128 * kk:128 * (kk + 1)],
                        nat[:, 1024 * c + 128 * kk:1024 * c + 128 * (kk + 1)],
                        ident[:, :])) for kk in range(KC)]
                    k.group("tensor", builds, reads=[bWnat[p], bConsts], writes=[bPT])
                    dst = wTgu_v[q][:, mi, :, c, :]
                    k.do("vector", lambda v, dst=dst: v.tensor_copy(
                        out=dst, in_=pT[:, 0:KC * 128]),
                        reads=[bPT], writes=[bWT[q]])
            else:
                for c2 in range(0, KC, 2):
                    builds = []
                    for j in range(IC):
                        for cc in range(2):
                            builds.append(lambda te, c=c2 + cc, j=j, cc=cc, nat=nat:
                                          te.transpose(
                                              pTd_v[:, j, cc, :],
                                              nat[:, 512 * c + 128 * j:512 * c + 128 * (j + 1)],
                                              ident[:, :]))
                    k.group("tensor", builds, reads=[bWnat[p], bConsts], writes=[bPT])
                    dst = wTd_v[q][:, :, c2:c2 + 2, :]
                    k.do("vector", lambda v, dst=dst: v.tensor_copy(
                        out=dst, in_=pT[:, 0:1024]),
                        reads=[bPT], writes=[bWTd[q]])
        tgt = wstage if (COLL and e < NSLOT) else wcache
        k.dma("sync", lambda s, e=e, wT=wT, tgt=tgt: s.dma_start(out=tgt[e], in_=wT[:, :]),
              q="st", reads=[bWT[q], bWTd[q]],
              writes=[bCache[e % NSLOT if (COLL and e < NR) else e], bCacheAll])

    def emit_weight_cache_load(e):
        q = e % 2
        wT = wT_sb[q]
        k.dma("sync", lambda s, e=e, wT=wT: s.dma_start(out=wT[:, :], in_=wcache[e]),
              q="sd", reads=[bCache[e]], writes=[bWT[q], bWTd[q]])

    def emit_expert_compute(e):
        q = e % 2
        wT = wT_sb[q]
        pool = (pD0, pD1, pX)
        for nt in range(NNT):
            tok0 = NTW * nt
            xs = [xfm[:, T * kk + tok0:T * kk + tok0 + NTW] for kk in range(KC)]
            for c in range(IC):
                gb, ub = bPGU[2 * (c % 2)], bPGU[2 * (c % 2) + 1]
                gps, ups = pGU[2 * (c % 2)], pGU[2 * (c % 2) + 1]
                builds = [(lambda te, kk=kk, c=c, gps=gps, wT=wT, x=xs[kk]: te.matmul(
                    gps[:, 0:NTW], wT[:, 512 * kk + 128 * c:512 * kk + 128 * (c + 1)],
                    x, start=(kk == 0), stop=(kk == KC - 1))) for kk in range(KC)]
                k.group("tensor", builds, reads=[bWT[q], bXfm], writes=[gb])
                builds = [(lambda te, kk=kk, c=c, ups=ups, wT=wT, x=xs[kk]: te.matmul(
                    ups[:, 0:NTW],
                    wT[:, 4096 + 512 * kk + 128 * c:4096 + 512 * kk + 128 * (c + 1)],
                    x, start=(kk == 0), stop=(kk == KC - 1))) for kk in range(KC)]
                k.group("tensor", builds, reads=[bWT[q], bXfm], writes=[ub])
                k.do("scalar", lambda s, c=c, gps=gps: s.activation(
                    out=sG[:, NTW * c:NTW * (c + 1)], in_=gps[:, 0:NTW], func=AF.Silu),
                    reads=[gb], writes=[bSG])
                k.do("vector", lambda v, c=c, ups=ups: v.tensor_mul(
                    out=hfm[:, NTW * c:NTW * (c + 1)], in0=ups[:, 0:NTW],
                    in1=sG[:, NTW * c:NTW * (c + 1)]),
                    reads=[ub, bSG], writes=[bHfm])
            gi = 0
            for s_ in range(NSUB):
                t_tile = (tok0 // 128) + s_
                for h2 in range(2):
                    pb = pool[gi % 3]
                    bb = bPDrot[gi % 3]
                    gi += 1
                    builds = [(lambda te, c=c, s_=s_, h2=h2, pb=pb, wT=wT: te.matmul(
                        pb[:, 0:512],
                        hfm[:, NTW * c + 128 * s_:NTW * c + 128 * (s_ + 1)],
                        wT[:, 8192 + 1024 * c + 512 * h2:8192 + 1024 * c + 512 * (h2 + 1)],
                        start=(c == 0), stop=(c == IC - 1))) for c in range(IC)]
                    k.group("tensor", builds, reads=[bHfm, bWT[q]], writes=[bb])
                    xsub = x_sb[:, 1024 * t_tile + 512 * h2:1024 * t_tile + 512 * (h2 + 1)]
                    if e < NR:
                        wcol = wrt[:, 32 * t_tile + e:32 * t_tile + e + 1]
                        k.do("vector", lambda v, pb=pb, xsub=xsub, wcol=wcol:
                             v.scalar_tensor_tensor(
                                 out=xsub, in0=pb[:, 0:512], scalar=wcol, in1=xsub,
                                 op0=ALU.mult, op1=ALU.add),
                             reads=[bb, bWrt, xtile[t_tile]], writes=[xtile[t_tile]])
                    else:
                        k.do("vector", lambda v, pb=pb, xsub=xsub: v.scalar_tensor_tensor(
                            out=xsub, in0=pb[:, 0:512], scalar=0.5, in1=xsub,
                            op0=ALU.mult, op1=ALU.add),
                            reads=[bb, xtile[t_tile]], writes=[xtile[t_tile]])

    BREG = {}

    def _mk_bregs(g):
        if "cv" not in BREG:
            BREG["cv"] = g.to_reg(NSCHED * 128 - 1)
            BREG["tok"] = g.to_reg(T - 1)
            BREG["wc"] = g.to_reg(NE_TBL * 128 - 1)
            BREG["wr"] = g.to_reg(T * 32 - 1)
        return BREG

    def emit_sched(inv):
        SPW = [bSpool, bMx[0], bMx[1]]
        k.do("vector", lambda v: v.tensor_copy(out=selbf[:, 0:32 * TT], in_=sel_all[:, 0:32 * TT]),
             reads=[bSpool], writes=[bSelbf])
        for t in range(TT):
            k.do("tensor", lambda te, t=t: te.matmul(
                pX[:, 32 * t:32 * (t + 1)], triu128[:, :], selbf[:, 32 * t:32 * (t + 1)],
                start=True, stop=True), reads=[bSelbf, bConsts], writes=[bPX])
            k.do("tensor", lambda te, t=t: te.matmul(
                pD0[:, t:t + 1] if False else pD0[0:32, t:t + 1],
                selbf[:, 32 * t:32 * (t + 1)], ones128c[:, :],
                start=True, stop=True), reads=[bSelbf, bConsts], writes=[bPDrot[0]])
        k.do("vector", lambda v: v.tensor_copy(out=tcnt_sb[:, 0:TT], in_=pD0[0:32, 0:TT]),
             reads=[bPDrot[0]], writes=[bSched])
        k.do("tensor", lambda te: te.transpose(pT[0:TT, 0:32], tcnt_sb[0:32, 0:TT],
                                               ident[0:32, 0:32]),
             reads=[bSched, bConsts], writes=[bPT])
        k.do("vector", lambda v: v.tensor_copy(out=tcnt_T[0:TT, 0:32], in_=pT[0:TT, 0:32]),
             reads=[bPT], writes=[bSched])
        k.do("tensor", lambda te: te.matmul(pD0[0:TT, 32:64], triu16s[0:TT, 0:TT],
                                            tcnt_T[0:TT, 0:32], start=True, stop=True),
             reads=[bSched, bConsts], writes=[bPDrot[0]])
        k.do("tensor", lambda te: te.matmul(pD1[0:1, 0:32], ones16c[0:TT, :],
                                            tcnt_T[0:TT, 0:32], start=True, stop=True),
             reads=[bSched, bConsts], writes=[bPDrot[1]])
        k.do("vector", lambda v: v.tensor_copy(out=csum_sb[0:TT, 0:32], in_=pD0[0:TT, 32:64]),
             reads=[bPDrot[0]], writes=[bSched])
        k.do("vector", lambda v: v.tensor_copy(out=erow[:, 0:32], in_=pD1[0:1, 0:32]),
             reads=[bPDrot[1]], writes=[bSched])
        k.do("tensor", lambda te: te.matmul(pD1[0:16, 64:96], ones1[0:1, 0:16],
                                            erow[0:1, 0:32], start=True, stop=True),
             reads=[bSched, bConsts], writes=[bPDrot[1]])
        k.do("vector", lambda v: v.tensor_scalar(out=tcnt_T[0:16, 0:32],
                                                 in0=pD1[0:16, 64:96],
                                                 scalar1=thr16[0:16, 0:1], scalar2=None,
                                                 op0=ALU.is_gt),
             reads=[bPDrot[1], bConsts], writes=[bSched])
        k.do("tensor", lambda te: te.matmul(pD1[0:1, 96:128], ones16c[:, :],
                                            tcnt_T[0:16, 0:32], start=True, stop=True),
             reads=[bSched, bConsts], writes=[bPDrot[1]])
        k.do("vector", lambda v: v.tensor_copy(out=erow[:, 32:64], in_=pD1[0:1, 96:128]),
             reads=[bPDrot[1]], writes=[bSched])
        k.do("vector", lambda v: v.tensor_copy(out=erow[:, 64:96], in_=erow[:, 32:64]),
             reads=[bSched], writes=[bSched])
        a_off, b_off = 64, 96
        for s in (1, 2, 4, 8, 16):
            k.do("vector", lambda v, s=s, a=a_off, b=b_off: v.tensor_copy(
                out=erow[:, b:b + s], in_=erow[:, a:a + s]), reads=[bSched], writes=[bSched])
            k.do("vector", lambda v, s=s, a=a_off, b=b_off: v.tensor_add(
                out=erow[:, b + s:b + 32], in0=erow[:, a + s:a + 32], in1=erow[:, a:a + 32 - s]),
                reads=[bSched], writes=[bSched])
            a_off, b_off = b_off, a_off
        tend_off = a_off
        k.do("vector", lambda v, t=tend_off: v.tensor_sub(out=erow[:, 128:160],
                                                          in0=erow[:, t:t + 32],
                                                          in1=erow[:, 32:64]),
             reads=[bSched], writes=[bSched])
        k.do("vector", lambda v: v.tensor_scalar(out=erow[:, 128:160], in0=erow[:, 128:160],
                                                 scalar1=128.0, scalar2=None, op0=ALU.mult),
             reads=[bSched], writes=[bSched])
        k.do("tensor", lambda te: te.matmul(pD1[0:TT, 32:64], ones1[0:1, 0:TT],
                                            erow[0:1, 128:160], start=True, stop=True),
             reads=[bSched, bConsts], writes=[bPDrot[1]])
        k.do("vector", lambda v: v.tensor_add(out=addrow_s[0:TT, :], in0=csum_sb[0:TT, 0:32],
                                              in1=pD1[0:TT, 32:64]),
             reads=[bSched, bPDrot[1]], writes=[bSched])
        k.do("vector", lambda v: v.tensor_scalar(out=addrow_s[0:TT, :], in0=addrow_s[0:TT, :],
                                                 scalar1=-1.0, scalar2=None, op0=ALU.add),
             reads=[bSched], writes=[bSched])
        k.dma("gpsimd", lambda g: g.dma_start(out=addflat[:, 0:32 * TT], in_=addrow_s[0:TT, :]),
              q="gd", reads=[bSched], writes=[bSched])
        k.do("tensor", lambda te: te.matmul(pD0[:, 0:32 * TT] if 32 * TT <= 512 else None,
                                            ones1[0:1, :], addflat[0:1, 0:32 * TT],
                                            start=True, stop=True),
             reads=[bSched, bConsts], writes=[bPDrot[0]])
        k.do("vector", lambda v: v.tensor_copy(out=abc_s[:, 0:32 * TT], in_=pD0[:, 0:32 * TT]),
             reads=[bPDrot[0]], writes=[bSched, bSpool])
        k.do("vector", lambda v: v.tensor_add(out=dstrow[:, 0:32 * TT], in0=pX[:, 0:32 * TT],
                                              in1=abc_s[:, 0:32 * TT]),
             reads=[bPX, bSched], writes=SPW)
        k.do("vector", lambda v: v.tensor_mul(out=dstrow[:, 0:32 * TT], in0=dstrow[:, 0:32 * TT],
                                              in1=sel_all[:, 0:32 * TT]), reads=SPW, writes=SPW)
        k.do("vector", lambda v: v.tensor_add(out=dstrow[:, 0:32 * TT], in0=dstrow[:, 0:32 * TT],
                                              in1=sel_all[:, 0:32 * TT]), reads=SPW, writes=SPW)
        k.do("vector", lambda v: v.tensor_scalar(out=dstrow[:, 0:32 * TT], in0=dstrow[:, 0:32 * TT],
                                                 scalar1=-1.0, scalar2=None, op0=ALU.add),
             reads=SPW, writes=SPW)
        k.do("gpsimd", lambda g: g.iota(cvio[:, 0:32 * TT], [[4096, TT], [1, 32]], base=0,
                                        channel_multiplier=32,
                                        allow_small_or_imprecise_dtypes=True),
             writes=SPW)
        k.do("vector", lambda v: v.tensor_mul(out=cvio[:, 0:32 * TT], in0=cvio[:, 0:32 * TT],
                                              in1=sel_all[:, 0:32 * TT]), reads=SPW, writes=SPW)
        k.do("vector", lambda v: v.tensor_add(out=cvio[:, 0:32 * TT], in0=cvio[:, 0:32 * TT],
                                              in1=sel_all[:, 0:32 * TT]), reads=SPW, writes=SPW)
        k.do("vector", lambda v: v.tensor_scalar(out=cvio[:, 0:32 * TT], in0=cvio[:, 0:32 * TT],
                                                 scalar1=-1.0, scalar2=None, op0=ALU.add),
             reads=SPW, writes=SPW)
        for t in range(TT):
            k.do("vector", lambda v, t=t: v.max(out=dst4[:, :], in_=dstrow[:, 32 * t:32 * (t + 1)]),
                 reads=SPW, writes=[bD4])
            k.do("vector", lambda v, t=t: v.max(out=cv4[:, :], in_=cvio[:, 32 * t:32 * (t + 1)]),
                 reads=SPW, writes=[bD4])
            k.do("vector", lambda v, t=t: v.tensor_copy(out=dst4a[:, 4 * t:4 * (t + 1)],
                                                        in_=dst4[:, 0:4]),
                 reads=[bD4], writes=[bD4])
            k.do("vector", lambda v, t=t: v.tensor_copy(out=cv4a[:, 4 * t:4 * (t + 1)],
                                                        in_=cv4[:, 0:4]),
                 reads=[bD4], writes=[bD4])
        k.do("vector", lambda v: v.tensor_copy(out=dst4i[:, 0:4 * TT], in_=dst4a[:, 0:4 * TT]),
             reads=[bD4], writes=[bD4])
        cvd_j = cv_d.rearrange("(j p) o -> j (p o)", p=128)
        for kq in range(4):
            k.dma("sync", lambda s, kq=kq: s.dma_start(
                out=cvd_j[:, 32 * kq:32 * (kq + 1)],
                in_=sent_t[0:NSCHED, 0:32]),
                q="sd", reads=[bConsts], writes=[bCvD])
        for c in range(4 * TT):
            if "cvs" not in DBG_SKIP:
                k.dma("gpsimd", lambda g, c=c: g.indirect_dma_start(
                    out=cv_d, out_offset=IndirectOffsetOnAxis(dst4i[:, c:c + 1], 0),
                    in_=cv4a[:, c:c + 1], in_offset=None,
                    bounds_check=_mk_bregs(g)["cv"], oob_is_err=False),
                    q="gd", reads=[bD4], writes=[bCvD])
        k.do("tensor", lambda te, t=tend_off: te.matmul(
            pD1[0:NSCHED, 0:32] if NSCHED <= 128 else None, ones1[0:1, 0:NSCHED],
            erow[0:1, t:t + 32], start=True, stop=True),
            reads=[bSched, bConsts], writes=[bPDrot[1]])
        k.do("vector", lambda v: v.tensor_scalar(out=abc_s[0:NSCHED, 0:32],
                                                 in0=pD1[0:NSCHED, 0:32],
                                                 scalar1=jcol[0:NSCHED, 0:1], scalar2=None,
                                                 op0=ALU.is_le),
             reads=[bPDrot[1], bConsts], writes=[bSched, bSpool])
        k.do("vector", lambda v: v.reduce_sum(out=schedc[0:NSCHED, :], in_=abc_s[0:NSCHED, 0:32],
                                              axis=AX.X), reads=[bSched, bSpool], writes=[bSched])
        k.do("vector", lambda v: v.tensor_scalar(out=schedbc[0:NSCHED, 0:1],
                                                 in0=schedc[0:NSCHED, :], scalar1=float(NR),
                                                 scalar2=None, op0=ALU.is_equal),
             reads=[bSched], writes=[bSched])
        k.do("vector", lambda v: v.scalar_tensor_tensor(out=schedc[0:NSCHED, :],
                                                        in0=schedbc[0:NSCHED, 0:1],
                                                        scalar=float(255 - NR),
                                                        in1=schedc[0:NSCHED, :],
                                                        op0=ALU.mult, op1=ALU.add),
             reads=[bSched], writes=[bSched])
        k.do("tensor", lambda te: te.transpose(pD1[0:1, 32:32 + NSCHED],
                                               schedc[0:NSCHED, 0:1],
                                               identf[0:NSCHED, 0:NSCHED]),
             reads=[bSched, bConsts], writes=[bPDrot[1]])
        k.do("vector", lambda v: v.tensor_copy(out=addflat[:, 0:NSCHED],
                                               in_=pD1[0:1, 32:32 + NSCHED]),
             reads=[bPDrot[1]], writes=[bSched])
        k.do("tensor", lambda te: te.matmul(pD0[:, 0:NSCHED], ones1[0:1, :],
                                            addflat[0:1, 0:NSCHED], start=True, stop=True),
             reads=[bSched, bConsts], writes=[bPDrot[0]])
        k.do("vector", lambda v: v.tensor_scalar(out=idxwi[:, :], in0=pD0[:, 0:NSCHED],
                                                 scalar1=128.0, scalar2=pcol[:, 0:1],
                                                 op0=ALU.mult, op1=ALU.add),
             reads=[bPDrot[0], bConsts], writes=[bCvm])
        nb8 = (NSCHED + 7) // 8
        for b in range(nb8):
            j0 = 8 * b
            jn = min(8, NSCHED - j0)
            k.dma("sync", lambda s, j0=j0, jn=jn: s.dma_start(
                out=cvload[0:jn, :],
                in_=cv_d[128 * j0:128 * (j0 + jn)].rearrange("(j p) o -> j (p o)", p=128)),
                q="sd", reads=[bCvD], writes=[bSched])
            k.do("tensor", lambda te, jn=jn: te.transpose(
                pD1[:, 64:64 + jn] if True else None, cvload[0:jn, :], identf[0:jn, 0:jn]),
                reads=[bSched, bConsts], writes=[bPDrot[1]])
            k.do("vector", lambda v, j0=j0, jn=jn: v.tensor_copy(
                out=rowi[:, j0:j0 + jn], in_=pD1[:, 64:64 + jn]),
                reads=[bPDrot[1]], writes=[bCvm])
        k.do("vector", lambda v: v.tensor_scalar(out=rowi[:, :], in0=rowi[:, :],
                                                 scalar1=5, scalar2=None,
                                                 op0=ALU.arith_shift_right),
             reads=[bCvm], writes=[bCvm])

    DBG_SKIP = set(cfg.get("dbg_skip", ()))

    def emit_w4_prep():
        k.do("vector", lambda v: v.tensor_copy(out=cv4i[:, 0:4 * TT], in_=cv4a[:, 0:4 * TT]),
             reads=[bD4], writes=[bD4])
        k.do("vector", lambda v: v.tensor_scalar(out=cv4i[:, 0:4 * TT], in0=cv4i[:, 0:4 * TT],
                                                 scalar1=31, scalar2=None,
                                                 op0=ALU.bitwise_and),
             reads=[bD4], writes=[bD4])
        k.do("vector", lambda v: v.tensor_copy(out=e4f[:, 0:4 * TT], in_=cv4i[:, 0:4 * TT]),
             reads=[bD4], writes=[bD4])

    def emit_w4_step(c):
        t = c // 4
        k.do("vector", lambda v, c=c: v.tensor_scalar(
            out=ohscr[:, :], in0=iota_e[:, :], scalar1=e4f[:, c:c + 1],
            scalar2=None, op0=ALU.is_equal),
            reads=[bD4, bConsts], writes=[bD4])
        k.do("vector", lambda v, t=t: v.tensor_mul(
            out=ohscr[:, :], in0=ohscr[:, :], in1=wrt[:, 32 * t:32 * (t + 1)]),
            reads=[bD4, bWrt], writes=[bD4])
        k.do("vector", lambda v, c=c: v.reduce_sum(
            out=w4a[:, c:c + 1], in_=ohscr[:, :], axis=AX.X),
            reads=[bD4], writes=[bD4])

    def sp_gather(j):
        q = j % 2
        gb = wnat[:, 2048 * q:2048 * q + 1024]
        if "rowg" not in DBG_SKIP:
            k.dma("gpsimd", lambda g, j=j, gb=gb: g.indirect_dma_start(
                out=gb, out_offset=None,
                in_=xln_d, in_offset=IndirectOffsetOnAxis(rowi[:, j:j + 1], 0),
                bounds_check=_mk_bregs(g)["tok"], oob_is_err=False),
                q="gd", reads=[bCvm, bXlnD], writes=[bGb[q]])
        if "wg" not in DBG_SKIP:
            k.dma("gpsimd", lambda g, j=j, q=q: g.indirect_dma_start(
                out=wT_sb[q][:, 0:8192], out_offset=None,
                in_=wcache.rearrange("e p f -> (e p) f"),
                in_offset=IndirectOffsetOnAxis(idxwi[:, j:j + 1], 0),
                bounds_check=_mk_bregs(g)["wc"], oob_is_err=False),
                q="gd", reads=[bCvm, bCacheAll], writes=[bWT[q]])
            k.dma("gpsimd", lambda g, j=j, q=q: g.indirect_dma_start(
                out=wT_sb[q][:, 8192:12288], out_offset=None,
                in_=wcache.rearrange("e p f -> (e p) f"),
                in_offset=IndirectOffsetOnAxis(idxwi[:, j:j + 1], 0),
                element_offset=8192,
                bounds_check=_mk_bregs(g)["wc"], oob_is_err=False),
                q="gd", reads=[bCvm, bCacheAll], writes=[bWTd[q]])


    def sp_tile_a(j):
        q = j % 2
        gb = wnat[:, 2048 * q:2048 * q + 1024]
        gf = wnat[:, 4096 + 2048 * q:4096 + 2048 * q + 1024]
        wT = wT_sb[q]
        builds = [(lambda te, kk=kk, gb=gb: te.transpose(
            pT[:, 128 * kk:128 * (kk + 1)], gb[:, 128 * kk:128 * (kk + 1)], ident[:, :]))
            for kk in range(KC)]
        k.group("tensor", builds, reads=[bGb[q], bConsts], writes=[bPT])
        k.do("scalar", lambda s, gf=gf: s.copy(out=gf, in_=pT[:, 0:1024]),
             reads=[bPT], writes=[bGf[q]])
        for c in range(IC):
            builds = [(lambda te, kk=kk, c=c, gf=gf, wT=wT: te.matmul(
                pGU[0][:, 128 * c:128 * (c + 1)],
                wT[:, 512 * kk + 128 * c:512 * kk + 128 * (c + 1)],
                gf[:, 128 * kk:128 * (kk + 1)],
                start=(kk == 0), stop=(kk == KC - 1))) for kk in range(KC)]
            k.group("tensor", builds, reads=[bWT[q], bGf[q]], writes=[bPGU[0]])
        for c in range(IC):
            builds = [(lambda te, kk=kk, c=c, gf=gf, wT=wT: te.matmul(
                pGU[1][:, 128 * c:128 * (c + 1)],
                wT[:, 4096 + 512 * kk + 128 * c:4096 + 512 * kk + 128 * (c + 1)],
                gf[:, 128 * kk:128 * (kk + 1)],
                start=(kk == 0), stop=(kk == KC - 1))) for kk in range(KC)]
            k.group("tensor", builds, reads=[bWT[q], bGf[q]], writes=[bPGU[1]])
        k.do("scalar", lambda s, q=q: s.activation(
            out=sG[:, 512 * q:512 * (q + 1)], in_=pGU[0][:, 0:512], func=AF.Silu),
            reads=[bPGU[0]], writes=[bSG])
        k.do("vector", lambda v, q=q: v.tensor_mul(
            out=hfm[:, 512 * q:512 * (q + 1)], in0=pGU[1][:, 0:512],
            in1=sG[:, 512 * q:512 * (q + 1)]),
            reads=[bPGU[1], bSG], writes=[bHfm])

    def sp_tile_b(j, first=False):
        q = j % 2
        wT = wT_sb[q]
        gf = wnat[:, 4096 + 2048 * q:4096 + 2048 * q + 1024]
        for h2 in range(2):
            pb = (pD0, pD1)[h2]
            builds = [(lambda te, c=c, h2=h2, pb=pb, q=q, wT=wT: te.matmul(
                pb[:, 0:512],
                hfm[:, 512 * q + 128 * c:512 * q + 128 * (c + 1)],
                wT[:, 8192 + 1024 * c + 512 * h2:8192 + 1024 * c + 512 * (h2 + 1)],
                start=(c == 0), stop=(c == IC - 1))) for c in range(IC)]
            k.group("tensor", builds, reads=[bHfm, bWTd[q]], writes=[bPDrot[h2]])
            k.do("vector", lambda v, pb=pb, h2=h2, gf=gf: v.tensor_copy(
                out=gf[:, 512 * h2:512 * (h2 + 1)], in_=pb[:, 0:512]),
                reads=[bPDrot[h2]], writes=[bGf[q]])
        k.dma("sync", lambda s, j=j, gf=gf: s.dma_start(
            out=pout_d[128 * j:128 * (j + 1), :], in_=gf[:, :]),
            q="sd", reads=[bGf[q]], writes=[bPoutD])

    def emit_sparse_tiles():
        k.do("gpsimd", lambda g: g.engine_nop(),
             writes=[bWnat[0], bWnat[1], bGb[0], bGb[1], bGf[0], bGf[1]])
        sp_gather(0)
        emit_w4_prep()
        for j in range(NSCHED):
            if j + 1 < NSCHED:
                sp_gather(j + 1)
            sp_tile_a(j)
            sp_tile_b(j, first=(j < 2))
            if j < 4 * TT:
                emit_w4_step(j)

    def emit_mix_merge():
        gbm = [wnat[:, 0:1024], wnat[:, 2048:3072]]
        for t in range(TT):
            xt = x_sb[:, 1024 * t:1024 * (t + 1)]
            for half in range(2):
                for m in range(2):
                    sl = 2 * half + m
                    k.dma("gpsimd", lambda g, t=t, sl=sl, m=m, gbm=gbm: g.indirect_dma_start(
                        out=gbm[m][:, :], out_offset=None,
                        in_=pout_d,
                        in_offset=IndirectOffsetOnAxis(dst4i[:, 4 * t + sl:4 * t + sl + 1], 0),
                        bounds_check=_mk_bregs(g)["cv"], oob_is_err=False),
                        q="gd", reads=[bD4, bPoutD], writes=[bGb[m]])
                for m in range(2):
                    sl = 2 * half + m
                    k.do("vector", lambda v, xt=xt, m=m, c=4 * t + sl, gbm=gbm: v.scalar_tensor_tensor(
                        out=xt, in0=gbm[m][:, :], scalar=w4a[:, c:c + 1], in1=xt,
                        op0=ALU.mult, op1=ALU.add),
                        reads=[bGb[m], bD4, xtile[t]], writes=[xtile[t]])

    def emit_ffn(inv):
        if SPARSE:
            emit_ln(dram=True)
            emit_routing()
            if inv == 0:
                k.do("gpsimd", lambda g: g.engine_nop(),
                     writes=[bPT, bPTh[0], bPTh[1]])
                if COLL:
                    for s in range(NSLOT):
                        emit_weight_load_transpose(s)
                    emit_weight_load_transpose(NR)
                    emit_weight_load_transpose(NR + 1)
                    for s in range(NSLOT):
                        k.do("gpsimd", lambda g, s=s: g.collective_compute(
                            "AllGather", ALU.bypass,
                            replica_groups=[list(range(8))],
                            ins=[wstage[s]],
                            outs=[wcache[8 * s:8 * (s + 1)].rearrange("e p f -> (e p) f")]),
                            reads=[bCache[s], bCacheAll], writes=[bCacheAll], sem="cc")
                else:
                    for e in range(NE):
                        emit_weight_load_transpose(e)
                k.do("gpsimd", lambda g: g.engine_nop(),
                     writes=[bPT, bPTh[0], bPTh[1]])
            emit_sched(inv)
            if inv > 0 or COLL:
                emit_weight_cache_load(NR)
                emit_weight_cache_load(NR + 1)
            emit_expert_compute(NR)
            emit_expert_compute(NR + 1)
            emit_sparse_tiles()
            emit_mix_merge()
            return
        emit_ln()
        emit_routing()
        if inv == 0:
            emit_weight_load_transpose(0)
        else:
            emit_weight_cache_load(0)
        for e in range(NE):
            if e + 1 < NE:
                if inv == 0:
                    emit_weight_load_transpose(e + 1)
                else:
                    emit_weight_cache_load(e + 1)
            emit_expert_compute(e)

    def emit_attention():
        emit_ln()
        nq = TT
        for (wb_, dest, bank, bbank) in ((wqb, qfm, pGU[0], bPGU[0]),
                                         (wkb, kfm, pGU[1], bPGU[1])):
            for nt in range(NNT):
                builds = [(lambda te, kk=kk, nt=nt, wb_=wb_, bank=bank: te.matmul(
                    bank[:, 0:NTW], wb_[:, 128 * kk:128 * (kk + 1)],
                    xfm[:, T * kk + NTW * nt:T * kk + NTW * (nt + 1)],
                    start=(kk == 0), stop=(kk == KC - 1))) for kk in range(KC)]
                k.group("tensor", builds, reads=[bXfm, bConsts], writes=[bbank])
                k.do("vector", lambda v, nt=nt, dest=dest, bank=bank: v.tensor_copy(
                    out=dest[:, NTW * nt:NTW * (nt + 1)], in_=bank[:, 0:NTW]),
                    reads=[bbank], writes=[bWT[1]])
        t = 0
        while t < nq:
            gn = min(4, nq - t)
            for ts_ in range(gn):
                builds = [(lambda te, kk=kk, tt_=t + ts_, ts_=ts_: te.matmul(
                    pGU[2][:, 128 * ts_:128 * (ts_ + 1)],
                    xfm[:, T * kk + 128 * tt_:T * kk + 128 * (tt_ + 1)],
                    wvb[:, 128 * kk:128 * (kk + 1)],
                    start=(kk == 0), stop=(kk == KC - 1))) for kk in range(KC)]
                k.group("tensor", builds, reads=[bXfm, bConsts], writes=[bPGU[2]])
            k.do("vector", lambda v, t=t, gn=gn: v.tensor_copy(
                out=vtm[:, 128 * t:128 * (t + gn)], in_=pGU[2][:, 0:128 * gn]),
                reads=[bPGU[2]], writes=[bWT[1]])
            t += gn

        def ph1(qi):
            W = (qi + 1) * 128
            nb = (qi + 4) // 4
            pr = qi % 2
            AT = ATS[pr]
            bA = bAtt[pr]
            ar = arow2[pr]
            for j in range(qi + 1):
                bk = j // 4
                k.do("tensor", lambda te, qi=qi, j=j, bk=bk: te.matmul(
                    pGU[bk][:, 128 * (j % 4):128 * (j % 4 + 1)],
                    qfm[:, 128 * qi:128 * (qi + 1)], kfm[:, 128 * j:128 * (j + 1)],
                    start=True, stop=True),
                    reads=[bWT[1]], writes=[bPGU[bk]])
            dbk = qi // 4
            k.do("vector", lambda v, qi=qi, dbk=dbk: v.tensor_add(
                out=pGU[dbk][:, 128 * (qi % 4):128 * (qi % 4 + 1)],
                in0=pGU[dbk][:, 128 * (qi % 4):128 * (qi % 4 + 1)], in1=maskc[:, :]),
                reads=[bPGU[dbk], bConsts], writes=[bPGU[dbk]])
            k.group("vector", [
                (lambda v, b=b, wb2=min(512, W - 512 * b), AT=AT: v.reduce_max(
                    out=AT["part"][:, b:b + 1], in_=pGU[b][:, 0:wb2], axis=AX.X))
                for b in range(nb)],
                reads=[bPGU[b2] for b2 in range(nb)], writes=[bA])
            k.do("vector", lambda v, nb=nb, AT=AT: v.reduce_max(
                out=AT["nm"][:, :], in_=AT["part"][:, 0:nb], axis=AX.X, negate=True),
                reads=[bA], writes=[bA])
            k.do("vector", lambda v, AT=AT: v.tensor_scalar(
                out=AT["nms"][:, :], in0=AT["nm"][:, :], scalar1=SCALE_ATT, scalar2=None,
                op0=ALU.mult), reads=[bA], writes=[bA])
            for b in range(nb):
                wb2 = min(512, W - 512 * b)
                k.do("scalar", lambda s, b=b, wb2=wb2, AT=AT, ar=ar: s.activation(
                    out=ar[:, 512 * b:512 * b + wb2], in_=pGU[b][:, 0:wb2],
                    func=AF.Exp, bias=AT["nms"][:, 0:1], scale=SCALE_ATT),
                    reads=[bPGU[b], bA], writes=[bArow[pr]])
            k.do("vector", lambda v, W=W, AT=AT, ar=ar: v.reduce_sum(
                out=AT["sum"][:, :], in_=ar[:, 0:W], axis=AX.X),
                reads=[bArow[pr]], writes=[bA])
            k.do("vector", lambda v, AT=AT: v.reciprocal(out=AT["rinv"][:, :], in_=AT["sum"][:, :]),
                 reads=[bA], writes=[bA])

        def ph2(qi):
            pr = qi % 2
            AT = ATS[pr]
            bA = bAtt[pr]
            ar = arow2[pr]
            j0 = 0
            while j0 <= qi:
                jn = min(8, qi + 1 - j0)
                builds = [(lambda te, j=j0 + jj, jj=jj, ar=ar: te.transpose(
                    pT[:, 128 * jj:128 * (jj + 1)], ar[:, 128 * j:128 * (j + 1)],
                    ident[:, :])) for jj in range(jn)]
                k.group("tensor", builds, reads=[bArow[pr], bConsts], writes=[bPT])
                k.do("vector", lambda v, j0=j0, jn=jn: v.tensor_copy(
                    out=aT[:, 128 * j0:128 * (j0 + jn)], in_=pT[:, 0:128 * jn]),
                    reads=[bPT], writes=[bAT])
                j0 += jn
            builds = [(lambda te, j=j, qi=qi: te.matmul(
                pX[:, 0:128], aT[:, 128 * j:128 * (j + 1)],
                vtm[:, 128 * j:128 * (j + 1)],
                start=(j == 0), stop=(j == qi))) for j in range(qi + 1)]
            k.group("tensor", builds, reads=[bAT, bWT[1]], writes=[bPX])
            k.do("vector", lambda v, AT=AT: v.tensor_scalar(
                out=ctxscr[:, :], in0=pX[:, 0:128], scalar1=AT["rinv"][:, 0:1],
                scalar2=None, op0=ALU.mult), reads=[bPX, bA], writes=[bCtxS])
            k.do("tensor", lambda te: te.transpose(pT[:, 0:128], ctxscr[:, :], ident[:, :]),
                 reads=[bCtxS, bConsts], writes=[bPT])
            k.do("vector", lambda v, qi=qi: v.tensor_copy(
                out=ctxfm[:, 128 * qi:128 * (qi + 1)], in_=pT[:, 0:128]),
                reads=[bPT], writes=[bWT[1]])

        k.do("gpsimd", lambda g: g.engine_nop(),
             writes=[bXln[0], bXln[1], bArow[0]])
        ph1(0)
        for qi in range(1, nq):
            ph1(qi)
            ph2(qi - 1)
        ph2(nq - 1)
        k.do("gpsimd", lambda g: g.engine_nop(),
             writes=[bArow[0], bXln[0], bXln[1]])

        for t in range(nq):
            for h2 in range(2):
                pb = (pD0, pD1)[h2]
                bb = bPDrot[h2]
                k.do("tensor", lambda te, t=t, h2=h2, pb=pb: te.matmul(
                    pb[:, 0:512], ctxfm[:, 128 * t:128 * (t + 1)],
                    wob[:, 512 * h2:512 * (h2 + 1)], start=True, stop=True),
                    reads=[bWT[1], bConsts], writes=[bb])
                xsub = x_sb[:, 1024 * t + 512 * h2:1024 * t + 512 * (h2 + 1)]
                k.do("vector", lambda v, pb=pb, xsub=xsub: v.tensor_add(
                    out=xsub, in0=xsub, in1=pb[:, 0:512]),
                    reads=[bb, xtile[t]], writes=[xtile[t]])

    with ExitStack() as semctx:
        for n in SEMS:
            k.sem[n] = semctx.enter_context(nc.semaphore(n))
        emit_prog()
        with nc.Block() as blk:
            @blk.sync
            def _(eng):
                for f in k.ops["sync"]:
                    f(eng)

            @blk.tensor
            def _(eng):
                for f in k.ops["tensor"]:
                    f(eng)

            @blk.vector
            def _(eng):
                for f in k.ops["vector"]:
                    f(eng)

            @blk.scalar
            def _(eng):
                for f in k.ops["scalar"]:
                    f(eng)

            @blk.gpsimd
            def _(eng):
                for f in k.ops["gpsimd"]:
                    f(eng)

    ctx.close()
    return nc


_IN_NAMES = {
    "x": "L0_attn_x", "wq": "L0_attn_wq", "wk": "L0_attn_wk", "wv": "L0_attn_wv",
    "wo": "L0_attn_wo", "rw": "L0_ffn_router_weight", "rb": "L0_ffn_router_bias",
    "gw": "L0_ffn_gate_w", "uw": "L0_ffn_up_w", "dw": "L0_ffn_down_w",
    "sgw": "L0_ffn_shared_gate_w", "suw": "L0_ffn_shared_up_w",
    "sdw": "L0_ffn_shared_down_w",
}


def kernel(**inputs):
    nc = build_program({"collective": True})
    g = lambda n: np.asarray(inputs[n])
    common = {
        "wq": np.ascontiguousarray(g("L0_attn_wq")),
        "wk": np.ascontiguousarray(g("L0_attn_wk")),
        "wv": np.ascontiguousarray(g("L0_attn_wv")),
        "wo": np.ascontiguousarray(g("L0_attn_wo")),
        "rw": np.ascontiguousarray(g("L0_ffn_router_weight")),
        "rb": np.ascontiguousarray(g("L0_ffn_router_bias")),
        "sgw": np.ascontiguousarray(g("L0_ffn_shared_gate_w")),
        "suw": np.ascontiguousarray(g("L0_ffn_shared_up_w")),
        "sdw": np.ascontiguousarray(g("L0_ffn_shared_down_w")),
    }
    gw, uw, dw = g("L0_ffn_gate_w"), g("L0_ffn_up_w"), g("L0_ffn_down_w")
    sgw, suw, sdw = g("L0_ffn_shared_gate_w"), g("L0_ffn_shared_up_w"), g("L0_ffn_shared_down_w")
    x = np.ascontiguousarray(g("L0_attn_x"))
    zg = np.zeros((512, 1024), np.float32)
    zd = np.zeros((1024, 512), np.float32)

    def slot(e, kind):
        if e < 32:
            return {"g": gw[e], "u": uw[e], "d": dw[e]}[kind]
        if e in (32, 33):
            a = e - 32
            if kind == "g":
                return sgw[512 * a:512 * (a + 1), :]
            if kind == "u":
                return suw[512 * a:512 * (a + 1), :]
            return np.ascontiguousarray(sdw[:, 512 * a:512 * (a + 1)])
        return zg if kind in ("g", "u") else zd

    in_maps = []
    for c in range(8):
        es = [8 * s + c for s in range(4)]
        in_maps.append(dict(
            common, x=x[c],
            gw5=np.ascontiguousarray(np.stack([slot(e, "g") for e in es])),
            uw5=np.ascontiguousarray(np.stack([slot(e, "u") for e in es])),
            dw5=np.ascontiguousarray(np.stack([slot(e, "d") for e in es])),
        ))
    res = run_bass_kernel_spmd(nc, in_maps, core_ids=list(range(8)))
    return np.stack([res.results[c]["y"] for c in range(8)], axis=0)
